# version 07
# baseline (speedup 1.0000x reference)
import contextlib
import numpy as np
import ml_dtypes

import concourse.bass as bass
import concourse.mybir as mybir
from concourse.bass_utils import run_bass_kernel_spmd

F32 = mybir.dt.float32
BF16 = mybir.dt.bfloat16
AF = mybir.ActivationFunctionType
ALU = mybir.AluOpType
AX = mybir.AxisListType

B, S, D = 2, 4096, 2048
H, Dh = 16, 128
CT = 8
QTOK = CT * 128
F0 = 5632
GU = 2 * F0
NEXP, FM, FS = 8, 1408, 5632
SCALE = 1.0 / float(np.sqrt(Dh))
SCALE2 = SCALE * SCALE
MAGIC = float(1.5 * 2 ** 23)
NEG = -float(2.0 ** 30)
EPS = 1e-5
DEBUG = False
DEBUG2 = False

WSHAPES = {
    "L0_attn_q_w": (D, D), "L0_attn_k_w": (D, Dh), "L0_attn_v_w": (D, Dh),
    "L0_attn_o_w": (D, D),
    "L0_ffn_w_gate_up": (D, GU), "L0_ffn_w_down": (F0, D),
    "L0_ffn_norm_weight": (D,),
    "L1_attn_W_K": (D, H * Dh), "L1_attn_W_Q": (D, H * Dh),
    "L1_attn_W_V": (D, H * Dh), "L1_attn_W_O": (H * Dh, D),
    "L1_ffn_gate_w": (NEXP, D),
    "L1_ffn_expert_fc1": (D, NEXP * FM), "L1_ffn_expert_fc2": (D, NEXP * FM),
    "L1_ffn_expert_proj": (FM, NEXP * D),
    "L1_ffn_shared_fc1": (D, FS), "L1_ffn_shared_fc2": (D, FS),
    "L1_ffn_shared_proj": (FS, D), "L1_ffn_shared_gate_w": (1, D),
}
NO_T = ("L0_ffn_norm_weight", "L1_ffn_gate_w", "L1_ffn_shared_gate_w")


def prep_weights(inputs):
    out = {}
    for k in WSHAPES:
        a = np.asarray(inputs[k], np.float32)
        if k in NO_T:
            out[k] = np.ascontiguousarray(a.reshape(WSHAPES[k]))
        else:
            out[k] = np.ascontiguousarray(a.reshape(-1, a.shape[-1]).T)
        assert out[k].shape == tuple(WSHAPES[k]), (k, out[k].shape)
    return out


def kg_of(kt):
    return (kt % 8) * 4 + kt // 8


KTS_LOW = [kt for kt in range(32) if kg_of(kt) < 16]
KTS_HIGH = [kt for kt in range(32) if kg_of(kt) >= 16]


class Sem:
    def __init__(self, stack, nc, name, ordered=False):
        self.h = stack.enter_context(nc.semaphore(name))
        self.v = 0
        self.ordered = ordered

    def inc(self, n=1):
        self.v += n
        return self.v


class Prog:
    ENGS = ("gpsimd", "sync", "scalar", "vector", "tensor")

    def __init__(self, nc):
        self.nc = nc
        self.q = {k: [] for k in self.ENGS}

    def emit(self, eng, fn):
        self.q[eng].append(fn)

    def flush(self):
        if not any(self.q.values()):
            return
        with self.nc.Block() as block:
            for name in self.ENGS:
                fns = self.q[name]
                if fns:
                    def runner(e, fns=fns):
                        for f in fns:
                            f(e)
                    getattr(block, name)(runner)
        self.q = {k: [] for k in self.ENGS}


class KC:
    pass


def gw(kc, eng, sem, tgt=None):
    if not sem.ordered:
        tgt = sem.v
    if tgt is None or tgt <= 0:
        return
    assert tgt <= sem.v, f"wait target {tgt} > queued {sem.v}"
    kc.P.emit(eng, lambda E, s=sem, t=tgt: E.wait_ge(s.h, t))


def dma(kc, eng, sem, out, in_, **kw):
    sem.inc(16)
    kc.P.emit(eng, lambda E, o=out, i=in_, s=sem, kw=kw:
              E.dma_start(out=o, in_=i, **kw).then_inc(s.h, 16))
    return sem


def tdma(kc, out, in_, sem=None):
    sem = sem if sem is not None else kc.sd_y
    sem.inc(16)
    kc.P.emit("sync", lambda E, o=out, i=in_, s=sem: E.dma_start(out=o, in_=i, transpose=True)
              .then_inc(s.h, 16))
    return sem


def vop(kc, fn, sep=False):
    if sep:
        kc.P.emit("vector", lambda V, t=kc.s_v.v: V.wait_ge(kc.s_v.h, t))
    tgt = kc.s_v.inc(1)
    kc.P.emit("vector", lambda V, f=fn: f(V).then_inc(kc.s_v.h, 1))
    return tgt


def sop(kc, fn, sep=False):
    if sep:
        kc.P.emit("scalar", lambda Sc, t=kc.s_s.v: Sc.wait_ge(kc.s_s.h, t))
    tgt = kc.s_s.inc(1)
    kc.P.emit("scalar", lambda Sc, f=fn: f(Sc).then_inc(kc.s_s.h, 1))
    return tgt


def top(kc, fn, bump=True):
    if bump:
        tgt = kc.s_t.inc(1)
        kc.P.emit("tensor", lambda T, f=fn: f(T).then_inc(kc.s_t.h, 1))
        return tgt
    kc.P.emit("tensor", lambda T, f=fn: f(T))
    return None


def bank_wait(kc, b):
    dep = kc.bankdep.get(b)
    if dep is not None:
        gw(kc, "tensor", *dep)
        kc.bankdep[b] = None


def bank_done(kc, b, sem, tgt):
    kc.bankdep[b] = (sem, tgt)


def buf_wait(kc, eng, name):
    dep = kc.bufdep.get(name)
    if dep is not None:
        gw(kc, eng, *dep)
        kc.bufdep[name] = None


def buf_done(kc, name, sem, tgt=None):
    kc.bufdep[name] = (sem, tgt)


def phase_barrier(kc):
    tv, tt_, ts_ = kc.s_v.v, kc.s_t.v, kc.s_s.v
    for eng in ("scalar", "vector", "tensor", "sync"):
        gw(kc, eng, kc.s_v, tv)
        gw(kc, eng, kc.s_t, tt_)
        gw(kc, eng, kc.s_s, ts_)
        for sm in kc.barrier_sems():
            gw(kc, eng, sm)


def wapf(kc, k):
    return kc.W[k].ap()


def emit_ln(kc, tt, out_ap, rms_col=None):
    st = kc.stats
    gw(kc, "scalar", kc.s_v, kc.x_ready_v)
    jk = out_ap
    sop(kc, lambda Sc, tt=tt, jk=jk: Sc.activation(jk, kc.x_res[:, tt, :],
                                                   AF.Identity, accum_out=st[:, 0:1]))
    t_s = sop(kc, lambda Sc, tt=tt, jk=jk: Sc.activation(jk, kc.x_res[:, tt, :],
                                                         AF.Square, accum_out=st[:, 1:2]))
    gw(kc, "vector", kc.s_s, t_s)
    vop(kc, lambda V: V.tensor_scalar(st[:, 2:3], st[:, 0:1], 1.0 / D, None, ALU.mult))
    vop(kc, lambda V: V.tensor_scalar(st[:, 3:4], st[:, 1:2], 1.0 / D, None, ALU.mult))
    vop(kc, lambda V: V.tensor_mul(st[:, 4:5], st[:, 2:3], st[:, 2:3]))
    tv = vop(kc, lambda V: V.tensor_sub(st[:, 5:6], st[:, 3:4], st[:, 4:5]), sep=True)
    gw(kc, "scalar", kc.s_v, tv)
    tsq = sop(kc, lambda Sc: Sc.activation(st[:, 6:7], st[:, 5:6], AF.Sqrt, bias=kc.cvals[:, 0:1]))
    gw(kc, "vector", kc.s_s, tsq)
    vop(kc, lambda V: V.reciprocal(st[:, 7:8], st[:, 6:7]))
    vop(kc, lambda V: V.tensor_mul(st[:, 8:9], st[:, 2:3], st[:, 7:8]), sep=True)
    vop(kc, lambda V: V.tensor_scalar(st[:, 8:9], st[:, 8:9], -1.0, None, ALU.mult), sep=True)
    tgt = vop(kc, lambda V, tt=tt, o=out_ap: V.tensor_scalar(
        o, kc.x_res[:, tt, :], st[:, 7:8], st[:, 8:9], ALU.mult, ALU.add), sep=True)
    if rms_col is not None:
        vop(kc, lambda V: V.tensor_mul(st[:, 9:10], st[:, 5:6], st[:, 7:8]))
        tv2 = vop(kc, lambda V: V.tensor_mul(st[:, 9:10], st[:, 9:10], st[:, 7:8]), sep=True)
        gw(kc, "scalar", kc.s_v, tv2)
        ts2 = sop(kc, lambda Sc: Sc.activation(st[:, 9:10], st[:, 9:10], AF.Sqrt, bias=kc.cvals[:, 0:1]))
        gw(kc, "vector", kc.s_s, ts2)
        vop(kc, lambda V, c=rms_col: V.reciprocal(kc.pvec0[:, c:c + 1], st[:, 9:10]))
    return tgt


def emit_t2f(kc, src_fn, dst_col0, kcn, featbuf, src_ready=None, colw=1024):
    for g in range((kcn + 3) // 4):
        nj = min(4, kcn - g * 4)
        b = 6 + (kc.tp_flip % 2)
        kc.tp_flip += 1
        bank_wait(kc, b)
        if src_ready is not None:
            gw(kc, "tensor", *src_ready)
            src_ready = None
        for j in range(nj):
            kcc = g * 4 + j
            top(kc, lambda T, b=b, j=j, kcc=kcc: T.matmul(
                kc.ps[b][:, j * 128:(j + 1) * 128], src_fn(kcc), kc.ident[:, :],
                start=True, stop=True), bump=(j == nj - 1))
        gw(kc, "vector", kc.s_t, kc.s_t.v)
        tv = vop(kc, lambda V, b=b, g=g, c0=dst_col0, nj=nj, fb=featbuf, colw=colw:
                 V.tensor_copy(
                     bass.AP(fb, g * 4 * colw + c0, [[16 * colw, 128], [colw, nj], [1, 128]]),
                     bass.AP(kc.ps[b], 0, [[512, 128], [128, nj], [1, 128]])))
        bank_done(kc, b, kc.s_v, tv)


def slab_load(kc, dst_buf, base, width, wT_ap, o0, ncols, kcn, sem=None, r0=0,
              f32=True):
    if not f32:
        t = None
        for c in range(kcn):
            t = dma(kc, "sync", sem,
                    dst_buf[:, base + c * width: base + c * width + ncols],
                    wT_ap[r0 + c * 128: r0 + (c + 1) * 128, o0:o0 + ncols])
        return (t, None)
    ts_ = None
    for c in range(kcn):
        sl = kc.stg_flip % 2
        kc.stg_flip += 1
        dep = kc.stg_dep[sl]
        if dep is not None:
            gw(kc, "sync", kc.s_s, dep)
        dma(kc, "sync", sem, kc.stg[:, sl, 0:ncols],
            wT_ap[r0 + c * 128: r0 + (c + 1) * 128, o0:o0 + ncols])
        gw(kc, "scalar", sem)
        ts_ = sop(kc, lambda Sc, c=c, sl=sl, ncols=ncols, base=base, width=width,
                  dst_buf=dst_buf: Sc.activation(
                      dst_buf[:, base + c * width: base + c * width + ncols],
                      kc.stg[:, sl, 0:ncols], AF.Identity))
        kc.stg_dep[sl] = ts_
    return (kc.s_s, ts_)


def cross_part_max(kc, src_ap, nh):
    tv = vop(kc, lambda V: V.tensor_copy(kc.cprb[:, 0:nh], src_ap), sep=True)
    b = 6 + (kc.tp_flip % 2)
    kc.tp_flip += 1
    bank_wait(kc, b)
    gw(kc, "tensor", kc.s_v, tv)
    t_t = top(kc, lambda T, b=b, nh=nh: T.matmul(
        kc.ps[b][0:nh, 0:128], kc.cprb[:, 0:nh], kc.ident[:, :],
        start=True, stop=True))
    gw(kc, "vector", kc.s_t, t_t)
    tv = vop(kc, lambda V, b=b, nh=nh: V.tensor_reduce(
        kc.kmaxb[0:nh, 0:1], kc.ps[b][0:nh, 0:128], AX.X, ALU.max))
    bank_done(kc, b, kc.s_v, tv)
    return tv


def cross_part_sum(kc, src_col):
    vop(kc, lambda V: V.tensor_copy(kc.cprb[:, 16:17], src_col), sep=True)
    vop(kc, lambda V: V.tensor_copy(kc.stats[:, 30:31], kc.cprb[:, 16:17]), sep=True)
    vop(kc, lambda V: V.tensor_sub(kc.stats[:, 31:32], src_col, kc.stats[:, 30:31]), sep=True)
    tv = vop(kc, lambda V: V.tensor_copy(kc.cprb[:, 17:18], kc.stats[:, 31:32]), sep=True)
    b = 6 + (kc.tp_flip % 2)
    kc.tp_flip += 1
    bank_wait(kc, b)
    gw(kc, "tensor", kc.s_v, tv)
    t_t = top(kc, lambda T, b=b: T.matmul(
        kc.ps[b][0:1, 0:2], kc.ones_bf[:, :], kc.cprb[:, 16:18],
        start=True, stop=True))
    gw(kc, "vector", kc.s_t, t_t)
    tv = vop(kc, lambda V, b=b: V.tensor_reduce(
        kc.kmaxb[0:1, 0:1], kc.ps[b][0:1, 0:2], AX.X, ALU.add))
    bank_done(kc, b, kc.s_v, tv)
    return tv


def emit_bcast16(kc, src_col_ap, nh, dst_ap):
    tv = vop(kc, lambda V, nh=nh: V.tensor_copy(kc.cprb[0:nh, 16:17], src_col_ap), sep=True)
    b = 6 + (kc.tp_flip % 2)
    kc.tp_flip += 1
    bank_wait(kc, b)
    gw(kc, "tensor", kc.s_v, tv)
    top(kc, lambda T, b=b, nh=nh: T.matmul(
        kc.ps[b][0:1, 0:nh], kc.cprb[0:nh, 16:17], kc.ident[0:nh, 0:nh],
        start=True, stop=True))
    gw(kc, "vector", kc.s_t, kc.s_t.v)
    tv = vop(kc, lambda V, b=b, nh=nh: V.tensor_copy(kc.cprb[0:1, 0:nh], kc.ps[b][0:1, 0:nh]))
    bank_done(kc, b, kc.s_v, tv)
    b2 = 6 + (kc.tp_flip % 2)
    kc.tp_flip += 1
    bank_wait(kc, b2)
    gw(kc, "tensor", kc.s_v, tv)
    top(kc, lambda T, b2=b2, nh=nh: T.matmul(
        kc.ps[b2][:, 0:nh], kc.ones_row[0:1, :], kc.cprb[0:1, 0:nh],
        start=True, stop=True))
    gw(kc, "vector", kc.s_t, kc.s_t.v)
    tv = vop(kc, lambda V, b2=b2, nh=nh, d=dst_ap: V.tensor_copy(d, kc.ps[b2][:, 0:nh]))
    bank_done(kc, b2, kc.s_v, tv)
    return tv


def emit_bcast_f32(kc, src_11, dst_col):
    vop(kc, lambda V: V.tensor_copy(kc.cprb[0:1, 0:1], src_11), sep=True)
    vop(kc, lambda V: V.tensor_copy(kc.stats[0:1, 30:31], kc.cprb[0:1, 0:1]), sep=True)
    vop(kc, lambda V: V.tensor_sub(kc.stats[0:1, 31:32], src_11, kc.stats[0:1, 30:31]), sep=True)
    tv = vop(kc, lambda V: V.tensor_copy(kc.cprb[0:1, 1:2], kc.stats[0:1, 31:32]), sep=True)
    b = 6 + (kc.tp_flip % 2)
    kc.tp_flip += 1
    bank_wait(kc, b)
    gw(kc, "tensor", kc.s_v, tv)
    top(kc, lambda T, b=b: T.matmul(
        kc.ps[b][:, 0:2], kc.ones_row[0:1, :], kc.cprb[0:1, 0:2],
        start=True, stop=True))
    gw(kc, "vector", kc.s_t, kc.s_t.v)
    vop(kc, lambda V, b=b: V.tensor_copy(kc.stats[:, 30:31], kc.ps[b][:, 0:1]))
    tv2 = vop(kc, lambda V, b=b, d=dst_col: V.tensor_add(
        d, kc.ps[b][:, 1:2], kc.stats[:, 30:31]), sep=True)
    bank_done(kc, b, kc.s_v, tv2)
    return tv2


def emit_attention(kc, layer):
    nc = kc.nc
    l1 = (layer == 1)
    kvloc, kvg = (kc.kv1loc, kc.kv1g) if l1 else (kc.kv0loc, kc.kv0g)
    KV = 2 * (D if l1 else Dh)
    half = KV // 2
    nh = half // 128
    wq, wk, wv, wo = (("L1_attn_W_Q", "L1_attn_W_K", "L1_attn_W_V", "L1_attn_W_O")
                      if l1 else
                      ("L0_attn_q_w", "L0_attn_k_w", "L0_attn_v_w", "L0_attn_o_w"))

    with contextlib.ExitStack() as actx:
        ee = actx.enter_context
        featbuf = ee(nc.sbuf_tensor("featbufA%d" % layer, [128, 16, 1024], BF16))
        QT = ee(nc.sbuf_tensor("QT%d" % layer, [128, 8, 1024], BF16))
        KTs = [ee(nc.sbuf_tensor(f"KTs{layer}{i}", [128, 4096], BF16)) for i in range(2)]
        Vaug = [ee(nc.sbuf_tensor(f"Vaug{layer}{i}", [128, 32, 136], BF16)) for i in range(2)]
        Eb = [ee(nc.sbuf_tensor(f"Eb{layer}{i}", [128, 512], BF16)) for i in range(2)]
        zblk = ee(nc.sbuf_tensor("zblk%d" % layer, [128, 128], BF16))
        phase_barrier(kc)

        for tt in range(CT):
            buf_wait(kc, "vector", f"xn{tt % 2}")
            xv = emit_ln(kc, tt, kc.xn_blk[:, tt % 2, :])
            if getattr(kc, "debug2", False) and not l1 and tt == 0:
                gw(kc, "scalar", kc.s_v, xv)
                dma(kc, "scalar", kc.sd_out, kc.dbg["dbg_xn0"][:, :], kc.xn_blk[:, 0, :])
                dma(kc, "scalar", kc.sd_out, kc.dbg["dbg_st0"][:, :], kc.stats[:, :])
            emit_t2f(kc, (lambda kcc, tt=tt: kc.xn_blk[:, tt % 2, kcc * 128:(kcc + 1) * 128]),
                     tt * 128, 16, featbuf, src_ready=(kc.s_v, xv))
            buf_done(kc, f"xn{tt % 2}", kc.s_t, kc.s_t.v)

        if getattr(kc, "debug2", False) and not l1:
            gw(kc, "scalar", kc.s_v, kc.s_v.v)
            gw(kc, "scalar", kc.s_t, kc.s_t.v)
            dma(kc, "scalar", kc.sd_out, kc.dbg["dbg_fb"][:, :],
                bass.AP(featbuf, 0, [[16 * 1024, 128], [1, 16 * 1024]]))
        featbuf_ready = kc.s_v.v
        gw(kc, "tensor", kc.s_v, featbuf_ready)
        vop(kc, lambda V: V.memset(kc.svec[:, :], 0.0))
        ocw = min(512, KV)
        n_oc = KV // ocw
        kv_store_t = None
        prev_knorm_v = None

        def load_kv_slab(oc):
            ab = kc.arena[oc % 2]
            abn = f"a{oc % 2}"
            buf_wait(kc, "scalar", abn)
            ysem = kc.ys(abn)
            if not l1:
                slab_load(kc, ab, 0, 256, wapf(kc, wk), 0, 128, 16, sem=ysem)
                t_sl = slab_load(kc, ab, 128, 256, wapf(kc, wv), 0, 128, 16, sem=ysem)
            else:
                key = wk if oc < 4 else wv
                r0 = (oc % 4) * 512
                t_sl = slab_load(kc, ab, 0, 512, wapf(kc, key), r0, 512, 16, sem=ysem)
            return ab, abn, t_sl
        pend_kv = load_kv_slab(0)
        for oc in range(n_oc):
            nxt_kv = load_kv_slab(oc + 1) if oc + 1 < n_oc else None
            ab, abn, t_sl = pend_kv
            pend_kv = nxt_kv
            gw(kc, "tensor", *t_sl)
            t_last_mm = None
            for tt in range(CT):
                b = tt % 2
                bank_wait(kc, b)
                for i in range(16):
                    top(kc, lambda T, b=b, i=i, tt=tt, ab=ab, ocw=ocw: T.matmul(
                        kc.ps[b][:, 0:ocw], featbuf[:, i, tt * 128:(tt + 1) * 128],
                        ab[:, i * ocw:(i + 1) * ocw],
                        start=(i == 0), stop=(i == 15)), bump=(i == 15))
                t_mm = kc.s_t.v
                t_last_mm = t_mm
                gw(kc, "vector", kc.s_t, t_mm)
                buf_wait(kc, "vector", f"xn{tt % 2}")
                tv = vop(kc, lambda V, b=b, tt=tt, ocw=ocw: V.tensor_copy(
                    kc.xn_blk[:, tt % 2, 0:ocw], kc.ps[b][:, 0:ocw]))
                bank_done(kc, b, kc.s_v, tv)
                if (not l1) or oc < 4:
                    kcols = 128 if not l1 else 512
                    gw(kc, "scalar", kc.s_v, tv)
                    if prev_knorm_v is not None:
                        gw(kc, "scalar", kc.s_v, prev_knorm_v)
                    t_sq = sop(kc, lambda Sc, tt=tt, kcols=kcols: Sc.activation(
                        kc.sqs[:, 0:kcols], kc.xn_blk[:, tt % 2, 0:kcols], AF.Square))
                    gw(kc, "vector", kc.s_s, t_sq)
                    nseg = kcols // 128
                    prev_knorm_v = vop(kc, lambda V, nseg=nseg: V.tensor_reduce(
                        kc.stats[:, 16:16 + nseg],
                        bass.AP(kc.sqs, 0, [[512, 128], [128, nseg], [1, 128]]),
                        AX.X, ALU.add))
                    c0 = 0 if not l1 else oc * 4
                    vop(kc, lambda V, c0=c0, nseg=nseg: V.tensor_max(
                        kc.svec[:, c0:c0 + nseg], kc.svec[:, c0:c0 + nseg],
                        kc.stats[:, 16:16 + nseg]), sep=True)
                gw(kc, "scalar", kc.s_v, kc.s_v.v)
                kv_store_t = dma(kc, "scalar", kc.sd_kv,
                                 kvloc[tt * 128:(tt + 1) * 128, oc * ocw:(oc + 1) * ocw],
                                 kc.xn_blk[:, tt % 2, 0:ocw])
                buf_done(kc, f"xn{tt % 2}", kc.sd_kv, None)
            buf_done(kc, abn, kc.s_t, t_last_mm)
        gw(kc, "gpsimd", kc.s_v, kc.s_v.v)
        n_t = dma(kc, "gpsimd", kc.sd_n, kvloc[1024:1152, 0:nh], kc.svec[:, 0:nh],
                  allow_slow_non_contiguous=True)
        gw(kc, "gpsimd", kc.sd_kv)
        gw(kc, "gpsimd", kc.sd_n)
        cc_t = kc.s_cc.inc(1)
        kc.P.emit("gpsimd", lambda G, i=kvloc, o=kvg: G.collective_compute(
            "AllGather", ALU.bypass, replica_groups=[[0, 1, 2, 3], [4, 5, 6, 7]],
            ins=[i.ap()], outs=[o.ap()]).then_inc(kc.s_cc.h, 1))
        if getattr(kc, "debug2", False) and not l1:
            gw(kc, "gpsimd", kc.s_cc, cc_t)
            dma(kc, "gpsimd", kc.sd_out, kc.dbg["dbg_kvg"][:, :], kvg[:, :])
        gw(kc, "gpsimd", kc.s_cc, cc_t)
        t = dma(kc, "gpsimd", kc.sd_n, kc.svec[:, 0:4 * nh],
                bass.AP(kvg, 1024 * KV, [[KV, 128], [1152 * KV, 4], [1, nh]]),
                allow_slow_non_contiguous=True)
        gw(kc, "vector", kc.sd_n)
        vop(kc, lambda V, nh=nh: V.tensor_reduce(
            kc.stats[:, 16:16 + nh],
            bass.AP(kc.svec, 0, [[64, 128], [1, nh], [nh, 4]]), AX.X, ALU.max))
        cross_part_max(kc, kc.stats[:, 16:16 + nh], nh)

        def load_kt(buf, h):
            buf_wait(kc, "sync", f"kt{buf}")
            sm = kc.ys(f"kt{buf}")
            for r in range(4):
                tdma(kc, KTs[buf][:, r * 1024:(r + 1) * 1024],
                     kvg[r * 1152:r * 1152 + 1024, h * 128:(h + 1) * 128], sem=sm)
            return sm

        def load_vaug(buf, h):
            buf_wait(kc, "scalar", f"va{buf}")
            sm = kc.ys(f"va{buf}")
            for r in range(4):
                dma(kc, "scalar", sm, Vaug[buf][:, r * 8:(r + 1) * 8, 0:128],
                    kvg.ap()[r * 1152:r * 1152 + 1024,
                             half + h * 128:half + (h + 1) * 128]
                    .rearrange("(t p) d -> p t d", p=128))
            return sm

        gw(kc, "sync", kc.s_cc, cc_t)
        gw(kc, "scalar", kc.s_cc, cc_t)
        for bb in range(2):
            vop(kc, lambda V, bb=bb: V.memset(
                bass.AP(Vaug[bb], 128, [[32 * 136, 128], [136, 32], [1, 1]]), 1.0))
        kt_t = {0: load_kt(0, 0)}
        va_t = {0: load_vaug(0, 0)}

        for grp in range(2):
            gw(kc, "tensor", kc.s_v, featbuf_ready)
            vop(kc, lambda V: V.memset(kc.stats[0:1, 12:13], 0.0))
            prev_on = None
            for hs in range(2):
                buf_wait(kc, "scalar", "a1")
                t_q = slab_load(kc, kc.arena[1], 0, 512, wapf(kc, wq),
                                grp * 1024 + hs * 512, 512, 16, sem=kc.ys("a1"))
                gw(kc, "tensor", *t_q)
                t_pj = None
                for sl in range(4):
                    s = hs * 4 + sl
                    for qc in range(2):
                        b = qc
                        bank_wait(kc, b)
                        for i in range(16):
                            top(kc, lambda T, b=b, i=i, s=s, sl=sl, qc=qc: T.matmul(
                                kc.ps[b][:, 0:512],
                                kc.arena[1][:, i * 512 + sl * 128:i * 512 + (sl + 1) * 128],
                                featbuf[:, i, qc * 512:(qc + 1) * 512],
                                start=(i == 0), stop=(i == 15)), bump=(i == 15))
                        t_pj = kc.s_t.v
                        gw(kc, "vector", kc.s_t, t_pj)
                        tv = vop(kc, lambda V, b=b, s=s, qc=qc: V.tensor_copy(
                            QT[:, s, qc * 512:(qc + 1) * 512], kc.ps[b][:, 0:512]))
                        bank_done(kc, b, kc.s_v, tv)
                        gw(kc, "scalar", kc.s_v, tv)
                        if prev_on is not None:
                            gw(kc, "scalar", kc.s_t, prev_on)
                        t_sq = sop(kc, lambda Sc, s=s, qc=qc: Sc.activation(
                            kc.sqs[:, 0:512], QT[:, s, qc * 512:(qc + 1) * 512], AF.Square))
                        bank_wait(kc, 5)
                        gw(kc, "tensor", kc.s_s, t_sq)
                        t_on = top(kc, lambda T: T.matmul(
                            kc.ps[5][0:1, 0:512], kc.ones_bf[:, :], kc.sqs[:, 0:512],
                            start=True, stop=True))
                        prev_on = t_on
                        gw(kc, "vector", kc.s_t, t_on)
                        vop(kc, lambda V: V.tensor_reduce(
                            kc.stats[0:1, 13:14], kc.ps[5][0:1, 0:512], AX.X, ALU.max))
                        tv2 = vop(kc, lambda V: V.tensor_max(
                            kc.stats[0:1, 12:13], kc.stats[0:1, 12:13], kc.stats[0:1, 13:14]),
                            sep=True)
                        bank_done(kc, 5, kc.s_v, tv2)
                buf_done(kc, "a1", kc.s_t, t_pj)
            qt_ready = kc.s_v.v
            emit_bcast16(kc, kc.stats[0:1, 12:13], 1, kc.stats[:, 13:14])
            tv = vop(kc, lambda V, nh=nh: V.tensor_mul(
                kc.kmaxb[0:nh, 2:3], kc.kmaxb[0:nh, 0:1], kc.stats[0:nh, 13:14]), sep=True)
            gw(kc, "scalar", kc.s_v, tv)
            ts_ = sop(kc, lambda Sc, nh=nh: Sc.activation(
                kc.kmaxb[0:nh, 3:4], kc.kmaxb[0:nh, 2:3], AF.Sqrt, scale=SCALE2))
            gw(kc, "vector", kc.s_s, ts_)
            tv = vop(kc, lambda V, nh=nh: V.tensor_scalar(
                kc.kmaxb[0:nh, 3:4], kc.kmaxb[0:nh, 3:4], -1.0, -1.0, ALU.mult, ALU.add))
            emit_bcast16(kc, kc.kmaxb[0:nh, 3:4], nh, kc.nmb[:, 0:nh])
            nmb_v = kc.s_v.v
            gw(kc, "scalar", kc.s_v, nmb_v)

            if getattr(kc, "debug2", False) and not l1 and grp == 0:
                gw(kc, "scalar", kc.s_v, kc.s_v.v)
                dma(kc, "scalar", kc.sd_out, kc.dbg["dbg_qt"][:, :],
                    bass.AP(QT, 0, [[8 * 1024, 128], [1, 8 * 1024]]))
                dma(kc, "scalar", kc.sd_out, kc.dbg["dbg_nmb"][:, :], kc.nmb[:, :])
                gw(kc, "scalar", kc.ys("kt0"))
                gw(kc, "scalar", kc.ys("va0"))
                dma(kc, "scalar", kc.sd_out, kc.dbg["dbg_kt"][:, :], KTs[0][:, :])
                dma(kc, "scalar", kc.sd_out, kc.dbg["dbg_va"][:, :],
                    bass.AP(Vaug[0], 0, [[32 * 136, 128], [1, 32 * 136]]))
            if l1:
                buf_wait(kc, "scalar", "a1")
                dma(kc, "scalar", kc.sd_mk, kc.arena[1][:, 0:8192],
                    kc.mask_in[:, 0:8192])
                gw(kc, "tensor", kc.sd_mk)

            gw(kc, "tensor", kc.s_v, qt_ready)
            for s in range(8):
                h = grp * 8 + s
                if l1:
                    if h + 1 < 16:
                        kt_t[(h + 1) % 2] = load_kt((h + 1) % 2, h + 1)
                        va_t[(h + 1) % 2] = load_vaug((h + 1) % 2, h + 1)
                    kbuf = h % 2
                else:
                    kbuf = 0
                gw(kc, "tensor", kt_t[kbuf])
                gw(kc, "tensor", va_t[kbuf])
                gw(kc, "scalar", kc.s_v, nmb_v)
                mcol = h if l1 else 0
                for qc in range(2):
                    if l1:
                        kts = ([(kt, True) for kt in KTS_LOW] if qc == 0 else
                               [(kt, False) for kt in KTS_LOW] +
                               [(kt, True) for kt in KTS_HIGH])
                    else:
                        kts = [(kt, False) for kt in range(32)]
                    nkt = len(kts)
                    for j in range(4):
                        bank_wait(kc, j)
                    e_free = [kc.s_t.v, kc.s_t.v]
                    for idx, (kt, masked) in enumerate(kts):
                        sb = 4 + (idx % 2)
                        bank_wait(kc, sb)
                        top(kc, lambda T, sb=sb, kt=kt, s=s, qc=qc, kbuf=kbuf, m=masked:
                            T.matmul(kc.ps[sb][:, 0:512],
                                     KTs[kbuf][:, kt * 128:(kt + 1) * 128],
                                     QT[:, s, qc * 512:(qc + 1) * 512],
                                     start=True, stop=not m), bump=not masked)
                        if masked:
                            bi = kg_of(kt) % 16
                            top(kc, lambda T, sb=sb, bi=bi: T.matmul(
                                kc.ps[sb][:, 0:512], kc.ident[:, :],
                                kc.arena[1][:, bi * 512:(bi + 1) * 512],
                                start=False, stop=True))
                        t_qk = kc.s_t.v
                        gw(kc, "scalar", kc.s_t, t_qk)
                        if e_free[idx % 2] is not None:
                            gw(kc, "scalar", kc.s_t, e_free[idx % 2])
                        t_exp = sop(kc, lambda Sc, sb=sb, ix=idx % 2, mc=mcol: Sc.activation(
                            Eb[ix][:, :], kc.ps[sb][:, 0:512], AF.Exp,
                            bias=kc.nmb[:, mc:mc + 1], scale=SCALE))
                        bank_done(kc, sb, kc.s_s, t_exp)
                        gw(kc, "tensor", kc.s_s, t_exp)
                        for j in range(4):
                            top(kc, lambda T, j=j, ix=idx % 2, kt=kt, kbuf=kbuf,
                                st=(idx == 0), sp=(idx == nkt - 1): T.matmul(
                                kc.ps[j][:, 0:129], Eb[ix][:, j * 128:(j + 1) * 128],
                                Vaug[kbuf][:, kt, 0:129],
                                start=st, stop=sp), bump=(j == 3))
                        e_free[idx % 2] = kc.s_t.v
                    t_av = kc.s_t.v
                    gw(kc, "vector", kc.s_t, t_av)
                    for j in range(4):
                        vop(kc, lambda V, j=j: V.reciprocal(
                            kc.stats[:, 14 + j:15 + j], kc.ps[j][:, 128:129]))
                    for j in range(4):
                        tv = vop(kc, lambda V, j=j: V.tensor_scalar(
                            zblk[:, :], kc.ps[j][:, 0:128], kc.stats[:, 14 + j:15 + j], None,
                            ALU.mult))
                        bank_done(kc, j, kc.s_v, tv)
                        b = 6 + (kc.tp_flip % 2)
                        kc.tp_flip += 1
                        bank_wait(kc, b)
                        gw(kc, "tensor", kc.s_v, tv)
                        t_tp = top(kc, lambda T, b=b: T.matmul(
                            kc.ps[b][:, 0:128], zblk[:, :], kc.ident[:, :],
                            start=True, stop=True))
                        gw(kc, "vector", kc.s_t, t_tp)
                        tv2 = vop(kc, lambda V, b=b, s=s, qc=qc, j=j: V.tensor_copy(
                            QT[:, s, qc * 512 + j * 128:qc * 512 + (j + 1) * 128],
                            kc.ps[b][:, 0:128]))
                        bank_done(kc, b, kc.s_v, tv2)
                if l1:
                    buf_done(kc, f"kt{kbuf}", kc.s_t, kc.s_t.v)
                    buf_done(kc, f"va{kbuf}", kc.s_t, kc.s_t.v)
            if l1:
                buf_done(kc, "a1", kc.s_t, kc.s_t.v)
            if getattr(kc, "debug2", False) and not l1 and grp == 0:
                gw(kc, "scalar", kc.s_v, kc.s_v.v)
                dma(kc, "scalar", kc.sd_out, kc.dbg["dbg_z"][:, :],
                    bass.AP(QT, 0, [[8 * 1024, 128], [1, 8 * 1024]]))
            zt_ready = kc.s_v.v
            for dc in range(2):
                buf_wait(kc, "scalar", "a0")
                slab_load(kc, kc.arena[0], 0, 1024, wapf(kc, wo),
                          dc * 1024, 512, 8, sem=kc.ys("a0"), r0=grp * 1024)
                t_wo = slab_load(kc, kc.arena[0], 512, 1024, wapf(kc, wo),
                                 dc * 1024 + 512, 512, 8, sem=kc.ys("a0"),
                                 r0=grp * 1024)
                gw(kc, "tensor", *t_wo)
                gw(kc, "tensor", kc.s_v, zt_ready)
                t_last = None
                for tt in range(CT):
                    for ncc in range(2):
                        b = (tt * 2 + ncc) % 4
                        bank_wait(kc, b)
                        for si in range(8):
                            top(kc, lambda T, b=b, si=si, tt=tt, ncc=ncc: T.matmul(
                                kc.ps[b][:, 0:512], QT[:, si, tt * 128:(tt + 1) * 128],
                                kc.arena[0][:, si * 1024 + ncc * 512:si * 1024 + (ncc + 1) * 512],
                                start=(si == 0), stop=(si == 7)), bump=(si == 7))
                        t_mm = kc.s_t.v
                        t_last = t_mm
                        gw(kc, "vector", kc.s_t, t_mm)
                        tv = vop(kc, lambda V, b=b, tt=tt, dc=dc, ncc=ncc: V.tensor_add(
                            kc.x_res[:, tt, dc * 1024 + ncc * 512:dc * 1024 + (ncc + 1) * 512],
                            kc.x_res[:, tt, dc * 1024 + ncc * 512:dc * 1024 + (ncc + 1) * 512],
                            kc.ps[b][:, 0:512]))
                        bank_done(kc, b, kc.s_v, tv)
                buf_done(kc, "a0", kc.s_t, t_last)
        kc.x_ready_v = kc.s_v.v


def emit_bitnet_prep(kc):
    nc = kc.nc
    with contextlib.ExitStack() as actx:
        ee = actx.enter_context
        wstage = ee(nc.sbuf_tensor("wstage", [128, 4, 2816], F32))
        wstage2 = ee(nc.sbuf_tensor("wstage2", [128, 2816], F32))
        wqb = ee(nc.sbuf_tensor("wqb", [128, 2816], BF16))
        chs = ee(nc.sbuf_tensor("chs", [128, 96], F32))
        grdt = ee(nc.sbuf_tensor("grdt", [128, 96], F32))
        phase_barrier(kc)

        specs = [("L0_ffn_w_gate_up", kc.gu_q, 4, 2816, 10, 18),
                 ("L0_ffn_w_down", kc.wd_q, 1, 2048, 11, 19)]
        for key, qdst, nslc, slw, mwcol, swcol in specs:
            src = kc.W[key].ap()
            T0, T1 = WSHAPES[key]
            pieces = [(bd * 128, sc * slw) for bd in range(T0 // 128)
                      for sc in range(nslc)]
            nch = len(pieces)
            for ch in range(min(4, nch)):
                r_, c_ = pieces[ch]
                dma(kc, "scalar", kc.ys(f"b{ch % 4}"), wstage[:, ch % 4, 0:slw],
                    src[r_:r_ + 128, c_:c_ + slw])
            for ch in range(nch):
                gw(kc, "scalar", kc.ys(f"b{ch % 4}"))
                sop(kc, lambda Sc, ch=ch, slw=slw: Sc.activation(
                    wstage2[:, 0:slw], wstage[:, ch % 4, 0:slw], AF.Abs,
                    accum_out=chs[:, ch:ch + 1]))
                if ch + 4 < nch:
                    r_, c_ = pieces[ch + 4]
                    dma(kc, "scalar", kc.ys(f"b{ch % 4}"), wstage[:, ch % 4, 0:slw],
                        src[r_:r_ + 128, c_:c_ + slw])
            t_s = kc.s_s.v
            gw(kc, "vector", kc.s_s, t_s)
            tgv = vop(kc, lambda V, nch=nch: V.tensor_scalar(
                grdt[:, 0:nch], chs[:, 0:nch], 1024.0, MAGIC, ALU.mult, ALU.add))
            gw(kc, "scalar", kc.s_v, tgv)
            tgs = sop(kc, lambda Sc, nch=nch: Sc.activation(
                grdt[:, 0:nch], grdt[:, 0:nch], AF.Identity, bias=kc.cvals[:, 2:3]), sep=True)
            gw(kc, "vector", kc.s_s, tgs)
            vop(kc, lambda V, nch=nch: V.tensor_scalar(
                grdt[:, 0:nch], grdt[:, 0:nch], 1.0 / 1024.0, None, ALU.mult), sep=True)
            vop(kc, lambda V, nch=nch: V.tensor_sub(
                chs[:, 0:nch], chs[:, 0:nch], grdt[:, 0:nch]), sep=True)
            vop(kc, lambda V, nch=nch: V.tensor_reduce(
                kc.stats[:, 16:17], grdt[:, 0:nch], AX.X, ALU.add), sep=True)
            vop(kc, lambda V, nch=nch: V.tensor_reduce(
                kc.stats[:, 17:18], chs[:, 0:nch], AX.X, ALU.add), sep=True)
            tgv = vop(kc, lambda V: V.tensor_scalar(
                kc.stats[:, 18:19], kc.stats[:, 16:17], 1.0 / 16.0, MAGIC,
                ALU.mult, ALU.add), sep=True)
            gw(kc, "scalar", kc.s_v, tgv)
            tgs = sop(kc, lambda Sc: Sc.activation(
                kc.stats[:, 18:19], kc.stats[:, 18:19], AF.Identity,
                bias=kc.cvals[:, 2:3]), sep=True)
            gw(kc, "vector", kc.s_s, tgs)
            vop(kc, lambda V: V.tensor_scalar(
                kc.stats[:, 18:19], kc.stats[:, 18:19], 16.0, None, ALU.mult), sep=True)
            vop(kc, lambda V: V.tensor_sub(
                kc.stats[:, 19:20], kc.stats[:, 16:17], kc.stats[:, 18:19]), sep=True)
            tgv = vop(kc, lambda V: V.tensor_scalar(
                kc.stats[:, 20:21], kc.stats[:, 19:20], 16.0, MAGIC,
                ALU.mult, ALU.add), sep=True)
            gw(kc, "scalar", kc.s_v, tgv)
            tgs = sop(kc, lambda Sc: Sc.activation(
                kc.stats[:, 20:21], kc.stats[:, 20:21], AF.Identity,
                bias=kc.cvals[:, 2:3]), sep=True)
            gw(kc, "vector", kc.s_s, tgs)
            vop(kc, lambda V: V.tensor_scalar(
                kc.stats[:, 20:21], kc.stats[:, 20:21], 1.0 / 16.0, None, ALU.mult), sep=True)
            vop(kc, lambda V: V.tensor_sub(
                kc.stats[:, 21:22], kc.stats[:, 19:20], kc.stats[:, 20:21]), sep=True)
            vop(kc, lambda V: V.tensor_copy(kc.cprb[:, 3:4], kc.stats[:, 17:18]), sep=True)
            vop(kc, lambda V: V.tensor_copy(kc.stats[:, 22:23], kc.cprb[:, 3:4]), sep=True)
            vop(kc, lambda V: V.tensor_sub(
                kc.stats[:, 23:24], kc.stats[:, 17:18], kc.stats[:, 22:23]), sep=True)
            vop(kc, lambda V: V.tensor_copy(kc.cprb[:, 0:1], kc.stats[:, 18:19]), sep=True)
            vop(kc, lambda V: V.tensor_copy(kc.cprb[:, 1:2], kc.stats[:, 20:21]), sep=True)
            vop(kc, lambda V: V.tensor_copy(kc.cprb[:, 2:3], kc.stats[:, 21:22]), sep=True)
            tvp = vop(kc, lambda V: V.tensor_copy(kc.cprb[:, 4:5], kc.stats[:, 23:24]), sep=True)
            b = 6 + (kc.tp_flip % 2)
            kc.tp_flip += 1
            bank_wait(kc, b)
            gw(kc, "tensor", kc.s_v, tvp)
            t_t = top(kc, lambda T, b=b: T.matmul(
                kc.ps[b][0:1, 0:5], kc.ones_bf[:, :], kc.cprb[:, 0:5],
                start=True, stop=True))
            gw(kc, "vector", kc.s_t, t_t)
            tv = vop(kc, lambda V, b=b: V.tensor_copy(kc.kmaxb[0:1, 8:13], kc.ps[b][0:1, 0:5]))
            bank_done(kc, b, kc.s_v, tv)
            vop(kc, lambda V: V.tensor_add(kc.kmaxb[0:1, 0:1], kc.kmaxb[0:1, 12:13],
                                           kc.kmaxb[0:1, 10:11]), sep=True)
            vop(kc, lambda V: V.tensor_add(kc.kmaxb[0:1, 0:1], kc.kmaxb[0:1, 0:1],
                                           kc.kmaxb[0:1, 11:12]), sep=True)
            vop(kc, lambda V: V.tensor_add(kc.kmaxb[0:1, 0:1], kc.kmaxb[0:1, 0:1],
                                           kc.kmaxb[0:1, 9:10]), sep=True)
            vop(kc, lambda V: V.tensor_add(kc.kmaxb[0:1, 0:1], kc.kmaxb[0:1, 0:1],
                                           kc.kmaxb[0:1, 8:9]), sep=True)
            vop(kc, lambda V, n=T0 * T1: V.tensor_scalar(
                kc.kmaxb[0:1, 1:2], kc.kmaxb[0:1, 0:1], float(np.float32(1.0 / n)),
                1e-5, ALU.mult, ALU.max), sep=True)
            vop(kc, lambda V: V.reciprocal(kc.kmaxb[0:1, 2:3], kc.kmaxb[0:1, 1:2]), sep=True)
            vop(kc, lambda V: V.tensor_scalar(kc.kmaxb[0:1, 3:4], kc.kmaxb[0:1, 1:2],
                                              4097.0, None, ALU.mult), sep=True)
            vop(kc, lambda V: V.tensor_sub(kc.kmaxb[0:1, 4:5], kc.kmaxb[0:1, 3:4],
                                           kc.kmaxb[0:1, 1:2]), sep=True)
            vop(kc, lambda V: V.tensor_sub(kc.kmaxb[0:1, 3:4], kc.kmaxb[0:1, 3:4],
                                           kc.kmaxb[0:1, 4:5]), sep=True)
            vop(kc, lambda V: V.tensor_sub(kc.kmaxb[0:1, 4:5], kc.kmaxb[0:1, 1:2],
                                           kc.kmaxb[0:1, 3:4]), sep=True)
            vop(kc, lambda V: V.tensor_scalar(kc.kmaxb[0:1, 5:6], kc.kmaxb[0:1, 2:3],
                                              4097.0, None, ALU.mult), sep=True)
            vop(kc, lambda V: V.tensor_sub(kc.kmaxb[0:1, 6:7], kc.kmaxb[0:1, 5:6],
                                           kc.kmaxb[0:1, 2:3]), sep=True)
            vop(kc, lambda V: V.tensor_sub(kc.kmaxb[0:1, 5:6], kc.kmaxb[0:1, 5:6],
                                           kc.kmaxb[0:1, 6:7]), sep=True)
            vop(kc, lambda V: V.tensor_sub(kc.kmaxb[0:1, 6:7], kc.kmaxb[0:1, 2:3],
                                           kc.kmaxb[0:1, 5:6]), sep=True)
            vop(kc, lambda V: V.tensor_mul(kc.kmaxb[0:1, 7:8], kc.kmaxb[0:1, 3:4],
                                           kc.kmaxb[0:1, 5:6]), sep=True)
            vop(kc, lambda V: V.tensor_scalar(kc.kmaxb[0:1, 7:8], kc.kmaxb[0:1, 7:8],
                                              -1.0, 1.0, ALU.mult, ALU.add), sep=True)
            vop(kc, lambda V: V.tensor_mul(kc.kmaxb[0:1, 13:14], kc.kmaxb[0:1, 3:4],
                                           kc.kmaxb[0:1, 6:7]), sep=True)
            vop(kc, lambda V: V.tensor_sub(kc.kmaxb[0:1, 7:8], kc.kmaxb[0:1, 7:8],
                                           kc.kmaxb[0:1, 13:14]), sep=True)
            vop(kc, lambda V: V.tensor_mul(kc.kmaxb[0:1, 13:14], kc.kmaxb[0:1, 4:5],
                                           kc.kmaxb[0:1, 5:6]), sep=True)
            vop(kc, lambda V: V.tensor_sub(kc.kmaxb[0:1, 7:8], kc.kmaxb[0:1, 7:8],
                                           kc.kmaxb[0:1, 13:14]), sep=True)
            vop(kc, lambda V: V.tensor_mul(kc.kmaxb[0:1, 13:14], kc.kmaxb[0:1, 4:5],
                                           kc.kmaxb[0:1, 6:7]), sep=True)
            vop(kc, lambda V: V.tensor_sub(kc.kmaxb[0:1, 7:8], kc.kmaxb[0:1, 7:8],
                                           kc.kmaxb[0:1, 13:14]), sep=True)
            vop(kc, lambda V: V.tensor_mul(kc.kmaxb[0:1, 7:8], kc.kmaxb[0:1, 7:8],
                                           kc.kmaxb[0:1, 2:3]), sep=True)
            vop(kc, lambda V: V.tensor_add(kc.kmaxb[0:1, 2:3], kc.kmaxb[0:1, 2:3],
                                           kc.kmaxb[0:1, 7:8]), sep=True)
            tvb = kc.s_v.v
            gw(kc, "scalar", kc.s_v, tvb)
            dma(kc, "scalar", kc.sd_st, bass.AP(kc.scrM, 0, [[1, 1], [1, 1]]),
                kc.kmaxb[0:1, 1:2])
            dma(kc, "scalar", kc.sd_st, bass.AP(kc.scr128, 0, [[1, 1], [1, 1]]),
                kc.kmaxb[0:1, 2:3])
            gw(kc, "scalar", kc.sd_st)
            dma(kc, "scalar", kc.sd_st, kc.stats[:, mwcol:mwcol + 1],
                bass.AP(kc.scrM, 0, [[0, 128], [1, 1]]))
            dma(kc, "scalar", kc.sd_st, kc.stats[:, swcol:swcol + 1],
                bass.AP(kc.scr128, 0, [[0, 128], [1, 1]]))
            gw(kc, "scalar", kc.sd_st)
            gw(kc, "vector", kc.sd_st)
            if getattr(kc, "debug2", False):
                dma(kc, "scalar", kc.sd_out,
                    kc.dbg["dbg_kmx"][0:16, (0 if key.endswith("up") else 128):
                                      (128 if key.endswith("up") else 256)],
                    kc.kmaxb[0:16, :])
                dma(kc, "scalar", kc.sd_out,
                    kc.dbg["dbg_acc"][:, (0 if key.endswith("up") else 4):
                                      (4 if key.endswith("up") else 8)],
                    kc.stats[:, 16:20])
            for ch in range(min(4, nch)):
                r_, c_ = pieces[ch]
                dma(kc, "scalar", kc.ys(f"b{ch % 4}"), wstage[:, ch % 4, 0:slw],
                    src[r_:r_ + 128, c_:c_ + slw])
            tv = None
            for ch in range(nch):
                r_, c_ = pieces[ch]
                gw(kc, "scalar", kc.ys(f"b{ch % 4}"))
                if ch >= 1:
                    gw(kc, "vector", kc.ys("bq"))
                gw(kc, "vector", kc.ys(f"b{ch % 4}"))
                tm = vop(kc, lambda V, ch=ch, slw=slw, c=swcol: V.tensor_scalar(
                    wstage2[:, 0:slw], wstage[:, ch % 4, 0:slw], kc.stats[:, c:c + 1],
                    None, ALU.mult))
                gw(kc, "scalar", kc.s_v, tm)
                sop(kc, lambda Sc, slw=slw: Sc.activation(
                    wstage2[:, 0:slw], wstage2[:, 0:slw], AF.Identity, bias=kc.cvals[:, 1:2]))
                t_r = sop(kc, lambda Sc, slw=slw: Sc.activation(
                    wstage2[:, 0:slw], wstage2[:, 0:slw], AF.Identity, bias=kc.cvals[:, 2:3]),
                    sep=True)
                gw(kc, "vector", kc.s_s, t_r)
                tv = vop(kc, lambda V, slw=slw: V.tensor_scalar(
                    wqb[:, 0:slw], wstage2[:, 0:slw], -1.0, 1.0, ALU.max, ALU.min))
                gw(kc, "scalar", kc.s_v, tv)
                dma(kc, "scalar", kc.ys("bq"), qdst[r_:r_ + 128, c_:c_ + slw],
                    wqb[:, 0:slw])
                if ch + 4 < nch:
                    r_, c_ = pieces[ch + 4]
                    dma(kc, "scalar", kc.ys(f"b{ch % 4}"), wstage[:, ch % 4, 0:slw],
                        src[r_:r_ + 128, c_:c_ + slw])


def emit_bitnet_ffn(kc):
    nc = kc.nc
    with contextlib.ExitStack() as actx:
        ee = actx.enter_context
        featbuf = ee(nc.sbuf_tensor("featbufB", [128, 16, 1024], BF16))
        xn32 = ee(nc.sbuf_tensor("xn32", [128, D], F32))
        t32 = ee(nc.sbuf_tensor("t32", [128, D], F32))
        wnb = ee(nc.sbuf_tensor("wnb", [128, D], F32))
        rbuf = ee(nc.sbuf_tensor("rbuf", [128, 256], F32))
        tbuf = ee(nc.sbuf_tensor("tbuf", [128, 256], F32))
        hbuf = ee(nc.sbuf_tensor("hbuf", [128, 2, 256], BF16))
        phase_barrier(kc)

        t_wn = dma(kc, "scalar", kc.sd_st, wnb[:, :],
                   bass.AP(kc.W["L0_ffn_norm_weight"], 0, [[0, 128], [1, D]]))
        vop(kc, lambda V: V.memset(kc.pvec3[:, :], 0.0))
        txq_prev = None
        for tt in range(CT):
            xv = emit_ln(kc, tt, xn32[:, :], rms_col=tt)
            gw(kc, "vector", kc.sd_st)
            vop(kc, lambda V: V.tensor_mul(xn32[:, :], xn32[:, :], wnb[:, :]), sep=True)
            tv = vop(kc, lambda V: V.tensor_reduce(
                kc.stats[:, 16:17], xn32[:, :], AX.X, ALU.max, apply_absolute_value=True),
                sep=True)
            vop(kc, lambda V, tt=tt: V.tensor_mul(kc.stats[:, 17:18],
                                                  kc.pvec0[:, tt:tt + 1], kc.stats[:, 16:17]),
                sep=True)
            vop(kc, lambda V: V.tensor_scalar(kc.stats[:, 17:18], kc.stats[:, 17:18],
                                              1e-5, None, ALU.max), sep=True)
            vop(kc, lambda V: V.reciprocal(kc.stats[:, 18:19], kc.stats[:, 17:18]), sep=True)
            vop(kc, lambda V, tt=tt: V.tensor_mul(kc.stats[:, 19:20],
                                                  kc.stats[:, 18:19], kc.pvec0[:, tt:tt + 1]),
                sep=True)
            vop(kc, lambda V: V.tensor_scalar(kc.stats[:, 19:20], kc.stats[:, 19:20],
                                              127.0, None, ALU.mult), sep=True)
            vop(kc, lambda V: V.tensor_scalar(kc.stats[:, 20:21], kc.stats[:, 17:18],
                                              1.0 / 127.0, None, ALU.mult))
            vop(kc, lambda V: V.tensor_mul(kc.stats[:, 20:21], kc.stats[:, 20:21],
                                           kc.stats[:, 10:11]), sep=True)
            vop(kc, lambda V: V.tensor_mul(kc.stats[:, 21:22], kc.stats[:, 20:21],
                                           kc.stats[:, 20:21]), sep=True)
            vop(kc, lambda V, tt=tt: V.tensor_mul(kc.pvec2[:, tt:tt + 1],
                                                  kc.stats[:, 21:22], kc.stats[:, 20:21]),
                sep=True)
            if txq_prev is not None:
                gw(kc, "vector", kc.s_s, txq_prev)
            tq = vop(kc, lambda V: V.tensor_scalar(t32[:, :], xn32[:, :],
                                                   kc.stats[:, 19:20], MAGIC,
                                                   ALU.mult, ALU.add))
            gw(kc, "scalar", kc.s_v, tq)
            buf_wait(kc, "scalar", f"xn{tt % 2}")
            txq = sop(kc, lambda Sc, tt=tt: Sc.activation(
                kc.xn_blk[:, tt % 2, :], t32[:, :], AF.Identity, bias=kc.cvals[:, 2:3]))
            txq_prev = txq
            if getattr(kc, "debug2", False) and tt == 0:
                gw(kc, "scalar", kc.s_s, txq)
                dma(kc, "scalar", kc.sd_out, kc.dbg["dbg_xq"][:, :], kc.xn_blk[:, 0, :])
                dma(kc, "scalar", kc.sd_out, kc.dbg["dbg_sc"][:, :], kc.stats[:, :])
            gw(kc, "tensor", kc.s_s, txq)
            emit_t2f(kc, (lambda kcc, tt=tt: kc.xn_blk[:, tt % 2, kcc * 128:(kcc + 1) * 128]),
                     tt * 128, 16, featbuf)
            buf_done(kc, f"xn{tt % 2}", kc.s_t, kc.s_t.v)
        featbuf_ready = kc.s_v.v
        gw(kc, "tensor", kc.s_v, featbuf_ready)
        gw(kc, "sync", kc.ys("bq"))
        prev_tb = None
        for fs in range(22):
            pp = fs % 2
            gbase, ubase = pp * 4096, pp * 4096
            buf_wait(kc, "sync", f"a0h{pp}")
            t_g = slab_load(kc, kc.arena[0], gbase, 256, kc.gu_q.ap(), fs * 256, 256, 16,
                            sem=kc.ys(f"a0h{pp}"), f32=False)
            buf_wait(kc, "sync", f"a1h{pp}")
            t_u = slab_load(kc, kc.arena[1], ubase, 256, kc.gu_q.ap(),
                            F0 + fs * 256, 256, 16, sem=kc.ys(f"a1h{pp}"), f32=False)
            gw(kc, "tensor", *t_g)
            gw(kc, "tensor", *t_u)
            t_last = None
            for tt in range(CT):
                bank_wait(kc, 4)
                for i in range(16):
                    top(kc, lambda T, i=i, tt=tt, gbase=gbase: T.matmul(
                        kc.ps[4][:, 0:256], featbuf[:, i, tt * 128:(tt + 1) * 128],
                        kc.arena[0][:, gbase + i * 256:gbase + (i + 1) * 256],
                        start=(i == 0), stop=(i == 15)), bump=(i == 15))
                bank_wait(kc, 5)
                for i in range(16):
                    top(kc, lambda T, i=i, tt=tt, ubase=ubase: T.matmul(
                        kc.ps[5][:, 0:256], featbuf[:, i, tt * 128:(tt + 1) * 128],
                        kc.arena[1][:, ubase + i * 256:ubase + (i + 1) * 256],
                        start=(i == 0), stop=(i == 15)), bump=(i == 15))
                t_last = kc.s_t.v
                gw(kc, "scalar", kc.s_t, t_last)
                if prev_tb is not None:
                    gw(kc, "scalar", kc.s_v, prev_tb)
                t_re = sop(kc, lambda Sc: Sc.activation(rbuf[:, :], kc.ps[4][:, 0:256],
                                                        AF.Relu))
                gw(kc, "vector", kc.s_s, t_re)
                prev_tb = vop(kc, lambda V: V.scalar_tensor_tensor(
                    tbuf[:, :], kc.ps[4][:, 0:256], 1.0, rbuf[:, :], ALU.mult, ALU.mult))
                buf_wait(kc, "vector", f"hb{tt % 2}")
                tv = vop(kc, lambda V, tt=tt: V.scalar_tensor_tensor(
                    hbuf[:, tt % 2, :], kc.ps[5][:, 0:256], 1.0, tbuf[:, :],
                    ALU.mult, ALU.mult), sep=True)
                bank_done(kc, 4, kc.s_v, tv)
                bank_done(kc, 5, kc.s_v, tv)
                vop(kc, lambda V, tt=tt: V.tensor_reduce(
                    kc.stats[:, 16:17], hbuf[:, tt % 2, :], AX.X, ALU.max,
                    apply_absolute_value=True), sep=True)
                vop(kc, lambda V, tt=tt: V.tensor_max(
                    kc.pvec3[:, tt:tt + 1], kc.pvec3[:, tt:tt + 1], kc.stats[:, 16:17]),
                    sep=True)
                gw(kc, "scalar", kc.s_v, kc.s_v.v)
                dma(kc, "scalar", kc.ys("h"),
                    kc.h_dram[tt * 128:(tt + 1) * 128, fs * 256:(fs + 1) * 256],
                    hbuf[:, tt % 2, :])
                buf_done(kc, f"hb{tt % 2}", kc.ys("h"), None)
            buf_done(kc, f"a0h{pp}", kc.s_t, t_last)
            buf_done(kc, f"a1h{pp}", kc.s_t, t_last)
        if getattr(kc, "debug2", False):
            gw(kc, "gpsimd", kc.ys("h"))
            dma(kc, "gpsimd", kc.sd_out, kc.dbg["dbg_h"][:, :], kc.h_dram[0:128, :])
            dma(kc, "gpsimd", kc.sd_out, kc.dbg["dbg_guq"][:, :], kc.gu_q[:, :])
            dma(kc, "gpsimd", kc.sd_out, kc.dbg["dbg_wdq"][:, :], kc.wd_q[:, :])
            gw(kc, "gpsimd", kc.s_v, kc.s_v.v)
            dma(kc, "gpsimd", kc.sd_out, kc.dbg["dbg_pv"][:, :], kc.pvec2[:, :])
            dma(kc, "gpsimd", kc.sd_out, kc.dbg["dbg_pv3"][:, :], kc.pvec3[:, :])

    with contextlib.ExitStack() as actx:
        ee = actx.enter_context
        hqT = ee(nc.sbuf_tensor("hqT", [128, 44, 512], BF16))
        t32h = ee(nc.sbuf_tensor("t32h", [128, F0], F32))
        hb = ee(nc.sbuf_tensor("hb", [128, F0], BF16))
        phase_barrier(kc)

        prev_tp = None
        prev_tq = None
        for halfb in range(2):
            for ttl in range(4):
                tt = halfb * 4 + ttl
                if prev_tp is not None:
                    gw(kc, "scalar", kc.s_t, prev_tp)
                dma(kc, "scalar", kc.ys("h"), hb[:, :],
                    kc.h_dram[tt * 128:(tt + 1) * 128, :])
                vop(kc, lambda V, tt=tt: V.tensor_mul(
                    kc.stats[:, 16:17], kc.pvec2[:, tt:tt + 1], kc.pvec3[:, tt:tt + 1]))
                vop(kc, lambda V: V.tensor_scalar(kc.stats[:, 16:17], kc.stats[:, 16:17],
                                                  1e-5, None, ALU.max), sep=True)
                vop(kc, lambda V: V.reciprocal(kc.stats[:, 17:18], kc.stats[:, 16:17]), sep=True)
                vop(kc, lambda V, tt=tt: V.tensor_mul(kc.stats[:, 18:19],
                                                      kc.stats[:, 17:18],
                                                      kc.pvec2[:, tt:tt + 1]), sep=True)
                vop(kc, lambda V: V.tensor_scalar(kc.stats[:, 18:19], kc.stats[:, 18:19],
                                                  127.0, None, ALU.mult), sep=True)
                vop(kc, lambda V: V.tensor_scalar(kc.stats[:, 19:20], kc.stats[:, 16:17],
                                                  1.0 / 127.0, None, ALU.mult))
                vop(kc, lambda V, tt=tt: V.tensor_mul(kc.pvec4[:, tt:tt + 1],
                                                      kc.stats[:, 19:20],
                                                      kc.stats[:, 11:12]), sep=True)
                gw(kc, "vector", kc.ys("h"))
                if prev_tq is not None:
                    gw(kc, "vector", kc.s_s, prev_tq)
                tq = vop(kc, lambda V: V.tensor_scalar(
                    t32h[:, :], hb[:, :], kc.stats[:, 18:19], MAGIC, ALU.mult, ALU.add))
                gw(kc, "scalar", kc.s_v, tq)
                thq = sop(kc, lambda Sc: Sc.activation(hb[:, :], t32h[:, :],
                                                       AF.Identity, bias=kc.cvals[:, 2:3]))
                prev_tq = thq
                for g in range(11):
                    bk = 6 + (kc.tp_flip % 2)
                    kc.tp_flip += 1
                    bank_wait(kc, bk)
                    if g == 0:
                        gw(kc, "tensor", kc.s_s, thq)
                    for j in range(4):
                        kcc = g * 4 + j
                        top(kc, lambda T, bk=bk, j=j, kcc=kcc: T.matmul(
                            kc.ps[bk][:, j * 128:(j + 1) * 128],
                            hb[:, kcc * 128:(kcc + 1) * 128], kc.ident[:, :],
                            start=True, stop=True), bump=(j == 3))
                    gw(kc, "vector", kc.s_t, kc.s_t.v)
                    prev_tp = kc.s_t.v
                    tv = vop(kc, lambda V, bk=bk, g=g, ttl=ttl: V.tensor_copy(
                        bass.AP(hqT, g * 4 * 512 + ttl * 128,
                                [[44 * 512, 128], [512, 4], [1, 128]]),
                        bass.AP(kc.ps[bk], 0, [[512, 128], [128, 4], [1, 128]])))
                    bank_done(kc, bk, kc.s_v, tv)
            hq_ready = kc.s_v.v
            for sl in range(8):
                pp = sl % 2
                buf_wait(kc, "sync", f"a{pp}")
                t_w = slab_load(kc, kc.arena[pp], 0, 256, kc.wd_q.ap(), sl * 256, 256, 44,
                                sem=kc.ys(f"a{pp}"), f32=False)
                if getattr(kc, "debug2", False) and halfb == 1 and sl == 4:
                    gw(kc, "gpsimd", kc.ys(f"a{pp}"))
                    dma(kc, "gpsimd", kc.sd_out, kc.dbg["dbg_slab"][:, :],
                        kc.arena[pp][:, 29 * 256:30 * 256])
                    gw(kc, "gpsimd", kc.s_v, hq_ready)
                    dma(kc, "gpsimd", kc.sd_out, kc.dbg["dbg_hqt"][:, :],
                        bass.AP(hqT, 29 * 512, [[44 * 512, 128], [1, 512]]))
                gw(kc, "tensor", *t_w)
                gw(kc, "tensor", kc.s_v, hq_ready)
                t_last = None
                for ttl in range(4):
                    tt = halfb * 4 + ttl
                    b = ttl % 4
                    bank_wait(kc, b)
                    for i in range(44):
                        top(kc, lambda T, b=b, i=i, ttl=ttl, pp=pp: T.matmul(
                            kc.ps[b][:, 0:256], hqT[:, i, ttl * 128:(ttl + 1) * 128],
                            kc.arena[pp][:, i * 256:(i + 1) * 256],
                            start=(i == 0), stop=(i == 43)), bump=(i == 43))
                    t_mm = kc.s_t.v
                    t_last = t_mm
                    gw(kc, "vector", kc.s_t, t_mm)
                    tv = vop(kc, lambda V, b=b, tt=tt, sl=sl: V.scalar_tensor_tensor(
                        kc.x_res[:, tt, sl * 256:(sl + 1) * 256], kc.ps[b][:, 0:256],
                        kc.pvec4[:, tt:tt + 1],
                        kc.x_res[:, tt, sl * 256:(sl + 1) * 256], ALU.mult, ALU.add))
                    bank_done(kc, b, kc.s_v, tv)
                buf_done(kc, f"a{pp}", kc.s_t, t_last)
        kc.x_ready_v = kc.s_v.v


def emit_moe(kc):
    nc = kc.nc
    with contextlib.ExitStack() as actx:
        ee = actx.enter_context
        featbuf = ee(nc.sbuf_tensor("featbufD", [128, 16, 1024], BF16))
        hT = ee(nc.sbuf_tensor("hT", [128, 11, 1024], BF16))
        gwT = ee(nc.sbuf_tensor("gwT", [128, 16, 8], BF16))
        sgwT = ee(nc.sbuf_tensor("sgwT", [128, 16, 1], BF16))
        cmb = ee(nc.sbuf_tensor("cmb", [128, CT, NEXP], F32))
        slb = ee(nc.sbuf_tensor("slb", [128, 512], BF16))
        gwsb = ee(nc.sbuf_tensor("gwsb", [8, D], BF16))
        sgwsb = ee(nc.sbuf_tensor("sgwsb", [1, D], BF16))
        gwf = ee(nc.sbuf_tensor("gwf", [8, D], F32))
        sgwf = ee(nc.sbuf_tensor("sgwf", [1, D], F32))
        phase_barrier(kc)

        for tt in range(CT):
            buf_wait(kc, "vector", f"xn{tt % 2}")
            xv = emit_ln(kc, tt, kc.xn_blk[:, tt % 2, :])
            emit_t2f(kc, (lambda kcc, tt=tt: kc.xn_blk[:, tt % 2, kcc * 128:(kcc + 1) * 128]),
                     tt * 128, 16, featbuf, src_ready=(kc.s_v, xv))
            buf_done(kc, f"xn{tt % 2}", kc.s_t, kc.s_t.v)

        featbuf_ready = kc.s_v.v
        gw(kc, "tensor", kc.s_v, featbuf_ready)
        kc._slb_prev = None
        dma(kc, "scalar", kc.sd_st, gwf[:, :], kc.W["L1_ffn_gate_w"][:, :])
        dma(kc, "scalar", kc.sd_st, sgwf[:, :], kc.W["L1_ffn_shared_gate_w"][:, :])
        gw(kc, "vector", kc.sd_st)
        vop(kc, lambda V: V.tensor_copy(gwsb[:, :], gwf[:, :]))
        tgc = vop(kc, lambda V: V.tensor_copy(sgwsb[:, :], sgwf[:, :]))
        gw(kc, "tensor", kc.s_v, tgc)
        for i in range(16):
            bk = 6 + (kc.tp_flip % 2)
            kc.tp_flip += 1
            bank_wait(kc, bk)
            top(kc, lambda T, bk=bk, i=i: T.matmul(
                kc.ps[bk][:, 0:8], gwsb[0:8, i * 128:(i + 1) * 128],
                kc.ident[0:8, 0:8], start=True, stop=True), bump=False)
            top(kc, lambda T, bk=bk, i=i: T.matmul(
                kc.ps[bk][:, 8:9], sgwsb[0:1, i * 128:(i + 1) * 128],
                kc.ident[0:1, 0:1], start=True, stop=True))
            gw(kc, "vector", kc.s_t, kc.s_t.v)
            vop(kc, lambda V, bk=bk, i=i: V.tensor_copy(gwT[:, i, :], kc.ps[bk][:, 0:8]))
            tv = vop(kc, lambda V, bk=bk, i=i: V.tensor_copy(sgwT[:, i, :], kc.ps[bk][:, 8:9]))
            bank_done(kc, bk, kc.s_v, tv)

        gw(kc, "tensor", kc.s_v, kc.s_v.v)
        prev_cmb = None
        for tt in range(CT):
            bank_wait(kc, 4)
            for i in range(16):
                top(kc, lambda T, i=i, tt=tt: T.matmul(
                    kc.ps[4][:, 0:8], featbuf[:, i, tt * 128:(tt + 1) * 128], gwT[:, i, :],
                    start=(i == 0), stop=(i == 15)), bump=(i == 15))
            bank_wait(kc, 5)
            for i in range(16):
                top(kc, lambda T, i=i, tt=tt: T.matmul(
                    kc.ps[5][:, 0:1], featbuf[:, i, tt * 128:(tt + 1) * 128], sgwT[:, i, :],
                    start=(i == 0), stop=(i == 15)), bump=(i == 15))
            t_mm = kc.s_t.v
            gw(kc, "vector", kc.s_t, t_mm)
            vop(kc, lambda V: V.tensor_reduce(kc.stats[:, 16:17], kc.ps[4][:, 0:8],
                                              AX.X, ALU.max))
            tv = vop(kc, lambda V: V.tensor_scalar(kc.stats[:, 16:17], kc.stats[:, 16:17],
                                                   -1.0, None, ALU.mult), sep=True)
            gw(kc, "scalar", kc.s_v, tv)
            if prev_cmb is not None:
                gw(kc, "scalar", kc.s_v, prev_cmb)
            t_e = sop(kc, lambda Sc: Sc.activation(kc.stats[:, 20:28], kc.ps[4][:, 0:8],
                                                   AF.Exp, bias=kc.stats[:, 16:17]))
            t_g = sop(kc, lambda Sc, tt=tt: Sc.activation(
                kc.pvec5[:, tt:tt + 1], kc.ps[5][:, 0:1], AF.Sigmoid))
            bank_done(kc, 4, kc.s_s, t_e)
            bank_done(kc, 5, kc.s_s, t_g)
            gw(kc, "vector", kc.s_s, t_g)
            vop(kc, lambda V: V.tensor_reduce(kc.stats[:, 16:17], kc.stats[:, 20:28],
                                              AX.X, ALU.add))
            vop(kc, lambda V: V.reciprocal(kc.stats[:, 17:18], kc.stats[:, 16:17]), sep=True)
            vop(kc, lambda V: V.tensor_scalar(kc.stats[:, 20:28], kc.stats[:, 20:28],
                                              kc.stats[:, 17:18], None, ALU.mult), sep=True)
            scr8 = kc.svec[:, 32:40]
            vop(kc, lambda V: V.tensor_reduce(kc.stats[:, 16:17], kc.stats[:, 20:28],
                                              AX.X, ALU.max), sep=True)
            vop(kc, lambda V, scr8=scr8: V.tensor_scalar(
                scr8, kc.stats[:, 20:28], kc.stats[:, 16:17], None, ALU.is_ge), sep=True)
            vop(kc, lambda V, scr8=scr8: V.scalar_tensor_tensor(
                scr8, scr8, -1e30, kc.stats[:, 20:28], ALU.mult, ALU.add), sep=True)
            vop(kc, lambda V, scr8=scr8: V.tensor_reduce(
                kc.stats[:, 18:19], scr8, AX.X, ALU.max), sep=True)
            vop(kc, lambda V: V.tensor_tensor(kc.stats[:, 18:19], kc.stats[:, 18:19],
                                              kc.stats[:, 16:17], ALU.min), sep=True)
            vop(kc, lambda V, scr8=scr8: V.tensor_scalar(
                scr8, kc.stats[:, 20:28], kc.stats[:, 18:19], None, ALU.is_ge), sep=True)
            prev_cmb = vop(kc, lambda V, tt=tt, scr8=scr8: V.tensor_mul(
                cmb[:, tt, :], kc.stats[:, 20:28], scr8), sep=True)

        def fc_slabs(e_idx, which, fs):
            key = "L1_ffn_expert_fc1" if which == 0 else "L1_ffn_expert_fc2"
            a = kc.arena[which]
            pp = fs % 2
            base = pp * 4096
            nm = f"a{which}h{pp}"
            buf_wait(kc, "scalar", nm)
            w_ = 256 if fs < 5 else 128
            t = slab_load(kc, a, base, 256, kc.W[key].ap(),
                          e_idx * FM + fs * 256, w_, 16, sem=kc.ys(nm))
            return nm, base, w_, t

        for e_idx in range(NEXP):
            pend = (fc_slabs(e_idx, 0, 0), fc_slabs(e_idx, 1, 0))
            for fs in range(6):
                nxt = ((fc_slabs(e_idx, 0, fs + 1), fc_slabs(e_idx, 1, fs + 1))
                       if fs + 1 < 6 else None)
                (nm1, base1, w_, t1), (nm2, base2, _, t2) = pend
                pend = nxt
                gw(kc, "tensor", *t1)
                gw(kc, "tensor", *t2)
                t_last = None
                for fcq in range(w_ // 128):
                    fcc = fs * 2 + fcq
                    for qc in range(2):
                        bank_wait(kc, 4)
                        for i in range(16):
                            top(kc, lambda T, i=i, qc=qc, base1=base1, fcq=fcq: T.matmul(
                                kc.ps[4][:, 0:512],
                                kc.arena[0][:, base1 + i * 256 + fcq * 128:
                                            base1 + i * 256 + (fcq + 1) * 128],
                                featbuf[:, i, qc * 512:(qc + 1) * 512],
                                start=(i == 0), stop=(i == 15)), bump=(i == 15))
                        t_m1 = kc.s_t.v
                        gw(kc, "scalar", kc.s_t, t_m1)
                        if kc._slb_prev is not None:
                            gw(kc, "scalar", kc.s_v, kc._slb_prev)
                        t_si = sop(kc, lambda Sc: Sc.activation(slb[:, :], kc.ps[4][:, 0:512],
                                                                AF.Silu))
                        bank_done(kc, 4, kc.s_s, t_si)
                        bank_wait(kc, 5)
                        for i in range(16):
                            top(kc, lambda T, i=i, qc=qc, base2=base2, fcq=fcq: T.matmul(
                                kc.ps[5][:, 0:512],
                                kc.arena[1][:, base2 + i * 256 + fcq * 128:
                                            base2 + i * 256 + (fcq + 1) * 128],
                                featbuf[:, i, qc * 512:(qc + 1) * 512],
                                start=(i == 0), stop=(i == 15)), bump=(i == 15))
                        t_m2 = kc.s_t.v
                        t_last = t_m2
                        gw(kc, "vector", kc.s_t, t_m2)
                        gw(kc, "vector", kc.s_s, t_si)
                        tv = vop(kc, lambda V, fcc=fcc, qc=qc: V.scalar_tensor_tensor(
                            hT[:, fcc, qc * 512:(qc + 1) * 512], kc.ps[5][:, 0:512],
                            1.0, slb[:, :], ALU.mult, ALU.mult))
                        kc._slb_prev = tv
                        bank_done(kc, 5, kc.s_v, tv)
                buf_done(kc, nm1, kc.s_t, t_last)
                buf_done(kc, nm2, kc.s_t, t_last)
            hT_ready = kc.s_v.v
            pw = kc.W["L1_ffn_expert_proj"].ap()

            def load_proj(sl, e_idx=e_idx, pw=pw):
                pp = sl % 2
                nm = f"aC{pp}"
                buf_wait(kc, "scalar", nm)
                t = slab_load(kc, kc.arena[pp], 8448, 256, pw,
                              e_idx * D + sl * 256, 256, 11, sem=kc.ys(nm))
                return nm, pp, t
            pendp = load_proj(0)
            for sl in range(8):
                nxtp = load_proj(sl + 1) if sl + 1 < 8 else None
                nm, pp, t_w = pendp
                pendp = nxtp
                gw(kc, "tensor", *t_w)
                gw(kc, "tensor", kc.s_v, hT_ready)
                t_last = None
                for tt in range(CT):
                    b = tt % 4
                    bank_wait(kc, b)
                    for i in range(11):
                        top(kc, lambda T, b=b, i=i, tt=tt, pp=pp: T.matmul(
                            kc.ps[b][:, 0:256], hT[:, i, tt * 128:(tt + 1) * 128],
                            kc.arena[pp][:, 8448 + i * 256:8448 + (i + 1) * 256],
                            start=(i == 0), stop=(i == 10)), bump=(i == 10))
                    t_mm = kc.s_t.v
                    t_last = t_mm
                    gw(kc, "vector", kc.s_t, t_mm)
                    tv = vop(kc, lambda V, b=b, tt=tt, sl=sl, e_idx=e_idx:
                             V.scalar_tensor_tensor(
                                 kc.x_res[:, tt, sl * 256:(sl + 1) * 256],
                                 kc.ps[b][:, 0:256], cmb[:, tt, e_idx:e_idx + 1],
                                 kc.x_res[:, tt, sl * 256:(sl + 1) * 256],
                                 ALU.mult, ALU.add))
                    bank_done(kc, b, kc.s_v, tv)
                buf_done(kc, nm, kc.s_t, t_last)

        def shared_fc_load(fbase, fs):
            w_ = 256 if fs < 5 else 128
            pp = fs % 2
            nm1, nm2 = f"a0h{pp}", f"a1h{pp}"
            buf_wait(kc, "scalar", nm1)
            t1 = slab_load(kc, kc.arena[0], pp * 4096, 256,
                           kc.W["L1_ffn_shared_fc1"].ap(), fbase + fs * 256, w_, 16,
                           sem=kc.ys(nm1))
            buf_wait(kc, "scalar", nm2)
            t2 = slab_load(kc, kc.arena[1], pp * 4096, 256,
                           kc.W["L1_ffn_shared_fc2"].ap(), fbase + fs * 256, w_, 16,
                           sem=kc.ys(nm2))
            return w_, pp, nm1, nm2, t1, t2

        for fb in range(4):
            fbase = fb * FM
            pend_s = shared_fc_load(fbase, 0)
            for fs in range(6):
                nxt_s = shared_fc_load(fbase, fs + 1) if fs + 1 < 6 else None
                w_, pp, nm1, nm2, t1, t2 = pend_s
                pend_s = nxt_s
                gw(kc, "tensor", *t1)
                gw(kc, "tensor", *t2)
                t_last = None
                for fcq in range(w_ // 128):
                    fcc = fs * 2 + fcq
                    for qc in range(2):
                        bank_wait(kc, 4)
                        for i in range(16):
                            top(kc, lambda T, i=i, qc=qc, pp=pp, fcq=fcq: T.matmul(
                                kc.ps[4][:, 0:512],
                                kc.arena[0][:, pp * 4096 + i * 256 + fcq * 128:
                                            pp * 4096 + i * 256 + (fcq + 1) * 128],
                                featbuf[:, i, qc * 512:(qc + 1) * 512],
                                start=(i == 0), stop=(i == 15)), bump=(i == 15))
                        t_m1 = kc.s_t.v
                        gw(kc, "scalar", kc.s_t, t_m1)
                        if kc._slb_prev is not None:
                            gw(kc, "scalar", kc.s_v, kc._slb_prev)
                        t_si = sop(kc, lambda Sc: Sc.activation(slb[:, :], kc.ps[4][:, 0:512],
                                                                AF.Silu))
                        bank_done(kc, 4, kc.s_s, t_si)
                        bank_wait(kc, 5)
                        for i in range(16):
                            top(kc, lambda T, i=i, qc=qc, pp=pp, fcq=fcq: T.matmul(
                                kc.ps[5][:, 0:512],
                                kc.arena[1][:, pp * 4096 + i * 256 + fcq * 128:
                                            pp * 4096 + i * 256 + (fcq + 1) * 128],
                                featbuf[:, i, qc * 512:(qc + 1) * 512],
                                start=(i == 0), stop=(i == 15)), bump=(i == 15))
                        t_m2 = kc.s_t.v
                        t_last = t_m2
                        gw(kc, "vector", kc.s_t, t_m2)
                        gw(kc, "vector", kc.s_s, t_si)
                        tv = vop(kc, lambda V, fcc=fcc, qc=qc: V.scalar_tensor_tensor(
                            hT[:, fcc, qc * 512:(qc + 1) * 512], kc.ps[5][:, 0:512],
                            1.0, slb[:, :], ALU.mult, ALU.mult))
                        kc._slb_prev = tv
                        bank_done(kc, 5, kc.s_v, tv)
                buf_done(kc, nm1, kc.s_t, t_last)
                buf_done(kc, nm2, kc.s_t, t_last)
            hT_ready = kc.s_v.v
            def load_sproj(sl, fbase=fbase):
                pp = sl % 2
                nm = f"aC{pp}"
                buf_wait(kc, "scalar", nm)
                t = slab_load(kc, kc.arena[pp], 8448, 256,
                              kc.W["L1_ffn_shared_proj"].ap(), sl * 256, 256, 11,
                              sem=kc.ys(nm), r0=fbase)
                return nm, pp, t
            pendq = load_sproj(0)
            for sl in range(8):
                nxtq = load_sproj(sl + 1) if sl + 1 < 8 else None
                nm, pp, t_w = pendq
                pendq = nxtq
                gw(kc, "tensor", *t_w)
                gw(kc, "tensor", kc.s_v, hT_ready)
                t_last = None
                for tt in range(CT):
                    b = tt % 4
                    bank_wait(kc, b)
                    for i in range(11):
                        top(kc, lambda T, b=b, i=i, tt=tt, pp=pp: T.matmul(
                            kc.ps[b][:, 0:256], hT[:, i, tt * 128:(tt + 1) * 128],
                            kc.arena[pp][:, 8448 + i * 256:8448 + (i + 1) * 256],
                            start=(i == 0), stop=(i == 10)), bump=(i == 10))
                    t_mm = kc.s_t.v
                    t_last = t_mm
                    gw(kc, "vector", kc.s_t, t_mm)
                    tv = vop(kc, lambda V, b=b, tt=tt, sl=sl: V.scalar_tensor_tensor(
                        kc.x_res[:, tt, sl * 256:(sl + 1) * 256], kc.ps[b][:, 0:256],
                        kc.pvec5[:, tt:tt + 1],
                        kc.x_res[:, tt, sl * 256:(sl + 1) * 256], ALU.mult, ALU.add))
                    bank_done(kc, b, kc.s_v, tv)
                buf_done(kc, nm, kc.s_t, t_last)
        kc.x_ready_v = kc.s_v.v


def build(debug=False):
    nc = bass.Bass("TRN2", target_bir_lowering=False)
    kc = KC()
    kc.nc = nc
    kc.tp_flip = 0
    kc.bankdep = {}
    kc.bufdep = {}
    kc.x_ready_v = 0

    kc.xin = nc.dram_tensor("xin", [QTOK, D], F32, kind="ExternalInput")
    kc.ident_in = nc.dram_tensor("ident_in", [128, 128], BF16, kind="ExternalInput")
    kc.mask_in = nc.dram_tensor("masktbl", [128, 16 * 512], BF16, kind="ExternalInput")
    kc.W = {k: nc.dram_tensor(k, list(sh), F32, kind="ExternalInput")
            for k, sh in WSHAPES.items()}
    kc.xout = nc.dram_tensor("xout", [QTOK, D], F32, kind="ExternalOutput")
    kc.dbg = {}
    if debug:
        for nm in ("dbg_x1", "dbg_x2", "dbg_x3"):
            kc.dbg[nm] = nc.dram_tensor(nm, [QTOK, D], F32, kind="ExternalOutput")
    kc.debug2 = bool(debug) and DEBUG2
    if kc.debug2:
        kc.dbg["dbg_fb"] = nc.dram_tensor("dbg_fb", [128, 16 * 1024], BF16, kind="ExternalOutput")
        kc.dbg["dbg_xn0"] = nc.dram_tensor("dbg_xn0", [128, D], BF16, kind="ExternalOutput")
        kc.dbg["dbg_st0"] = nc.dram_tensor("dbg_st0", [128, 32], F32, kind="ExternalOutput")
        kc.dbg["dbg_kvg"] = nc.dram_tensor("dbg_kvg", [4 * 1152, 256], BF16, kind="ExternalOutput")
        kc.dbg["dbg_qt"] = nc.dram_tensor("dbg_qt", [128, 8 * 1024], BF16, kind="ExternalOutput")
        kc.dbg["dbg_nmb"] = nc.dram_tensor("dbg_nmb", [128, 16], F32, kind="ExternalOutput")
        kc.dbg["dbg_kt"] = nc.dram_tensor("dbg_kt", [128, 4096], BF16, kind="ExternalOutput")
        kc.dbg["dbg_va"] = nc.dram_tensor("dbg_va", [128, 32 * 136], BF16, kind="ExternalOutput")
        kc.dbg["dbg_z"] = nc.dram_tensor("dbg_z", [128, 8 * 1024], BF16, kind="ExternalOutput")
        kc.dbg["dbg_xq"] = nc.dram_tensor("dbg_xq", [128, D], BF16, kind="ExternalOutput")
        kc.dbg["dbg_sc"] = nc.dram_tensor("dbg_sc", [128, 32], F32, kind="ExternalOutput")
        kc.dbg["dbg_h"] = nc.dram_tensor("dbg_h", [128, F0], BF16, kind="ExternalOutput")
        kc.dbg["dbg_slab"] = nc.dram_tensor("dbg_slab", [128, 256], BF16, kind="ExternalOutput")
        kc.dbg["dbg_hqt"] = nc.dram_tensor("dbg_hqt", [128, 512], BF16, kind="ExternalOutput")
        kc.dbg["dbg_guq"] = nc.dram_tensor("dbg_guq", [D, GU], BF16, kind="ExternalOutput")
        kc.dbg["dbg_wdq"] = nc.dram_tensor("dbg_wdq", [F0, D], BF16, kind="ExternalOutput")
        kc.dbg["dbg_pv"] = nc.dram_tensor("dbg_pv", [128, CT], F32, kind="ExternalOutput")
        kc.dbg["dbg_pv3"] = nc.dram_tensor("dbg_pv3", [128, CT], F32, kind="ExternalOutput")
        kc.dbg["dbg_kmx"] = nc.dram_tensor("dbg_kmx", [16, 256], F32, kind="ExternalOutput")
        kc.dbg["dbg_acc"] = nc.dram_tensor("dbg_acc", [128, 8], F32, kind="ExternalOutput")

    kc.gu_q = nc.dram_tensor("gu_q", [D, GU], BF16)
    kc.wd_q = nc.dram_tensor("wd_q", [F0, D], BF16)
    kc.h_dram = nc.dram_tensor("h_dram", [QTOK, F0], BF16)
    kc.kv0loc = nc.dram_tensor("kv0loc", [1152, 2 * Dh], BF16)
    kc.kv0g = nc.dram_tensor("kv0g", [4 * 1152, 2 * Dh], BF16)
    kc.kv1loc = nc.dram_tensor("kv1loc", [1152, 2 * D], BF16)
    kc.kv1g = nc.dram_tensor("kv1g", [4 * 1152, 2 * D], BF16)
    kc.scr128 = nc.dram_tensor("scr128", [16, 128], F32)
    kc.scrM = nc.dram_tensor("scrM", [1, 16], F32)

    stack = contextlib.ExitStack()
    with stack:
        e = stack.enter_context
        kc.x_res = e(nc.sbuf_tensor("x_res", [128, CT, D], F32))
        kc.arena = [e(nc.sbuf_tensor(f"arena{i}", [128, 11264], BF16)) for i in range(2)]
        kc.xn_blk = e(nc.sbuf_tensor("xn_blk", [128, 2, D], BF16))
        kc.ident = e(nc.sbuf_tensor("identt", [128, 128], BF16))
        kc.ones_bf = e(nc.sbuf_tensor("ones_bf", [128, 1], BF16))
        kc.sqs = e(nc.sbuf_tensor("sqs", [128, 512], BF16))
        kc.stats = e(nc.sbuf_tensor("stats", [128, 32], F32))
        kc.svec = e(nc.sbuf_tensor("svec", [128, 64], F32))
        kc.nmb = e(nc.sbuf_tensor("nmb", [128, 16], F32))
        kc.pvec0 = e(nc.sbuf_tensor("pvec0", [128, CT], F32))
        kc.pvec2 = e(nc.sbuf_tensor("pvec2", [128, CT], F32))
        kc.pvec3 = e(nc.sbuf_tensor("pvec3", [128, CT], F32))
        kc.pvec4 = e(nc.sbuf_tensor("pvec4", [128, CT], F32))
        kc.pvec5 = e(nc.sbuf_tensor("pvec5", [128, CT], F32))
        kc.kmaxb = e(nc.sbuf_tensor("kmaxb", [16, 128], F32))
        kc.cvals = e(nc.sbuf_tensor("cvals", [128, 4], F32))
        kc.jsep = e(nc.sbuf_tensor("jsep", [128, 4], F32))
        kc.ones_row = e(nc.sbuf_tensor("ones_row", [1, 128], BF16))
        kc.cprb = e(nc.sbuf_tensor("cprb", [128, 18], BF16))
        kc.ps = [e(nc.psum_tensor(f"psb{i}", [128, 512], F32)) for i in range(8)]
        kc.stg = e(nc.sbuf_tensor("stg", [128, 2, 512], F32))
        kc.sd_g = Sem(stack, nc, "sd_g")
        kc.sd_bulk = Sem(stack, nc, "sd_bulk")
        kc.sd_x = Sem(stack, nc, "sd_x")
        kc.sd_n = Sem(stack, nc, "sd_n")
        kc.sd_kv = Sem(stack, nc, "sd_kv")
        kc.sd_st = Sem(stack, nc, "sd_st")
        kc.sd_mk = Sem(stack, nc, "sd_mk")
        kc.sd_out = Sem(stack, nc, "sd_out")
        kc.sd_s = Sem(stack, nc, "sd_s")
        kc.sd_y = Sem(stack, nc, "sd_y")
        kc.s_cc = Sem(stack, nc, "s_cc")
        kc.s_t = Sem(stack, nc, "s_t", ordered=True)
        kc.s_v = Sem(stack, nc, "s_v", ordered=True)
        kc.s_s = Sem(stack, nc, "s_s", ordered=True)
        kc._ys = {}

        def ys(name):
            if name not in kc._ys:
                kc._ys[name] = Sem(stack, nc, "ys_" + name)
            return kc._ys[name]
        kc.ys = ys

        def barrier_sems():
            out = [kc.sd_x, kc.sd_n, kc.sd_kv, kc.sd_st, kc.sd_mk, kc.sd_out,
                   kc.sd_s, kc.sd_y]
            out.extend(kc._ys.values())
            return out
        kc.barrier_sems = barrier_sems
        kc.P = Prog(nc)
        kc.stg_dep = [None, None]
        kc.stg_flip = 0

        dma(kc, "gpsimd", kc.sd_x, kc.x_res[:, :, :],
            kc.xin.ap().rearrange("(t p) d -> p t d", p=128))
        dma(kc, "gpsimd", kc.sd_x, kc.ident[:, :], kc.ident_in[:, :])
        gw(kc, "vector", kc.sd_x)
        vop(kc, lambda V: V.memset(kc.ones_bf[:, :], 1.0))
        vop(kc, lambda V: V.memset(kc.ones_row[:, :], 1.0))
        vop(kc, lambda V: V.memset(kc.cvals[:, 0:1], EPS))
        vop(kc, lambda V: V.memset(kc.cvals[:, 1:2], MAGIC))
        vop(kc, lambda V: V.memset(kc.cvals[:, 2:3], -MAGIC))
        gw(kc, "tensor", kc.sd_x)
        gw(kc, "scalar", kc.sd_x)

        emit_attention(kc, 0)
        if debug:
            gw(kc, "scalar", kc.s_v, kc.x_ready_v)
            dma(kc, "scalar", kc.sd_out,
                kc.dbg["dbg_x1"].ap().rearrange("(t p) d -> p t d", p=128),
                kc.x_res[:, :, :])
        emit_bitnet_prep(kc)
        emit_bitnet_ffn(kc)
        if debug:
            gw(kc, "scalar", kc.s_v, kc.x_ready_v)
            dma(kc, "scalar", kc.sd_out,
                kc.dbg["dbg_x2"].ap().rearrange("(t p) d -> p t d", p=128),
                kc.x_res[:, :, :])
        emit_attention(kc, 1)
        if debug:
            gw(kc, "scalar", kc.s_v, kc.x_ready_v)
            dma(kc, "scalar", kc.sd_out,
                kc.dbg["dbg_x3"].ap().rearrange("(t p) d -> p t d", p=128),
                kc.x_res[:, :, :])
        emit_moe(kc)

        gw(kc, "scalar", kc.s_v, kc.x_ready_v)
        dma(kc, "scalar", kc.sd_out,
            kc.xout.ap().rearrange("(t p) d -> p t d", p=128),
            kc.x_res[:, :, :])
        gw(kc, "scalar", kc.sd_out)
        kc.P.flush()
    return nc


def _mask_table(gr):
    m = np.zeros((128, 16, 512), np.float32)
    for i in range(16):
        for s in range(4):
            gq_rel = gr + 4 * s
            blk = m[:, i, s * 128:(s + 1) * 128]
            if i < gq_rel:
                pass
            elif i > gq_rel:
                blk[:, :] = NEG
            else:
                p = np.arange(128)[:, None]
                qf = np.arange(128)[None, :]
                blk[:, :] = np.where(p <= qf, 0.0, NEG)
    return m.reshape(128, 16 * 512).astype(ml_dtypes.bfloat16)


_NC_CACHE = {}


def kernel(**inputs):
    key = "main"
    if key not in _NC_CACHE:
        _NC_CACHE[key] = build(debug=DEBUG)
    nc = _NC_CACHE[key]

    x = np.ascontiguousarray(inputs["L0_attn_x"])
    ident = np.eye(128, dtype=ml_dtypes.bfloat16)
    wts = prep_weights(inputs)
    in_maps = []
    for c in range(8):
        b, gr = c // 4, c % 4
        tiles = [gr + 4 * j for j in range(CT)]
        xs = np.ascontiguousarray(
            np.concatenate([x[b, 128 * g:128 * (g + 1), :] for g in tiles], axis=0))
        m = {"xin": xs, "ident_in": ident, "masktbl": _mask_table(gr)}
        m.update(wts)
        in_maps.append(m)

    res = run_bass_kernel_spmd(nc, in_maps, core_ids=list(range(8)))
    out = np.empty((B, S, D), np.float32)
    for c in range(8):
        b, gr = c // 4, c % 4
        xo = res.results[c]["xout"]
        for j in range(CT):
            g = gr + 4 * j
            out[b, 128 * g:128 * (g + 1), :] = xo[128 * j:128 * (j + 1), :]
    return out


if __name__ == "__main__":
    nc = build(debug=False)
    print("instructions:", nc.n_instructions())


# version 10
# speedup vs baseline: 1.2270x; 1.2270x over previous
import contextlib
import numpy as np
import ml_dtypes

import concourse.bass as bass
import concourse.mybir as mybir
from concourse.bass_utils import run_bass_kernel_spmd

F32 = mybir.dt.float32
BF16 = mybir.dt.bfloat16
AF = mybir.ActivationFunctionType
ALU = mybir.AluOpType
AX = mybir.AxisListType

B, S, D = 2, 4096, 2048
H, Dh = 16, 128
CT = 8
QTOK = CT * 128
F0 = 5632
GU = 2 * F0
NEXP, FM, FS = 8, 1408, 5632
SCALE = 1.0 / float(np.sqrt(Dh))
SCALE2 = SCALE * SCALE
MAGIC = float(1.5 * 2 ** 23)
NEG = -float(2.0 ** 30)
EPS = 1e-5
DEBUG = False
DEBUG2 = False

WSHAPES = {
    "L0_attn_q_w": (D, D), "L0_attn_k_w": (D, Dh), "L0_attn_v_w": (D, Dh),
    "L0_attn_o_w": (D, D),
    "L0_ffn_w_gate_up": (D, GU), "L0_ffn_w_down": (F0, D),
    "L0_ffn_norm_weight": (D,),
    "L1_attn_W_K": (D, H * Dh), "L1_attn_W_Q": (D, H * Dh),
    "L1_attn_W_V": (D, H * Dh), "L1_attn_W_O": (H * Dh, D),
    "L1_ffn_gate_w": (NEXP, D),
    "L1_ffn_expert_fc1": (D, NEXP * FM), "L1_ffn_expert_fc2": (D, NEXP * FM),
    "L1_ffn_expert_proj": (FM, NEXP * D),
    "L1_ffn_shared_fc1": (D, FS), "L1_ffn_shared_fc2": (D, FS),
    "L1_ffn_shared_proj": (FS, D), "L1_ffn_shared_gate_w": (1, D),
}
NO_T = ("L0_ffn_norm_weight", "L1_ffn_gate_w", "L1_ffn_shared_gate_w")


def prep_weights(inputs):
    out = {}
    for k in WSHAPES:
        a = np.asarray(inputs[k], np.float32)
        if k in NO_T:
            out[k] = np.ascontiguousarray(a.reshape(WSHAPES[k]))
        else:
            out[k] = np.ascontiguousarray(a.reshape(-1, a.shape[-1]).T)
        assert out[k].shape == tuple(WSHAPES[k]), (k, out[k].shape)
    return out


def kg_of(kt):
    return (kt % 8) * 4 + kt // 8


KTS_LOW = [kt for kt in range(32) if kg_of(kt) < 16]
KTS_HIGH = [kt for kt in range(32) if kg_of(kt) >= 16]


class Sem:
    def __init__(self, stack, nc, name, ordered=False):
        self.h = stack.enter_context(nc.semaphore(name))
        self.v = 0
        self.ordered = ordered

    def inc(self, n=1):
        self.v += n
        return self.v


class Prog:
    ENGS = ("gpsimd", "sync", "scalar", "vector", "tensor")

    def __init__(self, nc):
        self.nc = nc
        self.q = {k: [] for k in self.ENGS}

    def emit(self, eng, fn):
        self.q[eng].append(fn)

    def flush(self):
        if not any(self.q.values()):
            return
        with self.nc.Block() as block:
            for name in self.ENGS:
                fns = self.q[name]
                if fns:
                    def runner(e, fns=fns):
                        for f in fns:
                            f(e)
                    getattr(block, name)(runner)
        self.q = {k: [] for k in self.ENGS}


class KC:
    pass


def gw(kc, eng, sem, tgt=None):
    if not sem.ordered:
        tgt = sem.v
    if tgt is None or tgt <= 0:
        return
    assert tgt <= sem.v, f"wait target {tgt} > queued {sem.v}"
    kc.P.emit(eng, lambda E, s=sem, t=tgt: E.wait_ge(s.h, t))


def dma(kc, eng, sem, out, in_, **kw):
    sem.inc(16)
    kc.P.emit(eng, lambda E, o=out, i=in_, s=sem, kw=kw:
              E.dma_start(out=o, in_=i, **kw).then_inc(s.h, 16))
    return sem


def tdma(kc, out, in_, sem=None):
    sem = sem if sem is not None else kc.sd_y
    sem.inc(16)
    kc.P.emit("sync", lambda E, o=out, i=in_, s=sem: E.dma_start(out=o, in_=i, transpose=True)
              .then_inc(s.h, 16))
    return sem


def vop(kc, fn, sep=False):
    if sep:
        kc.P.emit("vector", lambda V, t=kc.s_v.v: V.wait_ge(kc.s_v.h, t))
    tgt = kc.s_v.inc(1)
    kc.P.emit("vector", lambda V, f=fn: f(V).then_inc(kc.s_v.h, 1))
    return tgt


def sop(kc, fn, sep=False):
    if sep:
        kc.P.emit("scalar", lambda Sc, t=kc.s_s.v: Sc.wait_ge(kc.s_s.h, t))
    tgt = kc.s_s.inc(1)
    kc.P.emit("scalar", lambda Sc, f=fn: f(Sc).then_inc(kc.s_s.h, 1))
    return tgt


def top(kc, fn, bump=True):
    if bump:
        tgt = kc.s_t.inc(1)
        kc.P.emit("tensor", lambda T, f=fn: f(T).then_inc(kc.s_t.h, 1))
        return tgt
    kc.P.emit("tensor", lambda T, f=fn: f(T))
    return None


def bank_wait(kc, b):
    dep = kc.bankdep.get(b)
    if dep is not None:
        gw(kc, "tensor", *dep)
        kc.bankdep[b] = None


def bank_done(kc, b, sem, tgt):
    kc.bankdep[b] = (sem, tgt)


def buf_wait(kc, eng, name):
    dep = kc.bufdep.get(name)
    if dep is not None:
        gw(kc, eng, *dep)
        kc.bufdep[name] = None


def buf_done(kc, name, sem, tgt=None):
    kc.bufdep[name] = (sem, tgt)


def phase_barrier(kc):
    tv, tt_, ts_ = kc.s_v.v, kc.s_t.v, kc.s_s.v
    for eng in ("scalar", "vector", "tensor", "sync"):
        gw(kc, eng, kc.s_v, tv)
        gw(kc, eng, kc.s_t, tt_)
        gw(kc, eng, kc.s_s, ts_)
        for sm in kc.barrier_sems():
            gw(kc, eng, sm)


def wapf(kc, k):
    return kc.W[k].ap()


def emit_ln(kc, tt, out_ap, rms_col=None):
    st = kc.stats
    gw(kc, "scalar", kc.s_v, kc.x_ready_v)
    jk = out_ap
    sop(kc, lambda Sc, tt=tt, jk=jk: Sc.activation(jk, kc.x_res[:, tt, :],
                                                   AF.Identity, accum_out=st[:, 0:1]))
    t_s = sop(kc, lambda Sc, tt=tt, jk=jk: Sc.activation(jk, kc.x_res[:, tt, :],
                                                         AF.Square, accum_out=st[:, 1:2]))
    gw(kc, "vector", kc.s_s, t_s)
    vop(kc, lambda V: V.tensor_scalar(st[:, 2:3], st[:, 0:1], 1.0 / D, None, ALU.mult))
    vop(kc, lambda V: V.tensor_scalar(st[:, 3:4], st[:, 1:2], 1.0 / D, None, ALU.mult))
    vop(kc, lambda V: V.tensor_mul(st[:, 4:5], st[:, 2:3], st[:, 2:3]))
    tv = vop(kc, lambda V: V.tensor_sub(st[:, 5:6], st[:, 3:4], st[:, 4:5]), sep=True)
    gw(kc, "scalar", kc.s_v, tv)
    tsq = sop(kc, lambda Sc: Sc.activation(st[:, 6:7], st[:, 5:6], AF.Sqrt, bias=kc.cvals[:, 0:1]))
    gw(kc, "vector", kc.s_s, tsq)
    vop(kc, lambda V: V.reciprocal(st[:, 7:8], st[:, 6:7]))
    vop(kc, lambda V: V.tensor_mul(st[:, 8:9], st[:, 2:3], st[:, 7:8]), sep=True)
    vop(kc, lambda V: V.tensor_scalar(st[:, 8:9], st[:, 8:9], -1.0, None, ALU.mult), sep=True)
    tgt = vop(kc, lambda V, tt=tt, o=out_ap: V.tensor_scalar(
        o, kc.x_res[:, tt, :], st[:, 7:8], st[:, 8:9], ALU.mult, ALU.add), sep=True)
    if rms_col is not None:
        vop(kc, lambda V: V.tensor_mul(st[:, 9:10], st[:, 5:6], st[:, 7:8]))
        tv2 = vop(kc, lambda V: V.tensor_mul(st[:, 9:10], st[:, 9:10], st[:, 7:8]), sep=True)
        gw(kc, "scalar", kc.s_v, tv2)
        ts2 = sop(kc, lambda Sc: Sc.activation(st[:, 9:10], st[:, 9:10], AF.Sqrt, bias=kc.cvals[:, 0:1]))
        gw(kc, "vector", kc.s_s, ts2)
        vop(kc, lambda V, c=rms_col: V.reciprocal(kc.pvec0[:, c:c + 1], st[:, 9:10]))
    return tgt


def emit_t2f(kc, src_fn, dst_col0, kcn, featbuf, src_ready=None, colw=1024):
    for g in range((kcn + 3) // 4):
        nj = min(4, kcn - g * 4)
        b = 6 + (kc.tp_flip % 2)
        kc.tp_flip += 1
        bank_wait(kc, b)
        if src_ready is not None:
            gw(kc, "tensor", *src_ready)
            src_ready = None
        for j in range(nj):
            kcc = g * 4 + j
            top(kc, lambda T, b=b, j=j, kcc=kcc: T.matmul(
                kc.ps[b][:, j * 128:(j + 1) * 128], src_fn(kcc), kc.ident[:, :],
                start=True, stop=True), bump=(j == nj - 1))
        gw(kc, "vector", kc.s_t, kc.s_t.v)
        tv = vop(kc, lambda V, b=b, g=g, c0=dst_col0, nj=nj, fb=featbuf, colw=colw:
                 V.tensor_copy(
                     bass.AP(fb, g * 4 * colw + c0, [[16 * colw, 128], [colw, nj], [1, 128]]),
                     bass.AP(kc.ps[b], 0, [[512, 128], [128, nj], [1, 128]])))
        bank_done(kc, b, kc.s_v, tv)


def slab_load(kc, dst_buf, base, width, wT_ap, o0, ncols, kcn, sem=None, r0=0,
              f32=True):
    if not f32:
        t = None
        for c in range(kcn):
            t = dma(kc, "sync", sem,
                    dst_buf[:, base + c * width: base + c * width + ncols],
                    wT_ap[r0 + c * 128: r0 + (c + 1) * 128, o0:o0 + ncols])
        return (t, None)
    ts_ = None
    stgT, stg_n = kc.stgT, kc.stg_n
    for c in range(kcn):
        sl = kc.stg_flip % stg_n
        kc.stg_flip += 1
        dep = kc.stg_dep[sl]
        if dep is not None:
            gw(kc, "sync", kc.s_s, dep)
        dma(kc, "sync", sem, stgT[:, sl, 0:ncols],
            wT_ap[r0 + c * 128: r0 + (c + 1) * 128, o0:o0 + ncols])
        gw(kc, "scalar", sem)
        ts_ = sop(kc, lambda Sc, c=c, sl=sl, ncols=ncols, base=base, width=width,
                  dst_buf=dst_buf, stgT=stgT: Sc.activation(
                      dst_buf[:, base + c * width: base + c * width + ncols],
                      stgT[:, sl, 0:ncols], AF.Identity))
        kc.stg_dep[sl] = ts_
    return (kc.s_s, ts_)


def cross_part_max(kc, src_ap, nh):
    tv = vop(kc, lambda V: V.tensor_copy(kc.cprb[:, 0:nh], src_ap), sep=True)
    b = 6 + (kc.tp_flip % 2)
    kc.tp_flip += 1
    bank_wait(kc, b)
    gw(kc, "tensor", kc.s_v, tv)
    t_t = top(kc, lambda T, b=b, nh=nh: T.matmul(
        kc.ps[b][0:nh, 0:128], kc.cprb[:, 0:nh], kc.ident[:, :],
        start=True, stop=True))
    gw(kc, "vector", kc.s_t, t_t)
    tv = vop(kc, lambda V, b=b, nh=nh: V.tensor_reduce(
        kc.kmaxb[0:nh, 0:1], kc.ps[b][0:nh, 0:128], AX.X, ALU.max))
    bank_done(kc, b, kc.s_v, tv)
    return tv


def cross_part_sum(kc, src_col):
    vop(kc, lambda V: V.tensor_copy(kc.cprb[:, 16:17], src_col), sep=True)
    vop(kc, lambda V: V.tensor_copy(kc.stats[:, 30:31], kc.cprb[:, 16:17]), sep=True)
    vop(kc, lambda V: V.tensor_sub(kc.stats[:, 31:32], src_col, kc.stats[:, 30:31]), sep=True)
    tv = vop(kc, lambda V: V.tensor_copy(kc.cprb[:, 17:18], kc.stats[:, 31:32]), sep=True)
    b = 6 + (kc.tp_flip % 2)
    kc.tp_flip += 1
    bank_wait(kc, b)
    gw(kc, "tensor", kc.s_v, tv)
    t_t = top(kc, lambda T, b=b: T.matmul(
        kc.ps[b][0:1, 0:2], kc.ones_bf[:, :], kc.cprb[:, 16:18],
        start=True, stop=True))
    gw(kc, "vector", kc.s_t, t_t)
    tv = vop(kc, lambda V, b=b: V.tensor_reduce(
        kc.kmaxb[0:1, 0:1], kc.ps[b][0:1, 0:2], AX.X, ALU.add))
    bank_done(kc, b, kc.s_v, tv)
    return tv


def emit_bcast16(kc, src_col_ap, nh, dst_ap):
    tv = vop(kc, lambda V, nh=nh: V.tensor_copy(kc.cprb[0:nh, 16:17], src_col_ap), sep=True)
    b = 6 + (kc.tp_flip % 2)
    kc.tp_flip += 1
    bank_wait(kc, b)
    gw(kc, "tensor", kc.s_v, tv)
    top(kc, lambda T, b=b, nh=nh: T.matmul(
        kc.ps[b][0:1, 0:nh], kc.cprb[0:nh, 16:17], kc.ident[0:nh, 0:nh],
        start=True, stop=True))
    gw(kc, "vector", kc.s_t, kc.s_t.v)
    tv = vop(kc, lambda V, b=b, nh=nh: V.tensor_copy(kc.cprb[0:1, 0:nh], kc.ps[b][0:1, 0:nh]))
    bank_done(kc, b, kc.s_v, tv)
    b2 = 6 + (kc.tp_flip % 2)
    kc.tp_flip += 1
    bank_wait(kc, b2)
    gw(kc, "tensor", kc.s_v, tv)
    top(kc, lambda T, b2=b2, nh=nh: T.matmul(
        kc.ps[b2][:, 0:nh], kc.ones_row[0:1, :], kc.cprb[0:1, 0:nh],
        start=True, stop=True))
    gw(kc, "vector", kc.s_t, kc.s_t.v)
    tv = vop(kc, lambda V, b2=b2, nh=nh, d=dst_ap: V.tensor_copy(d, kc.ps[b2][:, 0:nh]))
    bank_done(kc, b2, kc.s_v, tv)
    return tv


def emit_bcast_f32(kc, src_11, dst_col):
    vop(kc, lambda V: V.tensor_copy(kc.cprb[0:1, 0:1], src_11), sep=True)
    vop(kc, lambda V: V.tensor_copy(kc.stats[0:1, 30:31], kc.cprb[0:1, 0:1]), sep=True)
    vop(kc, lambda V: V.tensor_sub(kc.stats[0:1, 31:32], src_11, kc.stats[0:1, 30:31]), sep=True)
    tv = vop(kc, lambda V: V.tensor_copy(kc.cprb[0:1, 1:2], kc.stats[0:1, 31:32]), sep=True)
    b = 6 + (kc.tp_flip % 2)
    kc.tp_flip += 1
    bank_wait(kc, b)
    gw(kc, "tensor", kc.s_v, tv)
    top(kc, lambda T, b=b: T.matmul(
        kc.ps[b][:, 0:2], kc.ones_row[0:1, :], kc.cprb[0:1, 0:2],
        start=True, stop=True))
    gw(kc, "vector", kc.s_t, kc.s_t.v)
    vop(kc, lambda V, b=b: V.tensor_copy(kc.stats[:, 30:31], kc.ps[b][:, 0:1]))
    tv2 = vop(kc, lambda V, b=b, d=dst_col: V.tensor_add(
        d, kc.ps[b][:, 1:2], kc.stats[:, 30:31]), sep=True)
    bank_done(kc, b, kc.s_v, tv2)
    return tv2


def emit_attention(kc, layer):
    nc = kc.nc
    l1 = (layer == 1)
    kvloc, kvg = (kc.kv1loc, kc.kv1g) if l1 else (kc.kv0loc, kc.kv0g)
    KV = 2 * (D if l1 else Dh)
    half = KV // 2
    nh = half // 128
    wq, wk, wv, wo = (("L1_attn_W_Q", "L1_attn_W_K", "L1_attn_W_V", "L1_attn_W_O")
                      if l1 else
                      ("L0_attn_q_w", "L0_attn_k_w", "L0_attn_v_w", "L0_attn_o_w"))

    with contextlib.ExitStack() as actx:
        ee = actx.enter_context
        featbuf = ee(nc.sbuf_tensor("featbufA%d" % layer, [128, 16, 1024], BF16))
        QT = ee(nc.sbuf_tensor("QT%d" % layer, [128, 8, 1024], BF16))
        KTs = [ee(nc.sbuf_tensor(f"KTs{layer}{i}", [128, 4096], BF16)) for i in range(2)]
        Vaug = [ee(nc.sbuf_tensor(f"Vaug{layer}{i}", [128, 32, 136], BF16)) for i in range(2)]
        Eb = [ee(nc.sbuf_tensor(f"Eb{layer}{i}", [128, 512], BF16)) for i in range(2)]
        zblk = ee(nc.sbuf_tensor("zblk%d" % layer, [128, 128], BF16))
        phase_barrier(kc)

        for tt in range(CT):
            buf_wait(kc, "vector", f"xn{tt % 2}")
            xv = emit_ln(kc, tt, kc.xn_blk[:, tt % 2, :])
            if getattr(kc, "debug2", False) and not l1 and tt == 0:
                gw(kc, "scalar", kc.s_v, xv)
                dma(kc, "scalar", kc.sd_out, kc.dbg["dbg_xn0"][:, :], kc.xn_blk[:, 0, :])
                dma(kc, "scalar", kc.sd_out, kc.dbg["dbg_st0"][:, :], kc.stats[:, :])
            emit_t2f(kc, (lambda kcc, tt=tt: kc.xn_blk[:, tt % 2, kcc * 128:(kcc + 1) * 128]),
                     tt * 128, 16, featbuf, src_ready=(kc.s_v, xv))
            buf_done(kc, f"xn{tt % 2}", kc.s_t, kc.s_t.v)

        if getattr(kc, "debug2", False) and not l1:
            gw(kc, "scalar", kc.s_v, kc.s_v.v)
            gw(kc, "scalar", kc.s_t, kc.s_t.v)
            dma(kc, "scalar", kc.sd_out, kc.dbg["dbg_fb"][:, :],
                bass.AP(featbuf, 0, [[16 * 1024, 128], [1, 16 * 1024]]))
        featbuf_ready = kc.s_v.v
        gw(kc, "tensor", kc.s_v, featbuf_ready)
        vop(kc, lambda V: V.memset(kc.svec[:, :], 0.0))
        ocw = min(512, KV)
        n_oc = KV // ocw
        kv_store_t = None
        prev_knorm_v = None

        def load_kv_slab(oc):
            ab = kc.arena[oc % 2]
            abn = f"a{oc % 2}"
            buf_wait(kc, "scalar", abn)
            ysem = kc.ys(abn)
            if not l1:
                slab_load(kc, ab, 0, 256, wapf(kc, wk), 0, 128, 16, sem=ysem)
                t_sl = slab_load(kc, ab, 128, 256, wapf(kc, wv), 0, 128, 16, sem=ysem)
            else:
                key = wk if oc < 4 else wv
                r0 = (oc % 4) * 512
                t_sl = slab_load(kc, ab, 0, 512, wapf(kc, key), r0, 512, 16, sem=ysem)
            return ab, abn, t_sl
        pend_kv = load_kv_slab(0)
        for oc in range(n_oc):
            nxt_kv = load_kv_slab(oc + 1) if oc + 1 < n_oc else None
            ab, abn, t_sl = pend_kv
            pend_kv = nxt_kv
            gw(kc, "tensor", *t_sl)
            t_last_mm = None
            for tt in range(CT):
                b = tt % 2
                bank_wait(kc, b)
                for i in range(16):
                    top(kc, lambda T, b=b, i=i, tt=tt, ab=ab, ocw=ocw: T.matmul(
                        kc.ps[b][:, 0:ocw], featbuf[:, i, tt * 128:(tt + 1) * 128],
                        ab[:, i * ocw:(i + 1) * ocw],
                        start=(i == 0), stop=(i == 15)), bump=(i == 15))
                t_mm = kc.s_t.v
                t_last_mm = t_mm
                gw(kc, "vector", kc.s_t, t_mm)
                buf_wait(kc, "vector", f"xn{tt % 2}")
                tv = vop(kc, lambda V, b=b, tt=tt, ocw=ocw: V.tensor_copy(
                    kc.xn_blk[:, tt % 2, 0:ocw], kc.ps[b][:, 0:ocw]))
                bank_done(kc, b, kc.s_v, tv)
                if (not l1) or oc < 4:
                    kcols = 128 if not l1 else 512
                    gw(kc, "scalar", kc.s_v, tv)
                    if prev_knorm_v is not None:
                        gw(kc, "scalar", kc.s_v, prev_knorm_v)
                    t_sq = sop(kc, lambda Sc, tt=tt, kcols=kcols: Sc.activation(
                        kc.sqs[:, 0:kcols], kc.xn_blk[:, tt % 2, 0:kcols], AF.Square))
                    gw(kc, "vector", kc.s_s, t_sq)
                    nseg = kcols // 128
                    prev_knorm_v = vop(kc, lambda V, nseg=nseg: V.tensor_reduce(
                        kc.stats[:, 16:16 + nseg],
                        bass.AP(kc.sqs, 0, [[512, 128], [128, nseg], [1, 128]]),
                        AX.X, ALU.add))
                    c0 = 0 if not l1 else oc * 4
                    vop(kc, lambda V, c0=c0, nseg=nseg: V.tensor_max(
                        kc.svec[:, c0:c0 + nseg], kc.svec[:, c0:c0 + nseg],
                        kc.stats[:, 16:16 + nseg]), sep=True)
                gw(kc, "scalar", kc.s_v, kc.s_v.v)
                kv_store_t = dma(kc, "scalar", kc.sd_kv,
                                 kvloc[tt * 128:(tt + 1) * 128, oc * ocw:(oc + 1) * ocw],
                                 kc.xn_blk[:, tt % 2, 0:ocw])
                buf_done(kc, f"xn{tt % 2}", kc.sd_kv, None)
            buf_done(kc, abn, kc.s_t, t_last_mm)
        gw(kc, "gpsimd", kc.s_v, kc.s_v.v)
        n_t = dma(kc, "gpsimd", kc.sd_n, kvloc[1024:1152, 0:nh], kc.svec[:, 0:nh],
                  allow_slow_non_contiguous=True)
        gw(kc, "gpsimd", kc.sd_kv)
        gw(kc, "gpsimd", kc.sd_n)
        cc_t = kc.s_cc.inc(1)
        kc.P.emit("gpsimd", lambda G, i=kvloc, o=kvg: G.collective_compute(
            "AllGather", ALU.bypass, replica_groups=[[0, 1, 2, 3], [4, 5, 6, 7]],
            ins=[i.ap()], outs=[o.ap()]).then_inc(kc.s_cc.h, 1))
        if getattr(kc, "debug2", False) and not l1:
            gw(kc, "gpsimd", kc.s_cc, cc_t)
            dma(kc, "gpsimd", kc.sd_out, kc.dbg["dbg_kvg"][:, :], kvg[:, :])
        def norms_block():
            gw(kc, "gpsimd", kc.s_cc, cc_t)
            dma(kc, "gpsimd", kc.sd_n, kc.svec[:, 0:4 * nh],
                bass.AP(kvg, 1024 * KV, [[KV, 128], [1152 * KV, 4], [1, nh]]),
                allow_slow_non_contiguous=True)
            gw(kc, "vector", kc.sd_n)
            vop(kc, lambda V, nh=nh: V.tensor_reduce(
                kc.stats[:, 16:16 + nh],
                bass.AP(kc.svec, 0, [[64, 128], [1, nh], [nh, 4]]), AX.X, ALU.max))
            cross_part_max(kc, kc.stats[:, 16:16 + nh], nh)

        def load_kt(buf, h):
            buf_wait(kc, "sync", f"kt{buf}")
            sm = kc.ys(f"kt{buf}")
            for r in range(4):
                tdma(kc, KTs[buf][:, r * 1024:(r + 1) * 1024],
                     kvg[r * 1152:r * 1152 + 1024, h * 128:(h + 1) * 128], sem=sm)
            return sm

        def load_vaug(buf, h):
            buf_wait(kc, "scalar", f"va{buf}")
            sm = kc.ys(f"va{buf}")
            for r in range(4):
                dma(kc, "scalar", sm, Vaug[buf][:, r * 8:(r + 1) * 8, 0:128],
                    kvg.ap()[r * 1152:r * 1152 + 1024,
                             half + h * 128:half + (h + 1) * 128]
                    .rearrange("(t p) d -> p t d", p=128))
            return sm

        norms_block()
        gw(kc, "sync", kc.s_cc, cc_t)
        gw(kc, "scalar", kc.s_cc, cc_t)
        for bb in range(2):
            vop(kc, lambda V, bb=bb: V.memset(
                bass.AP(Vaug[bb], 128, [[32 * 136, 128], [136, 32], [1, 1]]), 1.0))
        kt_t = {0: load_kt(0, 0)}
        va_t = {0: load_vaug(0, 0)}

        for grp in range(2):
            gw(kc, "tensor", kc.s_v, featbuf_ready)
            vop(kc, lambda V: V.memset(kc.stats[0:1, 12:13], 0.0))
            prev_on = None
            for hs in range(2):
                buf_wait(kc, "scalar", "a1")
                t_q = slab_load(kc, kc.arena[1], 0, 512, wapf(kc, wq),
                                grp * 1024 + hs * 512, 512, 16, sem=kc.ys("a1"))
                gw(kc, "tensor", *t_q)
                t_pj = None
                for sl in range(4):
                    s = hs * 4 + sl
                    for qc in range(2):
                        b = qc
                        bank_wait(kc, b)
                        for i in range(16):
                            top(kc, lambda T, b=b, i=i, s=s, sl=sl, qc=qc: T.matmul(
                                kc.ps[b][:, 0:512],
                                kc.arena[1][:, i * 512 + sl * 128:i * 512 + (sl + 1) * 128],
                                featbuf[:, i, qc * 512:(qc + 1) * 512],
                                start=(i == 0), stop=(i == 15)), bump=(i == 15))
                        t_pj = kc.s_t.v
                        gw(kc, "vector", kc.s_t, t_pj)
                        tv = vop(kc, lambda V, b=b, s=s, qc=qc: V.tensor_copy(
                            QT[:, s, qc * 512:(qc + 1) * 512], kc.ps[b][:, 0:512]))
                        bank_done(kc, b, kc.s_v, tv)
                        gw(kc, "scalar", kc.s_v, tv)
                        if prev_on is not None:
                            gw(kc, "scalar", kc.s_t, prev_on)
                        t_sq = sop(kc, lambda Sc, s=s, qc=qc: Sc.activation(
                            kc.sqs[:, 0:512], QT[:, s, qc * 512:(qc + 1) * 512], AF.Square))
                        bank_wait(kc, 5)
                        gw(kc, "tensor", kc.s_s, t_sq)
                        t_on = top(kc, lambda T: T.matmul(
                            kc.ps[5][0:1, 0:512], kc.ones_bf[:, :], kc.sqs[:, 0:512],
                            start=True, stop=True))
                        prev_on = t_on
                        gw(kc, "vector", kc.s_t, t_on)
                        vop(kc, lambda V: V.tensor_reduce(
                            kc.stats[0:1, 13:14], kc.ps[5][0:1, 0:512], AX.X, ALU.max))
                        tv2 = vop(kc, lambda V: V.tensor_max(
                            kc.stats[0:1, 12:13], kc.stats[0:1, 12:13], kc.stats[0:1, 13:14]),
                            sep=True)
                        bank_done(kc, 5, kc.s_v, tv2)
                buf_done(kc, "a1", kc.s_t, t_pj)
            qt_ready = kc.s_v.v
            emit_bcast16(kc, kc.stats[0:1, 12:13], 1, kc.stats[:, 13:14])
            tv = vop(kc, lambda V, nh=nh: V.tensor_mul(
                kc.kmaxb[0:nh, 2:3], kc.kmaxb[0:nh, 0:1], kc.stats[0:nh, 13:14]), sep=True)
            gw(kc, "scalar", kc.s_v, tv)
            ts_ = sop(kc, lambda Sc, nh=nh: Sc.activation(
                kc.kmaxb[0:nh, 3:4], kc.kmaxb[0:nh, 2:3], AF.Sqrt, scale=SCALE2))
            gw(kc, "vector", kc.s_s, ts_)
            tv = vop(kc, lambda V, nh=nh: V.tensor_scalar(
                kc.kmaxb[0:nh, 3:4], kc.kmaxb[0:nh, 3:4], -1.0, -1.0, ALU.mult, ALU.add))
            emit_bcast16(kc, kc.kmaxb[0:nh, 3:4], nh, kc.nmb[:, 0:nh])
            nmb_v = kc.s_v.v
            gw(kc, "scalar", kc.s_v, nmb_v)

            if getattr(kc, "debug2", False) and not l1 and grp == 0:
                gw(kc, "scalar", kc.s_v, kc.s_v.v)
                dma(kc, "scalar", kc.sd_out, kc.dbg["dbg_qt"][:, :],
                    bass.AP(QT, 0, [[8 * 1024, 128], [1, 8 * 1024]]))
                dma(kc, "scalar", kc.sd_out, kc.dbg["dbg_nmb"][:, :], kc.nmb[:, :])
                gw(kc, "scalar", kc.ys("kt0"))
                gw(kc, "scalar", kc.ys("va0"))
                dma(kc, "scalar", kc.sd_out, kc.dbg["dbg_kt"][:, :], KTs[0][:, :])
                dma(kc, "scalar", kc.sd_out, kc.dbg["dbg_va"][:, :],
                    bass.AP(Vaug[0], 0, [[32 * 136, 128], [1, 32 * 136]]))
            if l1:
                buf_wait(kc, "scalar", "a1")
                dma(kc, "scalar", kc.sd_mk, kc.arena[1][:, 0:8192],
                    kc.mask_in[:, 0:8192])
                gw(kc, "tensor", kc.sd_mk)

            gw(kc, "tensor", kc.s_v, qt_ready)
            for s in range(8):
                h = grp * 8 + s
                if l1:
                    if h + 1 < 16:
                        kt_t[(h + 1) % 2] = load_kt((h + 1) % 2, h + 1)
                        va_t[(h + 1) % 2] = load_vaug((h + 1) % 2, h + 1)
                    kbuf = h % 2
                else:
                    kbuf = 0
                gw(kc, "tensor", kt_t[kbuf])
                gw(kc, "tensor", va_t[kbuf])
                gw(kc, "scalar", kc.s_v, nmb_v)
                mcol = h if l1 else 0
                for qc in range(2):
                    if l1:
                        kts = ([(kt, True) for kt in KTS_LOW] if qc == 0 else
                               [(kt, False) for kt in KTS_LOW] +
                               [(kt, True) for kt in KTS_HIGH])
                    else:
                        kts = [(kt, False) for kt in range(32)]
                    nkt = len(kts)
                    for j in range(4):
                        bank_wait(kc, j)
                    e_free = [kc.s_t.v, kc.s_t.v]
                    for idx, (kt, masked) in enumerate(kts):
                        sb = 4 + (idx % 2)
                        bank_wait(kc, sb)
                        top(kc, lambda T, sb=sb, kt=kt, s=s, qc=qc, kbuf=kbuf, m=masked:
                            T.matmul(kc.ps[sb][:, 0:512],
                                     KTs[kbuf][:, kt * 128:(kt + 1) * 128],
                                     QT[:, s, qc * 512:(qc + 1) * 512],
                                     start=True, stop=not m), bump=not masked)
                        if masked:
                            bi = kg_of(kt) % 16
                            top(kc, lambda T, sb=sb, bi=bi: T.matmul(
                                kc.ps[sb][:, 0:512], kc.ident[:, :],
                                kc.arena[1][:, bi * 512:(bi + 1) * 512],
                                start=False, stop=True))
                        t_qk = kc.s_t.v
                        gw(kc, "scalar", kc.s_t, t_qk)
                        if e_free[idx % 2] is not None:
                            gw(kc, "scalar", kc.s_t, e_free[idx % 2])
                        t_exp = sop(kc, lambda Sc, sb=sb, ix=idx % 2, mc=mcol: Sc.activation(
                            Eb[ix][:, :], kc.ps[sb][:, 0:512], AF.Exp,
                            bias=kc.nmb[:, mc:mc + 1], scale=SCALE))
                        bank_done(kc, sb, kc.s_s, t_exp)
                        gw(kc, "tensor", kc.s_s, t_exp)
                        for j in range(4):
                            top(kc, lambda T, j=j, ix=idx % 2, kt=kt, kbuf=kbuf,
                                st=(idx == 0), sp=(idx == nkt - 1): T.matmul(
                                kc.ps[j][:, 0:129], Eb[ix][:, j * 128:(j + 1) * 128],
                                Vaug[kbuf][:, kt, 0:129],
                                start=st, stop=sp), bump=(j == 3))
                        e_free[idx % 2] = kc.s_t.v
                    t_av = kc.s_t.v
                    gw(kc, "vector", kc.s_t, t_av)
                    for j in range(4):
                        vop(kc, lambda V, j=j: V.reciprocal(
                            kc.stats[:, 14 + j:15 + j], kc.ps[j][:, 128:129]))
                    for j in range(4):
                        tv = vop(kc, lambda V, j=j: V.tensor_scalar(
                            zblk[:, :], kc.ps[j][:, 0:128], kc.stats[:, 14 + j:15 + j], None,
                            ALU.mult))
                        bank_done(kc, j, kc.s_v, tv)
                        b = 6 + (kc.tp_flip % 2)
                        kc.tp_flip += 1
                        bank_wait(kc, b)
                        gw(kc, "tensor", kc.s_v, tv)
                        t_tp = top(kc, lambda T, b=b: T.matmul(
                            kc.ps[b][:, 0:128], zblk[:, :], kc.ident[:, :],
                            start=True, stop=True))
                        gw(kc, "vector", kc.s_t, t_tp)
                        tv2 = vop(kc, lambda V, b=b, s=s, qc=qc, j=j: V.tensor_copy(
                            QT[:, s, qc * 512 + j * 128:qc * 512 + (j + 1) * 128],
                            kc.ps[b][:, 0:128]))
                        bank_done(kc, b, kc.s_v, tv2)
                if l1:
                    buf_done(kc, f"kt{kbuf}", kc.s_t, kc.s_t.v)
                    buf_done(kc, f"va{kbuf}", kc.s_t, kc.s_t.v)
            if l1:
                buf_done(kc, "a1", kc.s_t, kc.s_t.v)
            if getattr(kc, "debug2", False) and not l1 and grp == 0:
                gw(kc, "scalar", kc.s_v, kc.s_v.v)
                dma(kc, "scalar", kc.sd_out, kc.dbg["dbg_z"][:, :],
                    bass.AP(QT, 0, [[8 * 1024, 128], [1, 8 * 1024]]))
            zt_ready = kc.s_v.v
            for dc in range(2):
                buf_wait(kc, "scalar", "a0")
                slab_load(kc, kc.arena[0], 0, 1024, wapf(kc, wo),
                          dc * 1024, 512, 8, sem=kc.ys("a0"), r0=grp * 1024)
                t_wo = slab_load(kc, kc.arena[0], 512, 1024, wapf(kc, wo),
                                 dc * 1024 + 512, 512, 8, sem=kc.ys("a0"),
                                 r0=grp * 1024)
                gw(kc, "tensor", *t_wo)
                gw(kc, "tensor", kc.s_v, zt_ready)
                t_last = None
                for tt in range(CT):
                    for ncc in range(2):
                        b = (tt * 2 + ncc) % 4
                        bank_wait(kc, b)
                        for si in range(8):
                            top(kc, lambda T, b=b, si=si, tt=tt, ncc=ncc: T.matmul(
                                kc.ps[b][:, 0:512], QT[:, si, tt * 128:(tt + 1) * 128],
                                kc.arena[0][:, si * 1024 + ncc * 512:si * 1024 + (ncc + 1) * 512],
                                start=(si == 0), stop=(si == 7)), bump=(si == 7))
                        t_mm = kc.s_t.v
                        t_last = t_mm
                        gw(kc, "vector", kc.s_t, t_mm)
                        tv = vop(kc, lambda V, b=b, tt=tt, dc=dc, ncc=ncc: V.tensor_add(
                            kc.x_res[:, tt, dc * 1024 + ncc * 512:dc * 1024 + (ncc + 1) * 512],
                            kc.x_res[:, tt, dc * 1024 + ncc * 512:dc * 1024 + (ncc + 1) * 512],
                            kc.ps[b][:, 0:512]))
                        bank_done(kc, b, kc.s_v, tv)
                buf_done(kc, "a0", kc.s_t, t_last)
        kc.x_ready_v = kc.s_v.v


def emit_bitnet_prep(kc):
    nc = kc.nc
    with contextlib.ExitStack() as actx:
        ee = actx.enter_context
        wstage = ee(nc.sbuf_tensor("wstage", [128, 4, 2816], F32))
        wstage2 = ee(nc.sbuf_tensor("wstage2", [128, 2816], F32))
        wqb = ee(nc.sbuf_tensor("wqb", [128, 2816], BF16))
        chs = ee(nc.sbuf_tensor("chs", [128, 96], F32))
        grdt = ee(nc.sbuf_tensor("grdt", [128, 96], F32))
        phase_barrier(kc)

        specs = [("L0_ffn_w_gate_up", kc.gu_q, 4, 2816, 10, 18),
                 ("L0_ffn_w_down", kc.wd_q, 1, 2048, 11, 19)]
        for key, qdst, nslc, slw, mwcol, swcol in specs:
            src = kc.W[key].ap()
            T0, T1 = WSHAPES[key]
            pieces = [(bd * 128, sc * slw) for bd in range(T0 // 128)
                      for sc in range(nslc)]
            nch = len(pieces)
            for ch in range(min(4, nch)):
                r_, c_ = pieces[ch]
                dma(kc, "scalar", kc.ys(f"b{ch % 4}"), wstage[:, ch % 4, 0:slw],
                    src[r_:r_ + 128, c_:c_ + slw])
            for ch in range(nch):
                gw(kc, "scalar", kc.ys(f"b{ch % 4}"))
                sop(kc, lambda Sc, ch=ch, slw=slw: Sc.activation(
                    wstage2[:, 0:slw], wstage[:, ch % 4, 0:slw], AF.Abs,
                    accum_out=chs[:, ch:ch + 1]))
                if ch + 4 < nch:
                    r_, c_ = pieces[ch + 4]
                    dma(kc, "scalar", kc.ys(f"b{ch % 4}"), wstage[:, ch % 4, 0:slw],
                        src[r_:r_ + 128, c_:c_ + slw])
            t_s = kc.s_s.v
            gw(kc, "vector", kc.s_s, t_s)
            tgv = vop(kc, lambda V, nch=nch: V.tensor_scalar(
                grdt[:, 0:nch], chs[:, 0:nch], 1024.0, MAGIC, ALU.mult, ALU.add))
            gw(kc, "scalar", kc.s_v, tgv)
            tgs = sop(kc, lambda Sc, nch=nch: Sc.activation(
                grdt[:, 0:nch], grdt[:, 0:nch], AF.Identity, bias=kc.cvals[:, 2:3]), sep=True)
            gw(kc, "vector", kc.s_s, tgs)
            vop(kc, lambda V, nch=nch: V.tensor_scalar(
                grdt[:, 0:nch], grdt[:, 0:nch], 1.0 / 1024.0, None, ALU.mult), sep=True)
            vop(kc, lambda V, nch=nch: V.tensor_sub(
                chs[:, 0:nch], chs[:, 0:nch], grdt[:, 0:nch]), sep=True)
            vop(kc, lambda V, nch=nch: V.tensor_reduce(
                kc.stats[:, 16:17], grdt[:, 0:nch], AX.X, ALU.add), sep=True)
            vop(kc, lambda V, nch=nch: V.tensor_reduce(
                kc.stats[:, 17:18], chs[:, 0:nch], AX.X, ALU.add), sep=True)
            tgv = vop(kc, lambda V: V.tensor_scalar(
                kc.stats[:, 18:19], kc.stats[:, 16:17], 1.0 / 16.0, MAGIC,
                ALU.mult, ALU.add), sep=True)
            gw(kc, "scalar", kc.s_v, tgv)
            tgs = sop(kc, lambda Sc: Sc.activation(
                kc.stats[:, 18:19], kc.stats[:, 18:19], AF.Identity,
                bias=kc.cvals[:, 2:3]), sep=True)
            gw(kc, "vector", kc.s_s, tgs)
            vop(kc, lambda V: V.tensor_scalar(
                kc.stats[:, 18:19], kc.stats[:, 18:19], 16.0, None, ALU.mult), sep=True)
            vop(kc, lambda V: V.tensor_sub(
                kc.stats[:, 19:20], kc.stats[:, 16:17], kc.stats[:, 18:19]), sep=True)
            tgv = vop(kc, lambda V: V.tensor_scalar(
                kc.stats[:, 20:21], kc.stats[:, 19:20], 16.0, MAGIC,
                ALU.mult, ALU.add), sep=True)
            gw(kc, "scalar", kc.s_v, tgv)
            tgs = sop(kc, lambda Sc: Sc.activation(
                kc.stats[:, 20:21], kc.stats[:, 20:21], AF.Identity,
                bias=kc.cvals[:, 2:3]), sep=True)
            gw(kc, "vector", kc.s_s, tgs)
            vop(kc, lambda V: V.tensor_scalar(
                kc.stats[:, 20:21], kc.stats[:, 20:21], 1.0 / 16.0, None, ALU.mult), sep=True)
            vop(kc, lambda V: V.tensor_sub(
                kc.stats[:, 21:22], kc.stats[:, 19:20], kc.stats[:, 20:21]), sep=True)
            vop(kc, lambda V: V.tensor_copy(kc.cprb[:, 3:4], kc.stats[:, 17:18]), sep=True)
            vop(kc, lambda V: V.tensor_copy(kc.stats[:, 22:23], kc.cprb[:, 3:4]), sep=True)
            vop(kc, lambda V: V.tensor_sub(
                kc.stats[:, 23:24], kc.stats[:, 17:18], kc.stats[:, 22:23]), sep=True)
            vop(kc, lambda V: V.tensor_copy(kc.cprb[:, 0:1], kc.stats[:, 18:19]), sep=True)
            vop(kc, lambda V: V.tensor_copy(kc.cprb[:, 1:2], kc.stats[:, 20:21]), sep=True)
            vop(kc, lambda V: V.tensor_copy(kc.cprb[:, 2:3], kc.stats[:, 21:22]), sep=True)
            tvp = vop(kc, lambda V: V.tensor_copy(kc.cprb[:, 4:5], kc.stats[:, 23:24]), sep=True)
            b = 6 + (kc.tp_flip % 2)
            kc.tp_flip += 1
            bank_wait(kc, b)
            gw(kc, "tensor", kc.s_v, tvp)
            t_t = top(kc, lambda T, b=b: T.matmul(
                kc.ps[b][0:1, 0:5], kc.ones_bf[:, :], kc.cprb[:, 0:5],
                start=True, stop=True))
            gw(kc, "vector", kc.s_t, t_t)
            tv = vop(kc, lambda V, b=b: V.tensor_copy(kc.kmaxb[0:1, 8:13], kc.ps[b][0:1, 0:5]))
            bank_done(kc, b, kc.s_v, tv)
            vop(kc, lambda V: V.tensor_add(kc.kmaxb[0:1, 0:1], kc.kmaxb[0:1, 12:13],
                                           kc.kmaxb[0:1, 10:11]), sep=True)
            vop(kc, lambda V: V.tensor_add(kc.kmaxb[0:1, 0:1], kc.kmaxb[0:1, 0:1],
                                           kc.kmaxb[0:1, 11:12]), sep=True)
            vop(kc, lambda V: V.tensor_add(kc.kmaxb[0:1, 0:1], kc.kmaxb[0:1, 0:1],
                                           kc.kmaxb[0:1, 9:10]), sep=True)
            vop(kc, lambda V: V.tensor_add(kc.kmaxb[0:1, 0:1], kc.kmaxb[0:1, 0:1],
                                           kc.kmaxb[0:1, 8:9]), sep=True)
            vop(kc, lambda V, n=T0 * T1: V.tensor_scalar(
                kc.kmaxb[0:1, 1:2], kc.kmaxb[0:1, 0:1], float(np.float32(1.0 / n)),
                1e-5, ALU.mult, ALU.max), sep=True)
            vop(kc, lambda V: V.reciprocal(kc.kmaxb[0:1, 2:3], kc.kmaxb[0:1, 1:2]), sep=True)
            vop(kc, lambda V: V.tensor_scalar(kc.kmaxb[0:1, 3:4], kc.kmaxb[0:1, 1:2],
                                              4097.0, None, ALU.mult), sep=True)
            vop(kc, lambda V: V.tensor_sub(kc.kmaxb[0:1, 4:5], kc.kmaxb[0:1, 3:4],
                                           kc.kmaxb[0:1, 1:2]), sep=True)
            vop(kc, lambda V: V.tensor_sub(kc.kmaxb[0:1, 3:4], kc.kmaxb[0:1, 3:4],
                                           kc.kmaxb[0:1, 4:5]), sep=True)
            vop(kc, lambda V: V.tensor_sub(kc.kmaxb[0:1, 4:5], kc.kmaxb[0:1, 1:2],
                                           kc.kmaxb[0:1, 3:4]), sep=True)
            vop(kc, lambda V: V.tensor_scalar(kc.kmaxb[0:1, 5:6], kc.kmaxb[0:1, 2:3],
                                              4097.0, None, ALU.mult), sep=True)
            vop(kc, lambda V: V.tensor_sub(kc.kmaxb[0:1, 6:7], kc.kmaxb[0:1, 5:6],
                                           kc.kmaxb[0:1, 2:3]), sep=True)
            vop(kc, lambda V: V.tensor_sub(kc.kmaxb[0:1, 5:6], kc.kmaxb[0:1, 5:6],
                                           kc.kmaxb[0:1, 6:7]), sep=True)
            vop(kc, lambda V: V.tensor_sub(kc.kmaxb[0:1, 6:7], kc.kmaxb[0:1, 2:3],
                                           kc.kmaxb[0:1, 5:6]), sep=True)
            vop(kc, lambda V: V.tensor_mul(kc.kmaxb[0:1, 7:8], kc.kmaxb[0:1, 3:4],
                                           kc.kmaxb[0:1, 5:6]), sep=True)
            vop(kc, lambda V: V.tensor_scalar(kc.kmaxb[0:1, 7:8], kc.kmaxb[0:1, 7:8],
                                              -1.0, 1.0, ALU.mult, ALU.add), sep=True)
            vop(kc, lambda V: V.tensor_mul(kc.kmaxb[0:1, 13:14], kc.kmaxb[0:1, 3:4],
                                           kc.kmaxb[0:1, 6:7]), sep=True)
            vop(kc, lambda V: V.tensor_sub(kc.kmaxb[0:1, 7:8], kc.kmaxb[0:1, 7:8],
                                           kc.kmaxb[0:1, 13:14]), sep=True)
            vop(kc, lambda V: V.tensor_mul(kc.kmaxb[0:1, 13:14], kc.kmaxb[0:1, 4:5],
                                           kc.kmaxb[0:1, 5:6]), sep=True)
            vop(kc, lambda V: V.tensor_sub(kc.kmaxb[0:1, 7:8], kc.kmaxb[0:1, 7:8],
                                           kc.kmaxb[0:1, 13:14]), sep=True)
            vop(kc, lambda V: V.tensor_mul(kc.kmaxb[0:1, 13:14], kc.kmaxb[0:1, 4:5],
                                           kc.kmaxb[0:1, 6:7]), sep=True)
            vop(kc, lambda V: V.tensor_sub(kc.kmaxb[0:1, 7:8], kc.kmaxb[0:1, 7:8],
                                           kc.kmaxb[0:1, 13:14]), sep=True)
            vop(kc, lambda V: V.tensor_mul(kc.kmaxb[0:1, 7:8], kc.kmaxb[0:1, 7:8],
                                           kc.kmaxb[0:1, 2:3]), sep=True)
            vop(kc, lambda V: V.tensor_add(kc.kmaxb[0:1, 2:3], kc.kmaxb[0:1, 2:3],
                                           kc.kmaxb[0:1, 7:8]), sep=True)
            tvb = kc.s_v.v
            gw(kc, "scalar", kc.s_v, tvb)
            dma(kc, "scalar", kc.sd_st, bass.AP(kc.scrM, 0, [[1, 1], [1, 1]]),
                kc.kmaxb[0:1, 1:2])
            dma(kc, "scalar", kc.sd_st, bass.AP(kc.scr128, 0, [[1, 1], [1, 1]]),
                kc.kmaxb[0:1, 2:3])
            gw(kc, "scalar", kc.sd_st)
            dma(kc, "scalar", kc.sd_st, kc.stats[:, mwcol:mwcol + 1],
                bass.AP(kc.scrM, 0, [[0, 128], [1, 1]]))
            dma(kc, "scalar", kc.sd_st, kc.stats[:, swcol:swcol + 1],
                bass.AP(kc.scr128, 0, [[0, 128], [1, 1]]))
            gw(kc, "scalar", kc.sd_st)
            gw(kc, "vector", kc.sd_st)
            if getattr(kc, "debug2", False):
                dma(kc, "scalar", kc.sd_out,
                    kc.dbg["dbg_kmx"][0:16, (0 if key.endswith("up") else 128):
                                      (128 if key.endswith("up") else 256)],
                    kc.kmaxb[0:16, :])
                dma(kc, "scalar", kc.sd_out,
                    kc.dbg["dbg_acc"][:, (0 if key.endswith("up") else 4):
                                      (4 if key.endswith("up") else 8)],
                    kc.stats[:, 16:20])
            for ch in range(min(4, nch)):
                r_, c_ = pieces[ch]
                dma(kc, "scalar", kc.ys(f"b{ch % 4}"), wstage[:, ch % 4, 0:slw],
                    src[r_:r_ + 128, c_:c_ + slw])
            tv = None
            for ch in range(nch):
                r_, c_ = pieces[ch]
                gw(kc, "scalar", kc.ys(f"b{ch % 4}"))
                if ch >= 1:
                    gw(kc, "vector", kc.ys("bq"))
                gw(kc, "vector", kc.ys(f"b{ch % 4}"))
                tm = vop(kc, lambda V, ch=ch, slw=slw, c=swcol: V.tensor_scalar(
                    wstage2[:, 0:slw], wstage[:, ch % 4, 0:slw], kc.stats[:, c:c + 1],
                    MAGIC, ALU.mult, ALU.add))
                gw(kc, "scalar", kc.s_v, tm)
                t_r = sop(kc, lambda Sc, slw=slw: Sc.activation(
                    wstage2[:, 0:slw], wstage2[:, 0:slw], AF.Identity, bias=kc.cvals[:, 2:3]))
                gw(kc, "vector", kc.s_s, t_r)
                tv = vop(kc, lambda V, slw=slw: V.tensor_scalar(
                    wqb[:, 0:slw], wstage2[:, 0:slw], -1.0, 1.0, ALU.max, ALU.min))
                gw(kc, "scalar", kc.s_v, tv)
                dma(kc, "scalar", kc.ys("bq"), qdst[r_:r_ + 128, c_:c_ + slw],
                    wqb[:, 0:slw])
                if ch + 4 < nch:
                    r_, c_ = pieces[ch + 4]
                    dma(kc, "scalar", kc.ys(f"b{ch % 4}"), wstage[:, ch % 4, 0:slw],
                        src[r_:r_ + 128, c_:c_ + slw])


def emit_bitnet_ffn(kc):
    nc = kc.nc
    with contextlib.ExitStack() as actx:
        ee = actx.enter_context
        featbuf = ee(nc.sbuf_tensor("featbufB", [128, 16, 1024], BF16))
        xn32 = ee(nc.sbuf_tensor("xn32", [128, D], F32))
        t32 = ee(nc.sbuf_tensor("t32", [128, D], F32))
        wnb = ee(nc.sbuf_tensor("wnb", [128, D], F32))
        rbuf = ee(nc.sbuf_tensor("rbuf", [128, 256], F32))
        tbuf = ee(nc.sbuf_tensor("tbuf", [128, 256], F32))
        hbuf = ee(nc.sbuf_tensor("hbuf", [128, 2, 256], BF16))
        phase_barrier(kc)

        t_wn = dma(kc, "scalar", kc.sd_st, wnb[:, :],
                   bass.AP(kc.W["L0_ffn_norm_weight"], 0, [[0, 128], [1, D]]))
        vop(kc, lambda V: V.memset(kc.pvec3[:, :], 0.0))
        txq_prev = None
        for tt in range(CT):
            xv = emit_ln(kc, tt, xn32[:, :], rms_col=tt)
            gw(kc, "vector", kc.sd_st)
            vop(kc, lambda V: V.tensor_mul(xn32[:, :], xn32[:, :], wnb[:, :]), sep=True)
            tv = vop(kc, lambda V: V.tensor_reduce(
                kc.stats[:, 16:17], xn32[:, :], AX.X, ALU.max, apply_absolute_value=True),
                sep=True)
            vop(kc, lambda V, tt=tt: V.tensor_mul(kc.stats[:, 17:18],
                                                  kc.pvec0[:, tt:tt + 1], kc.stats[:, 16:17]),
                sep=True)
            vop(kc, lambda V: V.tensor_scalar(kc.stats[:, 17:18], kc.stats[:, 17:18],
                                              1e-5, None, ALU.max), sep=True)
            vop(kc, lambda V: V.reciprocal(kc.stats[:, 18:19], kc.stats[:, 17:18]), sep=True)
            vop(kc, lambda V, tt=tt: V.tensor_mul(kc.stats[:, 19:20],
                                                  kc.stats[:, 18:19], kc.pvec0[:, tt:tt + 1]),
                sep=True)
            vop(kc, lambda V: V.tensor_scalar(kc.stats[:, 19:20], kc.stats[:, 19:20],
                                              127.0, None, ALU.mult), sep=True)
            vop(kc, lambda V: V.tensor_scalar(kc.stats[:, 20:21], kc.stats[:, 17:18],
                                              1.0 / 127.0, None, ALU.mult))
            vop(kc, lambda V: V.tensor_mul(kc.stats[:, 20:21], kc.stats[:, 20:21],
                                           kc.stats[:, 10:11]), sep=True)
            vop(kc, lambda V: V.tensor_mul(kc.stats[:, 21:22], kc.stats[:, 20:21],
                                           kc.stats[:, 20:21]), sep=True)
            vop(kc, lambda V, tt=tt: V.tensor_mul(kc.pvec2[:, tt:tt + 1],
                                                  kc.stats[:, 21:22], kc.stats[:, 20:21]),
                sep=True)
            if txq_prev is not None:
                gw(kc, "vector", kc.s_s, txq_prev)
            tq = vop(kc, lambda V: V.tensor_scalar(t32[:, :], xn32[:, :],
                                                   kc.stats[:, 19:20], MAGIC,
                                                   ALU.mult, ALU.add))
            gw(kc, "scalar", kc.s_v, tq)
            buf_wait(kc, "scalar", f"xn{tt % 2}")
            txq = sop(kc, lambda Sc, tt=tt: Sc.activation(
                kc.xn_blk[:, tt % 2, :], t32[:, :], AF.Identity, bias=kc.cvals[:, 2:3]))
            txq_prev = txq
            if getattr(kc, "debug2", False) and tt == 0:
                gw(kc, "scalar", kc.s_s, txq)
                dma(kc, "scalar", kc.sd_out, kc.dbg["dbg_xq"][:, :], kc.xn_blk[:, 0, :])
                dma(kc, "scalar", kc.sd_out, kc.dbg["dbg_sc"][:, :], kc.stats[:, :])
            gw(kc, "tensor", kc.s_s, txq)
            emit_t2f(kc, (lambda kcc, tt=tt: kc.xn_blk[:, tt % 2, kcc * 128:(kcc + 1) * 128]),
                     tt * 128, 16, featbuf)
            buf_done(kc, f"xn{tt % 2}", kc.s_t, kc.s_t.v)
        featbuf_ready = kc.s_v.v
        gw(kc, "tensor", kc.s_v, featbuf_ready)
        gw(kc, "sync", kc.ys("bq"))
        prev_tb = None
        for fs in range(22):
            pp = fs % 2
            gbase, ubase = pp * 4096, pp * 4096
            buf_wait(kc, "sync", f"a0h{pp}")
            t_g = slab_load(kc, kc.arena[0], gbase, 256, kc.gu_q.ap(), fs * 256, 256, 16,
                            sem=kc.ys(f"a0h{pp}"), f32=False)
            buf_wait(kc, "sync", f"a1h{pp}")
            t_u = slab_load(kc, kc.arena[1], ubase, 256, kc.gu_q.ap(),
                            F0 + fs * 256, 256, 16, sem=kc.ys(f"a1h{pp}"), f32=False)
            gw(kc, "tensor", *t_g)
            gw(kc, "tensor", *t_u)
            t_last = None
            for tt in range(CT):
                bank_wait(kc, 4)
                for i in range(16):
                    top(kc, lambda T, i=i, tt=tt, gbase=gbase: T.matmul(
                        kc.ps[4][:, 0:256], featbuf[:, i, tt * 128:(tt + 1) * 128],
                        kc.arena[0][:, gbase + i * 256:gbase + (i + 1) * 256],
                        start=(i == 0), stop=(i == 15)), bump=(i == 15))
                bank_wait(kc, 5)
                for i in range(16):
                    top(kc, lambda T, i=i, tt=tt, ubase=ubase: T.matmul(
                        kc.ps[5][:, 0:256], featbuf[:, i, tt * 128:(tt + 1) * 128],
                        kc.arena[1][:, ubase + i * 256:ubase + (i + 1) * 256],
                        start=(i == 0), stop=(i == 15)), bump=(i == 15))
                t_last = kc.s_t.v
                gw(kc, "scalar", kc.s_t, t_last)
                if prev_tb is not None:
                    gw(kc, "scalar", kc.s_v, prev_tb)
                t_re = sop(kc, lambda Sc: Sc.activation(rbuf[:, :], kc.ps[4][:, 0:256],
                                                        AF.Relu))
                gw(kc, "vector", kc.s_s, t_re)
                prev_tb = vop(kc, lambda V: V.scalar_tensor_tensor(
                    tbuf[:, :], kc.ps[4][:, 0:256], 1.0, rbuf[:, :], ALU.mult, ALU.mult))
                buf_wait(kc, "vector", f"hb{tt % 2}")
                tv = vop(kc, lambda V, tt=tt: V.scalar_tensor_tensor(
                    hbuf[:, tt % 2, :], kc.ps[5][:, 0:256], 1.0, tbuf[:, :],
                    ALU.mult, ALU.mult), sep=True)
                bank_done(kc, 4, kc.s_v, tv)
                bank_done(kc, 5, kc.s_v, tv)
                vop(kc, lambda V, tt=tt: V.tensor_reduce(
                    kc.stats[:, 16:17], hbuf[:, tt % 2, :], AX.X, ALU.max,
                    apply_absolute_value=True), sep=True)
                vop(kc, lambda V, tt=tt: V.tensor_max(
                    kc.pvec3[:, tt:tt + 1], kc.pvec3[:, tt:tt + 1], kc.stats[:, 16:17]),
                    sep=True)
                gw(kc, "scalar", kc.s_v, kc.s_v.v)
                dma(kc, "scalar", kc.ys("h"),
                    kc.h_dram[tt * 128:(tt + 1) * 128, fs * 256:(fs + 1) * 256],
                    hbuf[:, tt % 2, :])
                buf_done(kc, f"hb{tt % 2}", kc.ys("h"), None)
            buf_done(kc, f"a0h{pp}", kc.s_t, t_last)
            buf_done(kc, f"a1h{pp}", kc.s_t, t_last)
        if getattr(kc, "debug2", False):
            gw(kc, "gpsimd", kc.ys("h"))
            dma(kc, "gpsimd", kc.sd_out, kc.dbg["dbg_h"][:, :], kc.h_dram[0:128, :])
            dma(kc, "gpsimd", kc.sd_out, kc.dbg["dbg_guq"][:, :], kc.gu_q[:, :])
            dma(kc, "gpsimd", kc.sd_out, kc.dbg["dbg_wdq"][:, :], kc.wd_q[:, :])
            gw(kc, "gpsimd", kc.s_v, kc.s_v.v)
            dma(kc, "gpsimd", kc.sd_out, kc.dbg["dbg_pv"][:, :], kc.pvec2[:, :])
            dma(kc, "gpsimd", kc.sd_out, kc.dbg["dbg_pv3"][:, :], kc.pvec3[:, :])

    with contextlib.ExitStack() as actx:
        ee = actx.enter_context
        hqT = ee(nc.sbuf_tensor("hqT", [128, 44, 512], BF16))
        t32h = ee(nc.sbuf_tensor("t32h", [128, F0], F32))
        hb = ee(nc.sbuf_tensor("hb", [128, F0], BF16))
        phase_barrier(kc)

        prev_tp = None
        prev_tq = None
        for halfb in range(2):
            for ttl in range(4):
                tt = halfb * 4 + ttl
                if prev_tp is not None:
                    gw(kc, "scalar", kc.s_t, prev_tp)
                dma(kc, "scalar", kc.ys("h"), hb[:, :],
                    kc.h_dram[tt * 128:(tt + 1) * 128, :])
                vop(kc, lambda V, tt=tt: V.tensor_mul(
                    kc.stats[:, 16:17], kc.pvec2[:, tt:tt + 1], kc.pvec3[:, tt:tt + 1]))
                vop(kc, lambda V: V.tensor_scalar(kc.stats[:, 16:17], kc.stats[:, 16:17],
                                                  1e-5, None, ALU.max), sep=True)
                vop(kc, lambda V: V.reciprocal(kc.stats[:, 17:18], kc.stats[:, 16:17]), sep=True)
                vop(kc, lambda V, tt=tt: V.tensor_mul(kc.stats[:, 18:19],
                                                      kc.stats[:, 17:18],
                                                      kc.pvec2[:, tt:tt + 1]), sep=True)
                vop(kc, lambda V: V.tensor_scalar(kc.stats[:, 18:19], kc.stats[:, 18:19],
                                                  127.0, None, ALU.mult), sep=True)
                vop(kc, lambda V: V.tensor_scalar(kc.stats[:, 19:20], kc.stats[:, 16:17],
                                                  1.0 / 127.0, None, ALU.mult))
                vop(kc, lambda V, tt=tt: V.tensor_mul(kc.pvec4[:, tt:tt + 1],
                                                      kc.stats[:, 19:20],
                                                      kc.stats[:, 11:12]), sep=True)
                gw(kc, "vector", kc.ys("h"))
                if prev_tq is not None:
                    gw(kc, "vector", kc.s_s, prev_tq)
                tq = vop(kc, lambda V: V.tensor_scalar(
                    t32h[:, :], hb[:, :], kc.stats[:, 18:19], MAGIC, ALU.mult, ALU.add))
                gw(kc, "scalar", kc.s_v, tq)
                thq = sop(kc, lambda Sc: Sc.activation(hb[:, :], t32h[:, :],
                                                       AF.Identity, bias=kc.cvals[:, 2:3]))
                prev_tq = thq
                for g in range(11):
                    bk = 6 + (kc.tp_flip % 2)
                    kc.tp_flip += 1
                    bank_wait(kc, bk)
                    if g == 0:
                        gw(kc, "tensor", kc.s_s, thq)
                    for j in range(4):
                        kcc = g * 4 + j
                        top(kc, lambda T, bk=bk, j=j, kcc=kcc: T.matmul(
                            kc.ps[bk][:, j * 128:(j + 1) * 128],
                            hb[:, kcc * 128:(kcc + 1) * 128], kc.ident[:, :],
                            start=True, stop=True), bump=(j == 3))
                    gw(kc, "vector", kc.s_t, kc.s_t.v)
                    prev_tp = kc.s_t.v
                    tv = vop(kc, lambda V, bk=bk, g=g, ttl=ttl: V.tensor_copy(
                        bass.AP(hqT, g * 4 * 512 + ttl * 128,
                                [[44 * 512, 128], [512, 4], [1, 128]]),
                        bass.AP(kc.ps[bk], 0, [[512, 128], [128, 4], [1, 128]])))
                    bank_done(kc, bk, kc.s_v, tv)
            hq_ready = kc.s_v.v
            for sl in range(8):
                pp = sl % 2
                buf_wait(kc, "sync", f"a{pp}")
                t_w = slab_load(kc, kc.arena[pp], 0, 256, kc.wd_q.ap(), sl * 256, 256, 44,
                                sem=kc.ys(f"a{pp}"), f32=False)
                if getattr(kc, "debug2", False) and halfb == 1 and sl == 4:
                    gw(kc, "gpsimd", kc.ys(f"a{pp}"))
                    dma(kc, "gpsimd", kc.sd_out, kc.dbg["dbg_slab"][:, :],
                        kc.arena[pp][:, 29 * 256:30 * 256])
                    gw(kc, "gpsimd", kc.s_v, hq_ready)
                    dma(kc, "gpsimd", kc.sd_out, kc.dbg["dbg_hqt"][:, :],
                        bass.AP(hqT, 29 * 512, [[44 * 512, 128], [1, 512]]))
                gw(kc, "tensor", *t_w)
                gw(kc, "tensor", kc.s_v, hq_ready)
                t_last = None
                for ttl in range(4):
                    tt = halfb * 4 + ttl
                    b = ttl % 4
                    bank_wait(kc, b)
                    for i in range(44):
                        top(kc, lambda T, b=b, i=i, ttl=ttl, pp=pp: T.matmul(
                            kc.ps[b][:, 0:256], hqT[:, i, ttl * 128:(ttl + 1) * 128],
                            kc.arena[pp][:, i * 256:(i + 1) * 256],
                            start=(i == 0), stop=(i == 43)), bump=(i == 43))
                    t_mm = kc.s_t.v
                    t_last = t_mm
                    gw(kc, "vector", kc.s_t, t_mm)
                    tv = vop(kc, lambda V, b=b, tt=tt, sl=sl: V.scalar_tensor_tensor(
                        kc.x_res[:, tt, sl * 256:(sl + 1) * 256], kc.ps[b][:, 0:256],
                        kc.pvec4[:, tt:tt + 1],
                        kc.x_res[:, tt, sl * 256:(sl + 1) * 256], ALU.mult, ALU.add))
                    bank_done(kc, b, kc.s_v, tv)
                buf_done(kc, f"a{pp}", kc.s_t, t_last)
        kc.x_ready_v = kc.s_v.v


def emit_moe(kc):
    nc = kc.nc
    with contextlib.ExitStack() as actx:
        ee = actx.enter_context
        featbuf = ee(nc.sbuf_tensor("featbufD", [128, 16, 1024], BF16))
        hT = ee(nc.sbuf_tensor("hT", [128, 11, 1024], BF16))
        gwT = ee(nc.sbuf_tensor("gwT", [128, 16, 8], BF16))
        sgwT = ee(nc.sbuf_tensor("sgwT", [128, 16, 1], BF16))
        cmb = ee(nc.sbuf_tensor("cmb", [128, CT, NEXP], F32))
        slb = ee(nc.sbuf_tensor("slb", [128, 512], BF16))
        gwsb = ee(nc.sbuf_tensor("gwsb", [8, D], BF16))
        sgwsb = ee(nc.sbuf_tensor("sgwsb", [1, D], BF16))
        stgM = ee(nc.sbuf_tensor("stgM", [128, 8, 512], F32))
        phase_barrier(kc)
        kc.stgT, kc.stg_n = stgM, 8
        kc.stg_dep = [None] * 16
        kc.stg_flip = 0

        for tt in range(CT):
            buf_wait(kc, "vector", f"xn{tt % 2}")
            xv = emit_ln(kc, tt, kc.xn_blk[:, tt % 2, :])
            emit_t2f(kc, (lambda kcc, tt=tt: kc.xn_blk[:, tt % 2, kcc * 128:(kcc + 1) * 128]),
                     tt * 128, 16, featbuf, src_ready=(kc.s_v, xv))
            buf_done(kc, f"xn{tt % 2}", kc.s_t, kc.s_t.v)

        featbuf_ready = kc.s_v.v
        gw(kc, "tensor", kc.s_v, featbuf_ready)
        kc._slb_prev = None
        dma(kc, "scalar", kc.sd_st,
            bass.AP(stgM, 0, [[8 * 512, 8], [512, 4], [1, 512]]),
            kc.W["L1_ffn_gate_w"][:, :])
        dma(kc, "scalar", kc.sd_st,
            bass.AP(stgM, 4 * 512, [[8 * 512, 1], [512, 4], [1, 512]]),
            kc.W["L1_ffn_shared_gate_w"][:, :])
        gw(kc, "vector", kc.sd_st)
        vop(kc, lambda V: V.tensor_copy(
            bass.AP(gwsb, 0, [[D, 8], [512, 4], [1, 512]]),
            bass.AP(stgM, 0, [[8 * 512, 8], [512, 4], [1, 512]])))
        tgc = vop(kc, lambda V: V.tensor_copy(
            bass.AP(sgwsb, 0, [[D, 1], [512, 4], [1, 512]]),
            bass.AP(stgM, 4 * 512, [[8 * 512, 1], [512, 4], [1, 512]])))
        gw(kc, "tensor", kc.s_v, tgc)
        gw(kc, "scalar", kc.s_v, tgc)
        tss0 = sop(kc, lambda Sc: Sc.activation(kc.jsep[:, 0:1], kc.cvals[:, 0:1],
                                                AF.Identity))
        kc.stg_dep = [tss0] * 16
        for i in range(16):
            bk = 6 + (kc.tp_flip % 2)
            kc.tp_flip += 1
            bank_wait(kc, bk)
            top(kc, lambda T, bk=bk, i=i: T.matmul(
                kc.ps[bk][:, 0:8], gwsb[0:8, i * 128:(i + 1) * 128],
                kc.ident[0:8, 0:8], start=True, stop=True), bump=False)
            top(kc, lambda T, bk=bk, i=i: T.matmul(
                kc.ps[bk][:, 8:9], sgwsb[0:1, i * 128:(i + 1) * 128],
                kc.ident[0:1, 0:1], start=True, stop=True))
            gw(kc, "vector", kc.s_t, kc.s_t.v)
            vop(kc, lambda V, bk=bk, i=i: V.tensor_copy(gwT[:, i, :], kc.ps[bk][:, 0:8]))
            tv = vop(kc, lambda V, bk=bk, i=i: V.tensor_copy(sgwT[:, i, :], kc.ps[bk][:, 8:9]))
            bank_done(kc, bk, kc.s_v, tv)

        gw(kc, "tensor", kc.s_v, kc.s_v.v)
        prev_cmb = None
        for tt in range(CT):
            bank_wait(kc, 4)
            for i in range(16):
                top(kc, lambda T, i=i, tt=tt: T.matmul(
                    kc.ps[4][:, 0:8], featbuf[:, i, tt * 128:(tt + 1) * 128], gwT[:, i, :],
                    start=(i == 0), stop=(i == 15)), bump=(i == 15))
            bank_wait(kc, 5)
            for i in range(16):
                top(kc, lambda T, i=i, tt=tt: T.matmul(
                    kc.ps[5][:, 0:1], featbuf[:, i, tt * 128:(tt + 1) * 128], sgwT[:, i, :],
                    start=(i == 0), stop=(i == 15)), bump=(i == 15))
            t_mm = kc.s_t.v
            gw(kc, "vector", kc.s_t, t_mm)
            vop(kc, lambda V: V.tensor_reduce(kc.stats[:, 16:17], kc.ps[4][:, 0:8],
                                              AX.X, ALU.max))
            tv = vop(kc, lambda V: V.tensor_scalar(kc.stats[:, 16:17], kc.stats[:, 16:17],
                                                   -1.0, None, ALU.mult), sep=True)
            gw(kc, "scalar", kc.s_v, tv)
            if prev_cmb is not None:
                gw(kc, "scalar", kc.s_v, prev_cmb)
            t_e = sop(kc, lambda Sc: Sc.activation(kc.stats[:, 20:28], kc.ps[4][:, 0:8],
                                                   AF.Exp, bias=kc.stats[:, 16:17]))
            t_g = sop(kc, lambda Sc, tt=tt: Sc.activation(
                kc.pvec5[:, tt:tt + 1], kc.ps[5][:, 0:1], AF.Sigmoid))
            bank_done(kc, 4, kc.s_s, t_e)
            bank_done(kc, 5, kc.s_s, t_g)
            gw(kc, "vector", kc.s_s, t_g)
            vop(kc, lambda V: V.tensor_reduce(kc.stats[:, 16:17], kc.stats[:, 20:28],
                                              AX.X, ALU.add))
            vop(kc, lambda V: V.reciprocal(kc.stats[:, 17:18], kc.stats[:, 16:17]), sep=True)
            vop(kc, lambda V: V.tensor_scalar(kc.stats[:, 20:28], kc.stats[:, 20:28],
                                              kc.stats[:, 17:18], None, ALU.mult), sep=True)
            scr8 = kc.svec[:, 32:40]
            vop(kc, lambda V: V.tensor_reduce(kc.stats[:, 16:17], kc.stats[:, 20:28],
                                              AX.X, ALU.max), sep=True)
            vop(kc, lambda V, scr8=scr8: V.tensor_scalar(
                scr8, kc.stats[:, 20:28], kc.stats[:, 16:17], None, ALU.is_ge), sep=True)
            vop(kc, lambda V, scr8=scr8: V.scalar_tensor_tensor(
                scr8, scr8, -1e30, kc.stats[:, 20:28], ALU.mult, ALU.add), sep=True)
            vop(kc, lambda V, scr8=scr8: V.tensor_reduce(
                kc.stats[:, 18:19], scr8, AX.X, ALU.max), sep=True)
            vop(kc, lambda V: V.tensor_tensor(kc.stats[:, 18:19], kc.stats[:, 18:19],
                                              kc.stats[:, 16:17], ALU.min), sep=True)
            vop(kc, lambda V, scr8=scr8: V.tensor_scalar(
                scr8, kc.stats[:, 20:28], kc.stats[:, 18:19], None, ALU.is_ge), sep=True)
            prev_cmb = vop(kc, lambda V, tt=tt, scr8=scr8: V.tensor_mul(
                cmb[:, tt, :], kc.stats[:, 20:28], scr8), sep=True)

        def fc_slabs(e_idx, which, fs):
            key = "L1_ffn_expert_fc1" if which == 0 else "L1_ffn_expert_fc2"
            a = kc.arena[which]
            pp = fs % 2
            base = pp * 4096
            nm = f"a{which}h{pp}"
            buf_wait(kc, "scalar", nm)
            w_ = 256 if fs < 5 else 128
            t = slab_load(kc, a, base, 256, kc.W[key].ap(),
                          e_idx * FM + fs * 256, w_, 16, sem=kc.ys(nm))
            return nm, base, w_, t

        for e_idx in range(NEXP):
            pend = (fc_slabs(e_idx, 0, 0), fc_slabs(e_idx, 1, 0))
            for fs in range(6):
                nxt = ((fc_slabs(e_idx, 0, fs + 1), fc_slabs(e_idx, 1, fs + 1))
                       if fs + 1 < 6 else None)
                (nm1, base1, w_, t1), (nm2, base2, _, t2) = pend
                pend = nxt
                gw(kc, "tensor", *t1)
                gw(kc, "tensor", *t2)
                t_last = None
                for fcq in range(w_ // 128):
                    fcc = fs * 2 + fcq
                    for qc in range(2):
                        bank_wait(kc, 4)
                        for i in range(16):
                            top(kc, lambda T, i=i, qc=qc, base1=base1, fcq=fcq: T.matmul(
                                kc.ps[4][:, 0:512],
                                kc.arena[0][:, base1 + i * 256 + fcq * 128:
                                            base1 + i * 256 + (fcq + 1) * 128],
                                featbuf[:, i, qc * 512:(qc + 1) * 512],
                                start=(i == 0), stop=(i == 15)), bump=(i == 15))
                        t_m1 = kc.s_t.v
                        gw(kc, "scalar", kc.s_t, t_m1)
                        if kc._slb_prev is not None:
                            gw(kc, "scalar", kc.s_v, kc._slb_prev)
                        t_si = sop(kc, lambda Sc: Sc.activation(slb[:, :], kc.ps[4][:, 0:512],
                                                                AF.Silu))
                        bank_done(kc, 4, kc.s_s, t_si)
                        bank_wait(kc, 5)
                        for i in range(16):
                            top(kc, lambda T, i=i, qc=qc, base2=base2, fcq=fcq: T.matmul(
                                kc.ps[5][:, 0:512],
                                kc.arena[1][:, base2 + i * 256 + fcq * 128:
                                            base2 + i * 256 + (fcq + 1) * 128],
                                featbuf[:, i, qc * 512:(qc + 1) * 512],
                                start=(i == 0), stop=(i == 15)), bump=(i == 15))
                        t_m2 = kc.s_t.v
                        t_last = t_m2
                        gw(kc, "vector", kc.s_t, t_m2)
                        gw(kc, "vector", kc.s_s, t_si)
                        tv = vop(kc, lambda V, fcc=fcc, qc=qc: V.scalar_tensor_tensor(
                            hT[:, fcc, qc * 512:(qc + 1) * 512], kc.ps[5][:, 0:512],
                            1.0, slb[:, :], ALU.mult, ALU.mult))
                        kc._slb_prev = tv
                        bank_done(kc, 5, kc.s_v, tv)
                buf_done(kc, nm1, kc.s_t, t_last)
                buf_done(kc, nm2, kc.s_t, t_last)
            hT_ready = kc.s_v.v
            pw = kc.W["L1_ffn_expert_proj"].ap()

            def load_proj(sl, e_idx=e_idx, pw=pw):
                pp = sl % 2
                nm = f"aC{pp}"
                buf_wait(kc, "scalar", nm)
                t = slab_load(kc, kc.arena[pp], 8448, 256, pw,
                              e_idx * D + sl * 256, 256, 11, sem=kc.ys(nm))
                return nm, pp, t
            pendp = load_proj(0)
            for sl in range(8):
                nxtp = load_proj(sl + 1) if sl + 1 < 8 else None
                nm, pp, t_w = pendp
                pendp = nxtp
                gw(kc, "tensor", *t_w)
                gw(kc, "tensor", kc.s_v, hT_ready)
                t_last = None
                for tt in range(CT):
                    b = tt % 4
                    bank_wait(kc, b)
                    for i in range(11):
                        top(kc, lambda T, b=b, i=i, tt=tt, pp=pp: T.matmul(
                            kc.ps[b][:, 0:256], hT[:, i, tt * 128:(tt + 1) * 128],
                            kc.arena[pp][:, 8448 + i * 256:8448 + (i + 1) * 256],
                            start=(i == 0), stop=(i == 10)), bump=(i == 10))
                    t_mm = kc.s_t.v
                    t_last = t_mm
                    gw(kc, "vector", kc.s_t, t_mm)
                    tv = vop(kc, lambda V, b=b, tt=tt, sl=sl, e_idx=e_idx:
                             V.scalar_tensor_tensor(
                                 kc.x_res[:, tt, sl * 256:(sl + 1) * 256],
                                 kc.ps[b][:, 0:256], cmb[:, tt, e_idx:e_idx + 1],
                                 kc.x_res[:, tt, sl * 256:(sl + 1) * 256],
                                 ALU.mult, ALU.add))
                    bank_done(kc, b, kc.s_v, tv)
                buf_done(kc, nm, kc.s_t, t_last)

        def shared_fc_load(fbase, fs):
            w_ = 256 if fs < 5 else 128
            pp = fs % 2
            nm1, nm2 = f"a0h{pp}", f"a1h{pp}"
            buf_wait(kc, "scalar", nm1)
            t1 = slab_load(kc, kc.arena[0], pp * 4096, 256,
                           kc.W["L1_ffn_shared_fc1"].ap(), fbase + fs * 256, w_, 16,
                           sem=kc.ys(nm1))
            buf_wait(kc, "scalar", nm2)
            t2 = slab_load(kc, kc.arena[1], pp * 4096, 256,
                           kc.W["L1_ffn_shared_fc2"].ap(), fbase + fs * 256, w_, 16,
                           sem=kc.ys(nm2))
            return w_, pp, nm1, nm2, t1, t2

        for fb in range(4):
            fbase = fb * FM
            pend_s = shared_fc_load(fbase, 0)
            for fs in range(6):
                nxt_s = shared_fc_load(fbase, fs + 1) if fs + 1 < 6 else None
                w_, pp, nm1, nm2, t1, t2 = pend_s
                pend_s = nxt_s
                gw(kc, "tensor", *t1)
                gw(kc, "tensor", *t2)
                t_last = None
                for fcq in range(w_ // 128):
                    fcc = fs * 2 + fcq
                    for qc in range(2):
                        bank_wait(kc, 4)
                        for i in range(16):
                            top(kc, lambda T, i=i, qc=qc, pp=pp, fcq=fcq: T.matmul(
                                kc.ps[4][:, 0:512],
                                kc.arena[0][:, pp * 4096 + i * 256 + fcq * 128:
                                            pp * 4096 + i * 256 + (fcq + 1) * 128],
                                featbuf[:, i, qc * 512:(qc + 1) * 512],
                                start=(i == 0), stop=(i == 15)), bump=(i == 15))
                        t_m1 = kc.s_t.v
                        gw(kc, "scalar", kc.s_t, t_m1)
                        if kc._slb_prev is not None:
                            gw(kc, "scalar", kc.s_v, kc._slb_prev)
                        t_si = sop(kc, lambda Sc: Sc.activation(slb[:, :], kc.ps[4][:, 0:512],
                                                                AF.Silu))
                        bank_done(kc, 4, kc.s_s, t_si)
                        bank_wait(kc, 5)
                        for i in range(16):
                            top(kc, lambda T, i=i, qc=qc, pp=pp, fcq=fcq: T.matmul(
                                kc.ps[5][:, 0:512],
                                kc.arena[1][:, pp * 4096 + i * 256 + fcq * 128:
                                            pp * 4096 + i * 256 + (fcq + 1) * 128],
                                featbuf[:, i, qc * 512:(qc + 1) * 512],
                                start=(i == 0), stop=(i == 15)), bump=(i == 15))
                        t_m2 = kc.s_t.v
                        t_last = t_m2
                        gw(kc, "vector", kc.s_t, t_m2)
                        gw(kc, "vector", kc.s_s, t_si)
                        tv = vop(kc, lambda V, fcc=fcc, qc=qc: V.scalar_tensor_tensor(
                            hT[:, fcc, qc * 512:(qc + 1) * 512], kc.ps[5][:, 0:512],
                            1.0, slb[:, :], ALU.mult, ALU.mult))
                        kc._slb_prev = tv
                        bank_done(kc, 5, kc.s_v, tv)
                buf_done(kc, nm1, kc.s_t, t_last)
                buf_done(kc, nm2, kc.s_t, t_last)
            hT_ready = kc.s_v.v
            def load_sproj(sl, fbase=fbase):
                pp = sl % 2
                nm = f"aC{pp}"
                buf_wait(kc, "scalar", nm)
                t = slab_load(kc, kc.arena[pp], 8448, 256,
                              kc.W["L1_ffn_shared_proj"].ap(), sl * 256, 256, 11,
                              sem=kc.ys(nm), r0=fbase)
                return nm, pp, t
            pendq = load_sproj(0)
            for sl in range(8):
                nxtq = load_sproj(sl + 1) if sl + 1 < 8 else None
                nm, pp, t_w = pendq
                pendq = nxtq
                gw(kc, "tensor", *t_w)
                gw(kc, "tensor", kc.s_v, hT_ready)
                t_last = None
                for tt in range(CT):
                    b = tt % 4
                    bank_wait(kc, b)
                    for i in range(11):
                        top(kc, lambda T, b=b, i=i, tt=tt, pp=pp: T.matmul(
                            kc.ps[b][:, 0:256], hT[:, i, tt * 128:(tt + 1) * 128],
                            kc.arena[pp][:, 8448 + i * 256:8448 + (i + 1) * 256],
                            start=(i == 0), stop=(i == 10)), bump=(i == 10))
                    t_mm = kc.s_t.v
                    t_last = t_mm
                    gw(kc, "vector", kc.s_t, t_mm)
                    tv = vop(kc, lambda V, b=b, tt=tt, sl=sl: V.scalar_tensor_tensor(
                        kc.x_res[:, tt, sl * 256:(sl + 1) * 256], kc.ps[b][:, 0:256],
                        kc.pvec5[:, tt:tt + 1],
                        kc.x_res[:, tt, sl * 256:(sl + 1) * 256], ALU.mult, ALU.add))
                    bank_done(kc, b, kc.s_v, tv)
                buf_done(kc, nm, kc.s_t, t_last)
        kc.x_ready_v = kc.s_v.v


def build(debug=False):
    nc = bass.Bass("TRN2", target_bir_lowering=False)
    kc = KC()
    kc.nc = nc
    kc.tp_flip = 0
    kc.bankdep = {}
    kc.bufdep = {}
    kc.x_ready_v = 0

    kc.xin = nc.dram_tensor("xin", [QTOK, D], F32, kind="ExternalInput")
    kc.ident_in = nc.dram_tensor("ident_in", [128, 128], BF16, kind="ExternalInput")
    kc.mask_in = nc.dram_tensor("masktbl", [128, 16 * 512], BF16, kind="ExternalInput")
    kc.W = {k: nc.dram_tensor(k, list(sh), F32, kind="ExternalInput")
            for k, sh in WSHAPES.items()}
    kc.xout = nc.dram_tensor("xout", [QTOK, D], F32, kind="ExternalOutput")
    kc.dbg = {}
    if debug:
        for nm in ("dbg_x1", "dbg_x2", "dbg_x3"):
            kc.dbg[nm] = nc.dram_tensor(nm, [QTOK, D], F32, kind="ExternalOutput")
    kc.debug2 = bool(debug) and DEBUG2
    if kc.debug2:
        kc.dbg["dbg_fb"] = nc.dram_tensor("dbg_fb", [128, 16 * 1024], BF16, kind="ExternalOutput")
        kc.dbg["dbg_xn0"] = nc.dram_tensor("dbg_xn0", [128, D], BF16, kind="ExternalOutput")
        kc.dbg["dbg_st0"] = nc.dram_tensor("dbg_st0", [128, 32], F32, kind="ExternalOutput")
        kc.dbg["dbg_kvg"] = nc.dram_tensor("dbg_kvg", [4 * 1152, 256], BF16, kind="ExternalOutput")
        kc.dbg["dbg_qt"] = nc.dram_tensor("dbg_qt", [128, 8 * 1024], BF16, kind="ExternalOutput")
        kc.dbg["dbg_nmb"] = nc.dram_tensor("dbg_nmb", [128, 16], F32, kind="ExternalOutput")
        kc.dbg["dbg_kt"] = nc.dram_tensor("dbg_kt", [128, 4096], BF16, kind="ExternalOutput")
        kc.dbg["dbg_va"] = nc.dram_tensor("dbg_va", [128, 32 * 136], BF16, kind="ExternalOutput")
        kc.dbg["dbg_z"] = nc.dram_tensor("dbg_z", [128, 8 * 1024], BF16, kind="ExternalOutput")
        kc.dbg["dbg_xq"] = nc.dram_tensor("dbg_xq", [128, D], BF16, kind="ExternalOutput")
        kc.dbg["dbg_sc"] = nc.dram_tensor("dbg_sc", [128, 32], F32, kind="ExternalOutput")
        kc.dbg["dbg_h"] = nc.dram_tensor("dbg_h", [128, F0], BF16, kind="ExternalOutput")
        kc.dbg["dbg_slab"] = nc.dram_tensor("dbg_slab", [128, 256], BF16, kind="ExternalOutput")
        kc.dbg["dbg_hqt"] = nc.dram_tensor("dbg_hqt", [128, 512], BF16, kind="ExternalOutput")
        kc.dbg["dbg_guq"] = nc.dram_tensor("dbg_guq", [D, GU], BF16, kind="ExternalOutput")
        kc.dbg["dbg_wdq"] = nc.dram_tensor("dbg_wdq", [F0, D], BF16, kind="ExternalOutput")
        kc.dbg["dbg_pv"] = nc.dram_tensor("dbg_pv", [128, CT], F32, kind="ExternalOutput")
        kc.dbg["dbg_pv3"] = nc.dram_tensor("dbg_pv3", [128, CT], F32, kind="ExternalOutput")
        kc.dbg["dbg_kmx"] = nc.dram_tensor("dbg_kmx", [16, 256], F32, kind="ExternalOutput")
        kc.dbg["dbg_acc"] = nc.dram_tensor("dbg_acc", [128, 8], F32, kind="ExternalOutput")

    kc.gu_q = nc.dram_tensor("gu_q", [D, GU], BF16)
    kc.wd_q = nc.dram_tensor("wd_q", [F0, D], BF16)
    kc.h_dram = nc.dram_tensor("h_dram", [QTOK, F0], BF16)
    kc.kv0loc = nc.dram_tensor("kv0loc", [1152, 2 * Dh], BF16)
    kc.kv0g = nc.dram_tensor("kv0g", [4 * 1152, 2 * Dh], BF16)
    kc.kv1loc = nc.dram_tensor("kv1loc", [1152, 2 * D], BF16)
    kc.kv1g = nc.dram_tensor("kv1g", [4 * 1152, 2 * D], BF16)
    kc.scr128 = nc.dram_tensor("scr128", [16, 128], F32)
    kc.scrM = nc.dram_tensor("scrM", [1, 16], F32)

    stack = contextlib.ExitStack()
    with stack:
        e = stack.enter_context
        kc.x_res = e(nc.sbuf_tensor("x_res", [128, CT, D], F32))
        kc.arena = [e(nc.sbuf_tensor(f"arena{i}", [128, 11264], BF16)) for i in range(2)]
        kc.xn_blk = e(nc.sbuf_tensor("xn_blk", [128, 2, D], BF16))
        kc.ident = e(nc.sbuf_tensor("identt", [128, 128], BF16))
        kc.ones_bf = e(nc.sbuf_tensor("ones_bf", [128, 1], BF16))
        kc.sqs = e(nc.sbuf_tensor("sqs", [128, 512], BF16))
        kc.stats = e(nc.sbuf_tensor("stats", [128, 32], F32))
        kc.svec = e(nc.sbuf_tensor("svec", [128, 64], F32))
        kc.nmb = e(nc.sbuf_tensor("nmb", [128, 16], F32))
        kc.pvec0 = e(nc.sbuf_tensor("pvec0", [128, CT], F32))
        kc.pvec2 = e(nc.sbuf_tensor("pvec2", [128, CT], F32))
        kc.pvec3 = e(nc.sbuf_tensor("pvec3", [128, CT], F32))
        kc.pvec4 = e(nc.sbuf_tensor("pvec4", [128, CT], F32))
        kc.pvec5 = e(nc.sbuf_tensor("pvec5", [128, CT], F32))
        kc.kmaxb = e(nc.sbuf_tensor("kmaxb", [16, 128], F32))
        kc.cvals = e(nc.sbuf_tensor("cvals", [128, 4], F32))
        kc.jsep = e(nc.sbuf_tensor("jsep", [128, 4], F32))
        kc.ones_row = e(nc.sbuf_tensor("ones_row", [1, 128], BF16))
        kc.cprb = e(nc.sbuf_tensor("cprb", [128, 18], BF16))
        kc.ps = [e(nc.psum_tensor(f"psb{i}", [128, 512], F32)) for i in range(8)]
        kc.stg = e(nc.sbuf_tensor("stg", [128, 2, 512], F32))
        kc.sd_g = Sem(stack, nc, "sd_g")
        kc.sd_bulk = Sem(stack, nc, "sd_bulk")
        kc.sd_x = Sem(stack, nc, "sd_x")
        kc.sd_n = Sem(stack, nc, "sd_n")
        kc.sd_kv = Sem(stack, nc, "sd_kv")
        kc.sd_st = Sem(stack, nc, "sd_st")
        kc.sd_mk = Sem(stack, nc, "sd_mk")
        kc.sd_out = Sem(stack, nc, "sd_out")
        kc.sd_s = Sem(stack, nc, "sd_s")
        kc.sd_y = Sem(stack, nc, "sd_y")
        kc.s_cc = Sem(stack, nc, "s_cc")
        kc.s_t = Sem(stack, nc, "s_t", ordered=True)
        kc.s_v = Sem(stack, nc, "s_v", ordered=True)
        kc.s_s = Sem(stack, nc, "s_s", ordered=True)
        kc._ys = {}

        def ys(name):
            if name not in kc._ys:
                kc._ys[name] = Sem(stack, nc, "ys_" + name)
            return kc._ys[name]
        kc.ys = ys

        def barrier_sems():
            out = [kc.sd_x, kc.sd_n, kc.sd_kv, kc.sd_st, kc.sd_mk, kc.sd_out,
                   kc.sd_s, kc.sd_y]
            out.extend(kc._ys.values())
            return out
        kc.barrier_sems = barrier_sems
        kc.P = Prog(nc)
        kc.stgT = kc.stg
        kc.stg_n = 2
        kc.stg_dep = [None] * 16
        kc.stg_flip = 0

        dma(kc, "gpsimd", kc.sd_x, kc.x_res[:, :, :],
            kc.xin.ap().rearrange("(t p) d -> p t d", p=128))
        dma(kc, "gpsimd", kc.sd_x, kc.ident[:, :], kc.ident_in[:, :])
        gw(kc, "vector", kc.sd_x)
        vop(kc, lambda V: V.memset(kc.ones_bf[:, :], 1.0))
        vop(kc, lambda V: V.memset(kc.ones_row[:, :], 1.0))
        vop(kc, lambda V: V.memset(kc.cvals[:, 0:1], EPS))
        vop(kc, lambda V: V.memset(kc.cvals[:, 1:2], MAGIC))
        vop(kc, lambda V: V.memset(kc.cvals[:, 2:3], -MAGIC))
        gw(kc, "tensor", kc.sd_x)
        gw(kc, "scalar", kc.sd_x)

        emit_attention(kc, 0)
        if debug:
            gw(kc, "scalar", kc.s_v, kc.x_ready_v)
            dma(kc, "scalar", kc.sd_out,
                kc.dbg["dbg_x1"].ap().rearrange("(t p) d -> p t d", p=128),
                kc.x_res[:, :, :])
        emit_bitnet_prep(kc)
        emit_bitnet_ffn(kc)
        if debug:
            gw(kc, "scalar", kc.s_v, kc.x_ready_v)
            dma(kc, "scalar", kc.sd_out,
                kc.dbg["dbg_x2"].ap().rearrange("(t p) d -> p t d", p=128),
                kc.x_res[:, :, :])
        emit_attention(kc, 1)
        if debug:
            gw(kc, "scalar", kc.s_v, kc.x_ready_v)
            dma(kc, "scalar", kc.sd_out,
                kc.dbg["dbg_x3"].ap().rearrange("(t p) d -> p t d", p=128),
                kc.x_res[:, :, :])
        emit_moe(kc)

        gw(kc, "scalar", kc.s_v, kc.x_ready_v)
        dma(kc, "scalar", kc.sd_out,
            kc.xout.ap().rearrange("(t p) d -> p t d", p=128),
            kc.x_res[:, :, :])
        gw(kc, "scalar", kc.sd_out)
        kc.P.flush()
    return nc


def _mask_table(gr):
    m = np.zeros((128, 16, 512), np.float32)
    for i in range(16):
        for s in range(4):
            gq_rel = gr + 4 * s
            blk = m[:, i, s * 128:(s + 1) * 128]
            if i < gq_rel:
                pass
            elif i > gq_rel:
                blk[:, :] = NEG
            else:
                p = np.arange(128)[:, None]
                qf = np.arange(128)[None, :]
                blk[:, :] = np.where(p <= qf, 0.0, NEG)
    return m.reshape(128, 16 * 512).astype(ml_dtypes.bfloat16)


_NC_CACHE = {}


def kernel(**inputs):
    key = "main"
    if key not in _NC_CACHE:
        _NC_CACHE[key] = build(debug=DEBUG)
    nc = _NC_CACHE[key]

    x = np.ascontiguousarray(inputs["L0_attn_x"])
    ident = np.eye(128, dtype=ml_dtypes.bfloat16)
    wts = prep_weights(inputs)
    in_maps = []
    for c in range(8):
        b, gr = c // 4, c % 4
        tiles = [gr + 4 * j for j in range(CT)]
        xs = np.ascontiguousarray(
            np.concatenate([x[b, 128 * g:128 * (g + 1), :] for g in tiles], axis=0))
        m = {"xin": xs, "ident_in": ident, "masktbl": _mask_table(gr)}
        m.update(wts)
        in_maps.append(m)

    res = run_bass_kernel_spmd(nc, in_maps, core_ids=list(range(8)))
    out = np.empty((B, S, D), np.float32)
    for c in range(8):
        b, gr = c // 4, c % 4
        xo = res.results[c]["xout"]
        for j in range(CT):
            g = gr + 4 * j
            out[b, 128 * g:128 * (g + 1), :] = xo[128 * j:128 * (j + 1), :]
    return out


if __name__ == "__main__":
    nc = build(debug=False)
    print("instructions:", nc.n_instructions())


# version 12
# speedup vs baseline: 1.4586x; 1.1888x over previous
import contextlib
import numpy as np
import ml_dtypes

import concourse.bass as bass
import concourse.mybir as mybir
from concourse.bass_utils import run_bass_kernel_spmd

F32 = mybir.dt.float32
BF16 = mybir.dt.bfloat16
AF = mybir.ActivationFunctionType
ALU = mybir.AluOpType
AX = mybir.AxisListType

B, S, D = 2, 4096, 2048
H, Dh = 16, 128
CT = 8
QTOK = CT * 128
F0 = 5632
GU = 2 * F0
NEXP, FM, FS = 8, 1408, 5632
SCALE = 1.0 / float(np.sqrt(Dh))
SCALE2 = SCALE * SCALE
MAGIC = float(1.5 * 2 ** 23)
NEG = -float(2.0 ** 30)
EPS = 1e-5
DEBUG = False
DEBUG2 = False

WSHAPES = {
    "L0_attn_q_w": (D, D), "L0_attn_k_w": (D, Dh), "L0_attn_v_w": (D, Dh),
    "L0_attn_o_w": (D, D),
    "L0_ffn_w_gate_up": (D, GU), "L0_ffn_w_down": (F0, D),
    "L0_ffn_norm_weight": (D,),
    "L1_attn_W_K": (D, H * Dh), "L1_attn_W_Q": (D, H * Dh),
    "L1_attn_W_V": (D, H * Dh), "L1_attn_W_O": (H * Dh, D),
    "L1_ffn_gate_w": (NEXP, D),
    "L1_ffn_expert_fc1": (D, NEXP * FM), "L1_ffn_expert_fc2": (D, NEXP * FM),
    "L1_ffn_expert_proj": (FM, NEXP * D),
    "L1_ffn_shared_fc1": (D, FS), "L1_ffn_shared_fc2": (D, FS),
    "L1_ffn_shared_proj": (FS, D), "L1_ffn_shared_gate_w": (1, D),
}
NO_T = ("L0_ffn_norm_weight", "L1_ffn_gate_w", "L1_ffn_shared_gate_w")


def prep_weights(inputs):
    out = {}
    for k in WSHAPES:
        a = np.asarray(inputs[k], np.float32)
        if k in NO_T:
            out[k] = np.ascontiguousarray(a.reshape(WSHAPES[k]))
        else:
            out[k] = np.ascontiguousarray(a.reshape(-1, a.shape[-1]).T)
        assert out[k].shape == tuple(WSHAPES[k]), (k, out[k].shape)
    return out


def kg_of(kt):
    return (kt % 8) * 4 + kt // 8


KTS_LOW = [kt for kt in range(32) if kg_of(kt) < 16]
KTS_HIGH = [kt for kt in range(32) if kg_of(kt) >= 16]


class Sem:
    def __init__(self, stack, nc, name, ordered=False):
        self.h = stack.enter_context(nc.semaphore(name))
        self.v = 0
        self.ordered = ordered

    def inc(self, n=1):
        self.v += n
        return self.v


class Prog:
    ENGS = ("gpsimd", "sync", "scalar", "vector", "tensor")

    def __init__(self, nc):
        self.nc = nc
        self.q = {k: [] for k in self.ENGS}

    def emit(self, eng, fn):
        self.q[eng].append(fn)

    def flush(self):
        if not any(self.q.values()):
            return
        with self.nc.Block() as block:
            for name in self.ENGS:
                fns = self.q[name]
                if fns:
                    def runner(e, fns=fns):
                        for f in fns:
                            f(e)
                    getattr(block, name)(runner)
        self.q = {k: [] for k in self.ENGS}


class KC:
    pass


def gw(kc, eng, sem, tgt=None):
    if not sem.ordered:
        tgt = sem.v
    if tgt is None or tgt <= 0:
        return
    assert tgt <= sem.v, f"wait target {tgt} > queued {sem.v}"
    kc.P.emit(eng, lambda E, s=sem, t=tgt: E.wait_ge(s.h, t))


def dma(kc, eng, sem, out, in_, **kw):
    sem.inc(16)
    kc.P.emit(eng, lambda E, o=out, i=in_, s=sem, kw=kw:
              E.dma_start(out=o, in_=i, **kw).then_inc(s.h, 16))
    return sem


def tdma(kc, out, in_, sem=None):
    sem = sem if sem is not None else kc.sd_y
    sem.inc(16)
    kc.P.emit("sync", lambda E, o=out, i=in_, s=sem: E.dma_start(out=o, in_=i, transpose=True)
              .then_inc(s.h, 16))
    return sem


def vop(kc, fn, sep=False):
    if sep:
        kc.P.emit("vector", lambda V, t=kc.s_v.v: V.wait_ge(kc.s_v.h, t))
    tgt = kc.s_v.inc(1)
    kc.P.emit("vector", lambda V, f=fn: f(V).then_inc(kc.s_v.h, 1))
    return tgt


def sop(kc, fn, sep=False):
    if sep:
        kc.P.emit("scalar", lambda Sc, t=kc.s_s.v: Sc.wait_ge(kc.s_s.h, t))
    tgt = kc.s_s.inc(1)
    kc.P.emit("scalar", lambda Sc, f=fn: f(Sc).then_inc(kc.s_s.h, 1))
    return tgt


def top(kc, fn, bump=True):
    if bump:
        tgt = kc.s_t.inc(1)
        kc.P.emit("tensor", lambda T, f=fn: f(T).then_inc(kc.s_t.h, 1))
        return tgt
    kc.P.emit("tensor", lambda T, f=fn: f(T))
    return None


def bank_wait(kc, b):
    dep = kc.bankdep.get(b)
    if dep is not None:
        gw(kc, "tensor", *dep)
        kc.bankdep[b] = None


def bank_done(kc, b, sem, tgt):
    kc.bankdep[b] = (sem, tgt)


def buf_wait(kc, eng, name):
    dep = kc.bufdep.get(name)
    if dep is not None:
        gw(kc, eng, *dep)
        kc.bufdep[name] = None


def buf_done(kc, name, sem, tgt=None):
    kc.bufdep[name] = (sem, tgt)


def phase_barrier(kc):
    tv, tt_, ts_ = kc.s_v.v, kc.s_t.v, kc.s_s.v
    for eng in ("scalar", "vector", "tensor", "sync"):
        gw(kc, eng, kc.s_v, tv)
        gw(kc, eng, kc.s_t, tt_)
        gw(kc, eng, kc.s_s, ts_)
        for sm in kc.barrier_sems():
            gw(kc, eng, sm)


def wapf(kc, k):
    return kc.W[k].ap()


def emit_ln(kc, tt, out_ap, rms_col=None):
    st = kc.stats
    gw(kc, "scalar", kc.s_v, kc.x_ready_v)
    jk = out_ap
    sop(kc, lambda Sc, tt=tt, jk=jk: Sc.activation(jk, kc.x_res[:, tt, :],
                                                   AF.Identity, accum_out=st[:, 0:1]))
    t_s = sop(kc, lambda Sc, tt=tt, jk=jk: Sc.activation(jk, kc.x_res[:, tt, :],
                                                         AF.Square, accum_out=st[:, 1:2]))
    gw(kc, "vector", kc.s_s, t_s)
    vop(kc, lambda V: V.tensor_scalar(st[:, 2:3], st[:, 0:1], 1.0 / D, None, ALU.mult))
    vop(kc, lambda V: V.tensor_scalar(st[:, 3:4], st[:, 1:2], 1.0 / D, None, ALU.mult))
    vop(kc, lambda V: V.tensor_mul(st[:, 4:5], st[:, 2:3], st[:, 2:3]))
    tv = vop(kc, lambda V: V.tensor_sub(st[:, 5:6], st[:, 3:4], st[:, 4:5]), sep=True)
    gw(kc, "scalar", kc.s_v, tv)
    tsq = sop(kc, lambda Sc: Sc.activation(st[:, 6:7], st[:, 5:6], AF.Sqrt, bias=kc.cvals[:, 0:1]))
    gw(kc, "vector", kc.s_s, tsq)
    vop(kc, lambda V: V.reciprocal(st[:, 7:8], st[:, 6:7]))
    vop(kc, lambda V: V.tensor_mul(st[:, 8:9], st[:, 2:3], st[:, 7:8]), sep=True)
    vop(kc, lambda V: V.tensor_scalar(st[:, 8:9], st[:, 8:9], -1.0, None, ALU.mult), sep=True)
    tgt = vop(kc, lambda V, tt=tt, o=out_ap: V.tensor_scalar(
        o, kc.x_res[:, tt, :], st[:, 7:8], st[:, 8:9], ALU.mult, ALU.add), sep=True)
    if rms_col is not None:
        vop(kc, lambda V: V.tensor_mul(st[:, 9:10], st[:, 5:6], st[:, 7:8]))
        tv2 = vop(kc, lambda V: V.tensor_mul(st[:, 9:10], st[:, 9:10], st[:, 7:8]), sep=True)
        gw(kc, "scalar", kc.s_v, tv2)
        ts2 = sop(kc, lambda Sc: Sc.activation(st[:, 9:10], st[:, 9:10], AF.Sqrt, bias=kc.cvals[:, 0:1]))
        gw(kc, "vector", kc.s_s, ts2)
        vop(kc, lambda V, c=rms_col: V.reciprocal(kc.pvec0[:, c:c + 1], st[:, 9:10]))
    return tgt


def emit_t2f(kc, src_fn, dst_col0, kcn, featbuf, src_ready=None, colw=1024):
    for g in range((kcn + 3) // 4):
        nj = min(4, kcn - g * 4)
        b = 6 + (kc.tp_flip % 2)
        kc.tp_flip += 1
        bank_wait(kc, b)
        if src_ready is not None:
            gw(kc, "tensor", *src_ready)
            src_ready = None
        for j in range(nj):
            kcc = g * 4 + j
            top(kc, lambda T, b=b, j=j, kcc=kcc: T.matmul(
                kc.ps[b][:, j * 128:(j + 1) * 128], src_fn(kcc), kc.ident[:, :],
                start=True, stop=True), bump=(j == nj - 1))
        gw(kc, "vector", kc.s_t, kc.s_t.v)
        tv = vop(kc, lambda V, b=b, g=g, c0=dst_col0, nj=nj, fb=featbuf, colw=colw:
                 V.tensor_copy(
                     bass.AP(fb, g * 4 * colw + c0, [[16 * colw, 128], [colw, nj], [1, 128]]),
                     bass.AP(kc.ps[b], 0, [[512, 128], [128, nj], [1, 128]])))
        bank_done(kc, b, kc.s_v, tv)


def slab_load(kc, dst_buf, base, width, wT_ap, o0, ncols, kcn, sem=None, r0=0,
              f32=True):
    if not f32:
        t = None
        for c in range(kcn):
            t = dma(kc, "sync", sem,
                    dst_buf[:, base + c * width: base + c * width + ncols],
                    wT_ap[r0 + c * 128: r0 + (c + 1) * 128, o0:o0 + ncols])
        return (t, None)
    ts_ = None
    stgT, stg_n = kc.stgT, kc.stg_n
    for c in range(kcn):
        sl = kc.stg_flip % stg_n
        kc.stg_flip += 1
        dep = kc.stg_dep[sl]
        if dep is not None:
            gw(kc, "sync", kc.s_s, dep)
        dma(kc, "sync", sem, stgT[:, sl, 0:ncols],
            wT_ap[r0 + c * 128: r0 + (c + 1) * 128, o0:o0 + ncols])
        gw(kc, "scalar", sem)
        ts_ = sop(kc, lambda Sc, c=c, sl=sl, ncols=ncols, base=base, width=width,
                  dst_buf=dst_buf, stgT=stgT: Sc.activation(
                      dst_buf[:, base + c * width: base + c * width + ncols],
                      stgT[:, sl, 0:ncols], AF.Identity))
        kc.stg_dep[sl] = ts_
    return (kc.s_s, ts_)


def cross_part_max(kc, src_ap, nh):
    tv = vop(kc, lambda V: V.tensor_copy(kc.cprb[:, 0:nh], src_ap), sep=True)
    b = 6 + (kc.tp_flip % 2)
    kc.tp_flip += 1
    bank_wait(kc, b)
    gw(kc, "tensor", kc.s_v, tv)
    t_t = top(kc, lambda T, b=b, nh=nh: T.matmul(
        kc.ps[b][0:nh, 0:128], kc.cprb[:, 0:nh], kc.ident[:, :],
        start=True, stop=True))
    gw(kc, "vector", kc.s_t, t_t)
    tv = vop(kc, lambda V, b=b, nh=nh: V.tensor_reduce(
        kc.kmaxb[0:nh, 0:1], kc.ps[b][0:nh, 0:128], AX.X, ALU.max))
    bank_done(kc, b, kc.s_v, tv)
    return tv


def cross_part_sum(kc, src_col):
    vop(kc, lambda V: V.tensor_copy(kc.cprb[:, 16:17], src_col), sep=True)
    vop(kc, lambda V: V.tensor_copy(kc.stats[:, 30:31], kc.cprb[:, 16:17]), sep=True)
    vop(kc, lambda V: V.tensor_sub(kc.stats[:, 31:32], src_col, kc.stats[:, 30:31]), sep=True)
    tv = vop(kc, lambda V: V.tensor_copy(kc.cprb[:, 17:18], kc.stats[:, 31:32]), sep=True)
    b = 6 + (kc.tp_flip % 2)
    kc.tp_flip += 1
    bank_wait(kc, b)
    gw(kc, "tensor", kc.s_v, tv)
    t_t = top(kc, lambda T, b=b: T.matmul(
        kc.ps[b][0:1, 0:2], kc.ones_bf[:, :], kc.cprb[:, 16:18],
        start=True, stop=True))
    gw(kc, "vector", kc.s_t, t_t)
    tv = vop(kc, lambda V, b=b: V.tensor_reduce(
        kc.kmaxb[0:1, 0:1], kc.ps[b][0:1, 0:2], AX.X, ALU.add))
    bank_done(kc, b, kc.s_v, tv)
    return tv


def emit_bcast16(kc, src_col_ap, nh, dst_ap):
    tv = vop(kc, lambda V, nh=nh: V.tensor_copy(kc.cprb[0:nh, 16:17], src_col_ap), sep=True)
    b = 6 + (kc.tp_flip % 2)
    kc.tp_flip += 1
    bank_wait(kc, b)
    gw(kc, "tensor", kc.s_v, tv)
    top(kc, lambda T, b=b, nh=nh: T.matmul(
        kc.ps[b][0:1, 0:nh], kc.cprb[0:nh, 16:17], kc.ident[0:nh, 0:nh],
        start=True, stop=True))
    gw(kc, "vector", kc.s_t, kc.s_t.v)
    tv = vop(kc, lambda V, b=b, nh=nh: V.tensor_copy(kc.cprb[0:1, 0:nh], kc.ps[b][0:1, 0:nh]))
    bank_done(kc, b, kc.s_v, tv)
    b2 = 6 + (kc.tp_flip % 2)
    kc.tp_flip += 1
    bank_wait(kc, b2)
    gw(kc, "tensor", kc.s_v, tv)
    top(kc, lambda T, b2=b2, nh=nh: T.matmul(
        kc.ps[b2][:, 0:nh], kc.ones_row[0:1, :], kc.cprb[0:1, 0:nh],
        start=True, stop=True))
    gw(kc, "vector", kc.s_t, kc.s_t.v)
    tv = vop(kc, lambda V, b2=b2, nh=nh, d=dst_ap: V.tensor_copy(d, kc.ps[b2][:, 0:nh]))
    bank_done(kc, b2, kc.s_v, tv)
    return tv


def emit_bcast_f32(kc, src_11, dst_col):
    vop(kc, lambda V: V.tensor_copy(kc.cprb[0:1, 0:1], src_11), sep=True)
    vop(kc, lambda V: V.tensor_copy(kc.stats[0:1, 30:31], kc.cprb[0:1, 0:1]), sep=True)
    vop(kc, lambda V: V.tensor_sub(kc.stats[0:1, 31:32], src_11, kc.stats[0:1, 30:31]), sep=True)
    tv = vop(kc, lambda V: V.tensor_copy(kc.cprb[0:1, 1:2], kc.stats[0:1, 31:32]), sep=True)
    b = 6 + (kc.tp_flip % 2)
    kc.tp_flip += 1
    bank_wait(kc, b)
    gw(kc, "tensor", kc.s_v, tv)
    top(kc, lambda T, b=b: T.matmul(
        kc.ps[b][:, 0:2], kc.ones_row[0:1, :], kc.cprb[0:1, 0:2],
        start=True, stop=True))
    gw(kc, "vector", kc.s_t, kc.s_t.v)
    vop(kc, lambda V, b=b: V.tensor_copy(kc.stats[:, 30:31], kc.ps[b][:, 0:1]))
    tv2 = vop(kc, lambda V, b=b, d=dst_col: V.tensor_add(
        d, kc.ps[b][:, 1:2], kc.stats[:, 30:31]), sep=True)
    bank_done(kc, b, kc.s_v, tv2)
    return tv2


def emit_attention(kc, layer):
    nc = kc.nc
    l1 = (layer == 1)
    kvloc, kvg = (kc.kv1loc, kc.kv1g) if l1 else (kc.kv0loc, kc.kv0g)
    KV = 2 * (D if l1 else Dh)
    half = KV // 2
    nh = half // 128
    wq, wk, wv, wo = (("L1_attn_W_Q", "L1_attn_W_K", "L1_attn_W_V", "L1_attn_W_O")
                      if l1 else
                      ("L0_attn_q_w", "L0_attn_k_w", "L0_attn_v_w", "L0_attn_o_w"))

    with contextlib.ExitStack() as actx:
        ee = actx.enter_context
        featbuf = ee(nc.sbuf_tensor("featbufA%d" % layer, [128, 16, 1024], BF16))
        QT = ee(nc.sbuf_tensor("QT%d" % layer, [128, 8, 1024], BF16))
        KTs = [ee(nc.sbuf_tensor(f"KTs{layer}{i}", [128, 4096], BF16)) for i in range(2)]
        Vaug = [ee(nc.sbuf_tensor(f"Vaug{layer}{i}", [128, 32, 136], BF16)) for i in range(2)]
        Eb = [ee(nc.sbuf_tensor(f"Eb{layer}{i}", [128, 512], BF16)) for i in range(2)]
        zblk = ee(nc.sbuf_tensor("zblk%d" % layer, [128, 128], BF16))
        phase_barrier(kc)

        for tt in range(CT):
            buf_wait(kc, "vector", f"xn{tt % 2}")
            xv = emit_ln(kc, tt, kc.xn_blk[:, tt % 2, :])
            if getattr(kc, "debug2", False) and not l1 and tt == 0:
                gw(kc, "scalar", kc.s_v, xv)
                dma(kc, "scalar", kc.sd_out, kc.dbg["dbg_xn0"][:, :], kc.xn_blk[:, 0, :])
                dma(kc, "scalar", kc.sd_out, kc.dbg["dbg_st0"][:, :], kc.stats[:, :])
            emit_t2f(kc, (lambda kcc, tt=tt: kc.xn_blk[:, tt % 2, kcc * 128:(kcc + 1) * 128]),
                     tt * 128, 16, featbuf, src_ready=(kc.s_v, xv))
            buf_done(kc, f"xn{tt % 2}", kc.s_t, kc.s_t.v)

        if getattr(kc, "debug2", False) and not l1:
            gw(kc, "scalar", kc.s_v, kc.s_v.v)
            gw(kc, "scalar", kc.s_t, kc.s_t.v)
            dma(kc, "scalar", kc.sd_out, kc.dbg["dbg_fb"][:, :],
                bass.AP(featbuf, 0, [[16 * 1024, 128], [1, 16 * 1024]]))
        featbuf_ready = kc.s_v.v
        gw(kc, "tensor", kc.s_v, featbuf_ready)
        vop(kc, lambda V: V.memset(kc.svec[:, :], 0.0))
        ocw = min(512, KV)
        n_oc = KV // ocw
        kv_store_t = None
        prev_knorm_v = None

        def load_kv_slab(oc):
            ab = kc.arena[oc % 2]
            abn = f"a{oc % 2}"
            buf_wait(kc, "scalar", abn)
            ysem = kc.ys(abn)
            if not l1:
                slab_load(kc, ab, 0, 256, wapf(kc, wk), 0, 128, 16, sem=ysem)
                t_sl = slab_load(kc, ab, 128, 256, wapf(kc, wv), 0, 128, 16, sem=ysem)
            else:
                key = wk if oc < 4 else wv
                r0 = (oc % 4) * 512
                t_sl = slab_load(kc, ab, 0, 512, wapf(kc, key), r0, 512, 16, sem=ysem)
            return ab, abn, t_sl
        pend_kv = load_kv_slab(0)
        for oc in range(n_oc):
            nxt_kv = load_kv_slab(oc + 1) if oc + 1 < n_oc else None
            ab, abn, t_sl = pend_kv
            pend_kv = nxt_kv
            gw(kc, "tensor", *t_sl)
            t_last_mm = None
            for tt in range(CT):
                b = tt % 2
                bank_wait(kc, b)
                for i in range(16):
                    top(kc, lambda T, b=b, i=i, tt=tt, ab=ab, ocw=ocw: T.matmul(
                        kc.ps[b][:, 0:ocw], featbuf[:, i, tt * 128:(tt + 1) * 128],
                        ab[:, i * ocw:(i + 1) * ocw],
                        start=(i == 0), stop=(i == 15)), bump=(i == 15))
                t_mm = kc.s_t.v
                t_last_mm = t_mm
                gw(kc, "vector", kc.s_t, t_mm)
                buf_wait(kc, "vector", f"xn{tt % 2}")
                tv = vop(kc, lambda V, b=b, tt=tt, ocw=ocw: V.tensor_copy(
                    kc.xn_blk[:, tt % 2, 0:ocw], kc.ps[b][:, 0:ocw]))
                bank_done(kc, b, kc.s_v, tv)
                if (not l1) or oc < 4:
                    kcols = 128 if not l1 else 512
                    gw(kc, "scalar", kc.s_v, tv)
                    if prev_knorm_v is not None:
                        gw(kc, "scalar", kc.s_v, prev_knorm_v)
                    t_sq = sop(kc, lambda Sc, tt=tt, kcols=kcols: Sc.activation(
                        kc.sqs[:, 0:kcols], kc.xn_blk[:, tt % 2, 0:kcols], AF.Square))
                    gw(kc, "vector", kc.s_s, t_sq)
                    nseg = kcols // 128
                    prev_knorm_v = vop(kc, lambda V, nseg=nseg: V.tensor_reduce(
                        kc.stats[:, 16:16 + nseg],
                        bass.AP(kc.sqs, 0, [[512, 128], [128, nseg], [1, 128]]),
                        AX.X, ALU.add))
                    c0 = 0 if not l1 else oc * 4
                    vop(kc, lambda V, c0=c0, nseg=nseg: V.tensor_max(
                        kc.svec[:, c0:c0 + nseg], kc.svec[:, c0:c0 + nseg],
                        kc.stats[:, 16:16 + nseg]), sep=True)
                gw(kc, "scalar", kc.s_v, kc.s_v.v)
                kv_store_t = dma(kc, "scalar", kc.sd_kv,
                                 kvloc[tt * 128:(tt + 1) * 128, oc * ocw:(oc + 1) * ocw],
                                 kc.xn_blk[:, tt % 2, 0:ocw])
                buf_done(kc, f"xn{tt % 2}", kc.sd_kv, None)
            buf_done(kc, abn, kc.s_t, t_last_mm)
        gw(kc, "gpsimd", kc.s_v, kc.s_v.v)
        n_t = dma(kc, "gpsimd", kc.sd_n, kvloc[1024:1152, 0:nh], kc.svec[:, 0:nh],
                  allow_slow_non_contiguous=True)
        gw(kc, "gpsimd", kc.sd_kv)
        gw(kc, "gpsimd", kc.sd_n)
        cc_t = kc.s_cc.inc(1)
        kc.P.emit("gpsimd", lambda G, i=kvloc, o=kvg: G.collective_compute(
            "AllGather", ALU.bypass, replica_groups=[[0, 1, 2, 3], [4, 5, 6, 7]],
            ins=[i.ap()], outs=[o.ap()]).then_inc(kc.s_cc.h, 1))
        if getattr(kc, "debug2", False) and not l1:
            gw(kc, "gpsimd", kc.s_cc, cc_t)
            dma(kc, "gpsimd", kc.sd_out, kc.dbg["dbg_kvg"][:, :], kvg[:, :])
        def norms_block():
            gw(kc, "gpsimd", kc.s_cc, cc_t)
            dma(kc, "gpsimd", kc.sd_n, kc.svec[:, 0:4 * nh],
                bass.AP(kvg, 1024 * KV, [[KV, 128], [1152 * KV, 4], [1, nh]]),
                allow_slow_non_contiguous=True)
            gw(kc, "vector", kc.sd_n)
            vop(kc, lambda V, nh=nh: V.tensor_reduce(
                kc.stats[:, 16:16 + nh],
                bass.AP(kc.svec, 0, [[64, 128], [1, nh], [nh, 4]]), AX.X, ALU.max))
            cross_part_max(kc, kc.stats[:, 16:16 + nh], nh)

        def load_kt(buf, h):
            buf_wait(kc, "sync", f"kt{buf}")
            sm = kc.ys(f"kt{buf}")
            for r in range(4):
                tdma(kc, KTs[buf][:, r * 1024:(r + 1) * 1024],
                     kvg[r * 1152:r * 1152 + 1024, h * 128:(h + 1) * 128], sem=sm)
            return sm

        def load_vaug(buf, h):
            buf_wait(kc, "scalar", f"va{buf}")
            sm = kc.ys(f"va{buf}")
            for r in range(4):
                dma(kc, "scalar", sm, Vaug[buf][:, r * 8:(r + 1) * 8, 0:128],
                    kvg.ap()[r * 1152:r * 1152 + 1024,
                             half + h * 128:half + (h + 1) * 128]
                    .rearrange("(t p) d -> p t d", p=128))
            return sm

        norms_block()
        gw(kc, "sync", kc.s_cc, cc_t)
        gw(kc, "scalar", kc.s_cc, cc_t)
        for bb in range(2):
            vop(kc, lambda V, bb=bb: V.memset(
                bass.AP(Vaug[bb], 128, [[32 * 136, 128], [136, 32], [1, 1]]), 1.0))
        kt_t = {0: load_kt(0, 0)}
        va_t = {0: load_vaug(0, 0)}

        for grp in range(2):
            gw(kc, "tensor", kc.s_v, featbuf_ready)
            vop(kc, lambda V: V.memset(kc.stats[0:1, 12:13], 0.0))
            prev_on = None
            for hs in range(2):
                buf_wait(kc, "scalar", "a1")
                t_q = slab_load(kc, kc.arena[1], 0, 512, wapf(kc, wq),
                                grp * 1024 + hs * 512, 512, 16, sem=kc.ys("a1"))
                gw(kc, "tensor", *t_q)
                t_pj = None
                for sl in range(4):
                    s = hs * 4 + sl
                    for qc in range(2):
                        b = qc
                        bank_wait(kc, b)
                        for i in range(16):
                            top(kc, lambda T, b=b, i=i, s=s, sl=sl, qc=qc: T.matmul(
                                kc.ps[b][:, 0:512],
                                kc.arena[1][:, i * 512 + sl * 128:i * 512 + (sl + 1) * 128],
                                featbuf[:, i, qc * 512:(qc + 1) * 512],
                                start=(i == 0), stop=(i == 15)), bump=(i == 15))
                        t_pj = kc.s_t.v
                        gw(kc, "vector", kc.s_t, t_pj)
                        tv = vop(kc, lambda V, b=b, s=s, qc=qc: V.tensor_copy(
                            QT[:, s, qc * 512:(qc + 1) * 512], kc.ps[b][:, 0:512]))
                        bank_done(kc, b, kc.s_v, tv)
                        gw(kc, "scalar", kc.s_v, tv)
                        if prev_on is not None:
                            gw(kc, "scalar", kc.s_t, prev_on)
                        t_sq = sop(kc, lambda Sc, s=s, qc=qc: Sc.activation(
                            kc.sqs[:, 0:512], QT[:, s, qc * 512:(qc + 1) * 512], AF.Square))
                        bank_wait(kc, 5)
                        gw(kc, "tensor", kc.s_s, t_sq)
                        t_on = top(kc, lambda T: T.matmul(
                            kc.ps[5][0:1, 0:512], kc.ones_bf[:, :], kc.sqs[:, 0:512],
                            start=True, stop=True))
                        prev_on = t_on
                        gw(kc, "vector", kc.s_t, t_on)
                        vop(kc, lambda V: V.tensor_reduce(
                            kc.stats[0:1, 13:14], kc.ps[5][0:1, 0:512], AX.X, ALU.max))
                        tv2 = vop(kc, lambda V: V.tensor_max(
                            kc.stats[0:1, 12:13], kc.stats[0:1, 12:13], kc.stats[0:1, 13:14]),
                            sep=True)
                        bank_done(kc, 5, kc.s_v, tv2)
                buf_done(kc, "a1", kc.s_t, t_pj)
            qt_ready = kc.s_v.v
            emit_bcast16(kc, kc.stats[0:1, 12:13], 1, kc.stats[:, 13:14])
            tv = vop(kc, lambda V, nh=nh: V.tensor_mul(
                kc.kmaxb[0:nh, 2:3], kc.kmaxb[0:nh, 0:1], kc.stats[0:nh, 13:14]), sep=True)
            gw(kc, "scalar", kc.s_v, tv)
            ts_ = sop(kc, lambda Sc, nh=nh: Sc.activation(
                kc.kmaxb[0:nh, 3:4], kc.kmaxb[0:nh, 2:3], AF.Sqrt, scale=SCALE2))
            gw(kc, "vector", kc.s_s, ts_)
            tv = vop(kc, lambda V, nh=nh: V.tensor_scalar(
                kc.kmaxb[0:nh, 3:4], kc.kmaxb[0:nh, 3:4], -1.0, -1.0, ALU.mult, ALU.add))
            emit_bcast16(kc, kc.kmaxb[0:nh, 3:4], nh, kc.nmb[:, 0:nh])
            nmb_v = kc.s_v.v
            gw(kc, "scalar", kc.s_v, nmb_v)

            if getattr(kc, "debug2", False) and not l1 and grp == 0:
                gw(kc, "scalar", kc.s_v, kc.s_v.v)
                dma(kc, "scalar", kc.sd_out, kc.dbg["dbg_qt"][:, :],
                    bass.AP(QT, 0, [[8 * 1024, 128], [1, 8 * 1024]]))
                dma(kc, "scalar", kc.sd_out, kc.dbg["dbg_nmb"][:, :], kc.nmb[:, :])
                gw(kc, "scalar", kc.ys("kt0"))
                gw(kc, "scalar", kc.ys("va0"))
                dma(kc, "scalar", kc.sd_out, kc.dbg["dbg_kt"][:, :], KTs[0][:, :])
                dma(kc, "scalar", kc.sd_out, kc.dbg["dbg_va"][:, :],
                    bass.AP(Vaug[0], 0, [[32 * 136, 128], [1, 32 * 136]]))
            if l1:
                buf_wait(kc, "scalar", "a1")
                dma(kc, "scalar", kc.sd_mk, kc.arena[1][:, 0:8192],
                    kc.mask_in[:, 0:8192])
                gw(kc, "tensor", kc.sd_mk)

            gw(kc, "tensor", kc.s_v, qt_ready)
            for s in range(8):
                h = grp * 8 + s
                if l1:
                    if h + 1 < 16:
                        kt_t[(h + 1) % 2] = load_kt((h + 1) % 2, h + 1)
                        va_t[(h + 1) % 2] = load_vaug((h + 1) % 2, h + 1)
                    kbuf = h % 2
                else:
                    kbuf = 0
                gw(kc, "tensor", kt_t[kbuf])
                gw(kc, "tensor", va_t[kbuf])
                gw(kc, "scalar", kc.s_v, nmb_v)
                mcol = h if l1 else 0
                for qc in range(2):
                    if l1:
                        kts = ([(kt, True) for kt in KTS_LOW] if qc == 0 else
                               [(kt, False) for kt in KTS_LOW] +
                               [(kt, True) for kt in KTS_HIGH])
                    else:
                        kts = [(kt, False) for kt in range(32)]
                    nkt = len(kts)
                    for j in range(4):
                        bank_wait(kc, j)
                    e_free = [kc.s_t.v, kc.s_t.v]

                    def emit_qk(idx, s=s, qc=qc, kbuf=kbuf, kts=kts):
                        kt, masked = kts[idx]
                        sb = 4 + (idx % 2)
                        bank_wait(kc, sb)
                        top(kc, lambda T, sb=sb, kt=kt, s=s, qc=qc, kbuf=kbuf, m=masked:
                            T.matmul(kc.ps[sb][:, 0:512],
                                     KTs[kbuf][:, kt * 128:(kt + 1) * 128],
                                     QT[:, s, qc * 512:(qc + 1) * 512],
                                     start=True, stop=not m), bump=not masked)
                        if masked:
                            bi = kg_of(kt) % 16
                            top(kc, lambda T, sb=sb, bi=bi: T.matmul(
                                kc.ps[sb][:, 0:512], kc.ident[:, :],
                                kc.arena[1][:, bi * 512:(bi + 1) * 512],
                                start=False, stop=True))
                        t_qk = kc.s_t.v
                        gw(kc, "scalar", kc.s_t, t_qk)
                        if e_free[idx % 2] is not None:
                            gw(kc, "scalar", kc.s_t, e_free[idx % 2])
                        t_exp = sop(kc, lambda Sc, sb=sb, ix=idx % 2, mc=mcol: Sc.activation(
                            Eb[ix][:, :], kc.ps[sb][:, 0:512], AF.Exp,
                            bias=kc.nmb[:, mc:mc + 1], scale=SCALE))
                        bank_done(kc, sb, kc.s_s, t_exp)
                        return t_exp

                    t_exp_cur = emit_qk(0)
                    for idx in range(nkt):
                        kt, masked = kts[idx]
                        t_exp_nxt = emit_qk(idx + 1) if idx + 1 < nkt else None
                        gw(kc, "tensor", kc.s_s, t_exp_cur)
                        for j in range(4):
                            top(kc, lambda T, j=j, ix=idx % 2, kt=kt, kbuf=kbuf,
                                st=(idx == 0), sp=(idx == nkt - 1): T.matmul(
                                kc.ps[j][:, 0:129], Eb[ix][:, j * 128:(j + 1) * 128],
                                Vaug[kbuf][:, kt, 0:129],
                                start=st, stop=sp), bump=(j == 3))
                        e_free[idx % 2] = kc.s_t.v
                        t_exp_cur = t_exp_nxt
                    t_av = kc.s_t.v
                    gw(kc, "vector", kc.s_t, t_av)
                    for j in range(4):
                        vop(kc, lambda V, j=j: V.reciprocal(
                            kc.stats[:, 14 + j:15 + j], kc.ps[j][:, 128:129]))
                    for j in range(4):
                        tv = vop(kc, lambda V, j=j: V.tensor_scalar(
                            zblk[:, :], kc.ps[j][:, 0:128], kc.stats[:, 14 + j:15 + j], None,
                            ALU.mult))
                        bank_done(kc, j, kc.s_v, tv)
                        b = 6 + (kc.tp_flip % 2)
                        kc.tp_flip += 1
                        bank_wait(kc, b)
                        gw(kc, "tensor", kc.s_v, tv)
                        t_tp = top(kc, lambda T, b=b: T.matmul(
                            kc.ps[b][:, 0:128], zblk[:, :], kc.ident[:, :],
                            start=True, stop=True))
                        gw(kc, "vector", kc.s_t, t_tp)
                        tv2 = vop(kc, lambda V, b=b, s=s, qc=qc, j=j: V.tensor_copy(
                            QT[:, s, qc * 512 + j * 128:qc * 512 + (j + 1) * 128],
                            kc.ps[b][:, 0:128]))
                        bank_done(kc, b, kc.s_v, tv2)
                if l1:
                    buf_done(kc, f"kt{kbuf}", kc.s_t, kc.s_t.v)
                    buf_done(kc, f"va{kbuf}", kc.s_t, kc.s_t.v)
            if l1:
                buf_done(kc, "a1", kc.s_t, kc.s_t.v)
            if getattr(kc, "debug2", False) and not l1 and grp == 0:
                gw(kc, "scalar", kc.s_v, kc.s_v.v)
                dma(kc, "scalar", kc.sd_out, kc.dbg["dbg_z"][:, :],
                    bass.AP(QT, 0, [[8 * 1024, 128], [1, 8 * 1024]]))
            zt_ready = kc.s_v.v
            for dc in range(2):
                buf_wait(kc, "scalar", "a0")
                slab_load(kc, kc.arena[0], 0, 1024, wapf(kc, wo),
                          dc * 1024, 512, 8, sem=kc.ys("a0"), r0=grp * 1024)
                t_wo = slab_load(kc, kc.arena[0], 512, 1024, wapf(kc, wo),
                                 dc * 1024 + 512, 512, 8, sem=kc.ys("a0"),
                                 r0=grp * 1024)
                gw(kc, "tensor", *t_wo)
                gw(kc, "tensor", kc.s_v, zt_ready)
                t_last = None
                for tt in range(CT):
                    for ncc in range(2):
                        b = (tt * 2 + ncc) % 4
                        bank_wait(kc, b)
                        for si in range(8):
                            top(kc, lambda T, b=b, si=si, tt=tt, ncc=ncc: T.matmul(
                                kc.ps[b][:, 0:512], QT[:, si, tt * 128:(tt + 1) * 128],
                                kc.arena[0][:, si * 1024 + ncc * 512:si * 1024 + (ncc + 1) * 512],
                                start=(si == 0), stop=(si == 7)), bump=(si == 7))
                        t_mm = kc.s_t.v
                        t_last = t_mm
                        gw(kc, "vector", kc.s_t, t_mm)
                        tv = vop(kc, lambda V, b=b, tt=tt, dc=dc, ncc=ncc: V.tensor_add(
                            kc.x_res[:, tt, dc * 1024 + ncc * 512:dc * 1024 + (ncc + 1) * 512],
                            kc.x_res[:, tt, dc * 1024 + ncc * 512:dc * 1024 + (ncc + 1) * 512],
                            kc.ps[b][:, 0:512]))
                        bank_done(kc, b, kc.s_v, tv)
                buf_done(kc, "a0", kc.s_t, t_last)
        kc.x_ready_v = kc.s_v.v


def emit_bitnet_prep(kc):
    nc = kc.nc
    with contextlib.ExitStack() as actx:
        ee = actx.enter_context
        wstage = ee(nc.sbuf_tensor("wstage", [128, 8, 1024], F32))
        wstage2 = ee(nc.sbuf_tensor("wstage2", [128, 2, 1024], F32))
        wqb = ee(nc.sbuf_tensor("wqb", [128, 2, 1024], BF16))
        chs = ee(nc.sbuf_tensor("chs", [128, 288], F32))
        grdt = ee(nc.sbuf_tensor("grdt", [128, 288], F32))
        phase_barrier(kc)

        specs = [("L0_ffn_w_gate_up", kc.gu_q, 11, 1024, 10, 18),
                 ("L0_ffn_w_down", kc.wd_q, 2, 1024, 11, 19)]
        for key, qdst, nslc, slw, mwcol, swcol in specs:
            src = kc.W[key].ap()
            T0, T1 = WSHAPES[key]
            pieces = [(bd * 128, sc * slw) for bd in range(T0 // 128)
                      for sc in range(nslc)]
            nch = len(pieces)
            for ch in range(min(8, nch)):
                r_, c_ = pieces[ch]
                dma(kc, "scalar", kc.ys(f"b{ch % 8}"), wstage[:, ch % 8, 0:slw],
                    src[r_:r_ + 128, c_:c_ + slw])
            for ch in range(nch):
                gw(kc, "scalar", kc.ys(f"b{ch % 8}"))
                sop(kc, lambda Sc, ch=ch, slw=slw: Sc.activation(
                    wstage2[:, 0, 0:slw], wstage[:, ch % 8, 0:slw], AF.Abs,
                    accum_out=chs[:, ch:ch + 1]))
                if ch + 8 < nch:
                    r_, c_ = pieces[ch + 8]
                    dma(kc, "scalar", kc.ys(f"b{ch % 8}"), wstage[:, ch % 8, 0:slw],
                        src[r_:r_ + 128, c_:c_ + slw])
            t_s = kc.s_s.v
            gw(kc, "vector", kc.s_s, t_s)
            tgv = vop(kc, lambda V, nch=nch: V.tensor_scalar(
                grdt[:, 0:nch], chs[:, 0:nch], 1024.0, MAGIC, ALU.mult, ALU.add))
            gw(kc, "scalar", kc.s_v, tgv)
            tgs = sop(kc, lambda Sc, nch=nch: Sc.activation(
                grdt[:, 0:nch], grdt[:, 0:nch], AF.Identity, bias=kc.cvals[:, 2:3]), sep=True)
            gw(kc, "vector", kc.s_s, tgs)
            vop(kc, lambda V, nch=nch: V.tensor_scalar(
                grdt[:, 0:nch], grdt[:, 0:nch], 1.0 / 1024.0, None, ALU.mult), sep=True)
            vop(kc, lambda V, nch=nch: V.tensor_sub(
                chs[:, 0:nch], chs[:, 0:nch], grdt[:, 0:nch]), sep=True)
            vop(kc, lambda V, nch=nch: V.tensor_reduce(
                kc.stats[:, 16:17], grdt[:, 0:nch], AX.X, ALU.add), sep=True)
            vop(kc, lambda V, nch=nch: V.tensor_reduce(
                kc.stats[:, 17:18], chs[:, 0:nch], AX.X, ALU.add), sep=True)
            tgv = vop(kc, lambda V: V.tensor_scalar(
                kc.stats[:, 18:19], kc.stats[:, 16:17], 1.0 / 16.0, MAGIC,
                ALU.mult, ALU.add), sep=True)
            gw(kc, "scalar", kc.s_v, tgv)
            tgs = sop(kc, lambda Sc: Sc.activation(
                kc.stats[:, 18:19], kc.stats[:, 18:19], AF.Identity,
                bias=kc.cvals[:, 2:3]), sep=True)
            gw(kc, "vector", kc.s_s, tgs)
            vop(kc, lambda V: V.tensor_scalar(
                kc.stats[:, 18:19], kc.stats[:, 18:19], 16.0, None, ALU.mult), sep=True)
            vop(kc, lambda V: V.tensor_sub(
                kc.stats[:, 19:20], kc.stats[:, 16:17], kc.stats[:, 18:19]), sep=True)
            tgv = vop(kc, lambda V: V.tensor_scalar(
                kc.stats[:, 20:21], kc.stats[:, 19:20], 16.0, MAGIC,
                ALU.mult, ALU.add), sep=True)
            gw(kc, "scalar", kc.s_v, tgv)
            tgs = sop(kc, lambda Sc: Sc.activation(
                kc.stats[:, 20:21], kc.stats[:, 20:21], AF.Identity,
                bias=kc.cvals[:, 2:3]), sep=True)
            gw(kc, "vector", kc.s_s, tgs)
            vop(kc, lambda V: V.tensor_scalar(
                kc.stats[:, 20:21], kc.stats[:, 20:21], 1.0 / 16.0, None, ALU.mult), sep=True)
            vop(kc, lambda V: V.tensor_sub(
                kc.stats[:, 21:22], kc.stats[:, 19:20], kc.stats[:, 20:21]), sep=True)
            vop(kc, lambda V: V.tensor_copy(kc.cprb[:, 3:4], kc.stats[:, 17:18]), sep=True)
            vop(kc, lambda V: V.tensor_copy(kc.stats[:, 22:23], kc.cprb[:, 3:4]), sep=True)
            vop(kc, lambda V: V.tensor_sub(
                kc.stats[:, 23:24], kc.stats[:, 17:18], kc.stats[:, 22:23]), sep=True)
            vop(kc, lambda V: V.tensor_copy(kc.cprb[:, 0:1], kc.stats[:, 18:19]), sep=True)
            vop(kc, lambda V: V.tensor_copy(kc.cprb[:, 1:2], kc.stats[:, 20:21]), sep=True)
            vop(kc, lambda V: V.tensor_copy(kc.cprb[:, 2:3], kc.stats[:, 21:22]), sep=True)
            tvp = vop(kc, lambda V: V.tensor_copy(kc.cprb[:, 4:5], kc.stats[:, 23:24]), sep=True)
            b = 6 + (kc.tp_flip % 2)
            kc.tp_flip += 1
            bank_wait(kc, b)
            gw(kc, "tensor", kc.s_v, tvp)
            t_t = top(kc, lambda T, b=b: T.matmul(
                kc.ps[b][0:1, 0:5], kc.ones_bf[:, :], kc.cprb[:, 0:5],
                start=True, stop=True))
            gw(kc, "vector", kc.s_t, t_t)
            tv = vop(kc, lambda V, b=b: V.tensor_copy(kc.kmaxb[0:1, 8:13], kc.ps[b][0:1, 0:5]))
            bank_done(kc, b, kc.s_v, tv)
            vop(kc, lambda V: V.tensor_add(kc.kmaxb[0:1, 0:1], kc.kmaxb[0:1, 12:13],
                                           kc.kmaxb[0:1, 10:11]), sep=True)
            vop(kc, lambda V: V.tensor_add(kc.kmaxb[0:1, 0:1], kc.kmaxb[0:1, 0:1],
                                           kc.kmaxb[0:1, 11:12]), sep=True)
            vop(kc, lambda V: V.tensor_add(kc.kmaxb[0:1, 0:1], kc.kmaxb[0:1, 0:1],
                                           kc.kmaxb[0:1, 9:10]), sep=True)
            vop(kc, lambda V: V.tensor_add(kc.kmaxb[0:1, 0:1], kc.kmaxb[0:1, 0:1],
                                           kc.kmaxb[0:1, 8:9]), sep=True)
            vop(kc, lambda V, n=T0 * T1: V.tensor_scalar(
                kc.kmaxb[0:1, 1:2], kc.kmaxb[0:1, 0:1], float(np.float32(1.0 / n)),
                1e-5, ALU.mult, ALU.max), sep=True)
            vop(kc, lambda V: V.reciprocal(kc.kmaxb[0:1, 2:3], kc.kmaxb[0:1, 1:2]), sep=True)
            vop(kc, lambda V: V.tensor_scalar(kc.kmaxb[0:1, 3:4], kc.kmaxb[0:1, 1:2],
                                              4097.0, None, ALU.mult), sep=True)
            vop(kc, lambda V: V.tensor_sub(kc.kmaxb[0:1, 4:5], kc.kmaxb[0:1, 3:4],
                                           kc.kmaxb[0:1, 1:2]), sep=True)
            vop(kc, lambda V: V.tensor_sub(kc.kmaxb[0:1, 3:4], kc.kmaxb[0:1, 3:4],
                                           kc.kmaxb[0:1, 4:5]), sep=True)
            vop(kc, lambda V: V.tensor_sub(kc.kmaxb[0:1, 4:5], kc.kmaxb[0:1, 1:2],
                                           kc.kmaxb[0:1, 3:4]), sep=True)
            vop(kc, lambda V: V.tensor_scalar(kc.kmaxb[0:1, 5:6], kc.kmaxb[0:1, 2:3],
                                              4097.0, None, ALU.mult), sep=True)
            vop(kc, lambda V: V.tensor_sub(kc.kmaxb[0:1, 6:7], kc.kmaxb[0:1, 5:6],
                                           kc.kmaxb[0:1, 2:3]), sep=True)
            vop(kc, lambda V: V.tensor_sub(kc.kmaxb[0:1, 5:6], kc.kmaxb[0:1, 5:6],
                                           kc.kmaxb[0:1, 6:7]), sep=True)
            vop(kc, lambda V: V.tensor_sub(kc.kmaxb[0:1, 6:7], kc.kmaxb[0:1, 2:3],
                                           kc.kmaxb[0:1, 5:6]), sep=True)
            vop(kc, lambda V: V.tensor_mul(kc.kmaxb[0:1, 7:8], kc.kmaxb[0:1, 3:4],
                                           kc.kmaxb[0:1, 5:6]), sep=True)
            vop(kc, lambda V: V.tensor_scalar(kc.kmaxb[0:1, 7:8], kc.kmaxb[0:1, 7:8],
                                              -1.0, 1.0, ALU.mult, ALU.add), sep=True)
            vop(kc, lambda V: V.tensor_mul(kc.kmaxb[0:1, 13:14], kc.kmaxb[0:1, 3:4],
                                           kc.kmaxb[0:1, 6:7]), sep=True)
            vop(kc, lambda V: V.tensor_sub(kc.kmaxb[0:1, 7:8], kc.kmaxb[0:1, 7:8],
                                           kc.kmaxb[0:1, 13:14]), sep=True)
            vop(kc, lambda V: V.tensor_mul(kc.kmaxb[0:1, 13:14], kc.kmaxb[0:1, 4:5],
                                           kc.kmaxb[0:1, 5:6]), sep=True)
            vop(kc, lambda V: V.tensor_sub(kc.kmaxb[0:1, 7:8], kc.kmaxb[0:1, 7:8],
                                           kc.kmaxb[0:1, 13:14]), sep=True)
            vop(kc, lambda V: V.tensor_mul(kc.kmaxb[0:1, 13:14], kc.kmaxb[0:1, 4:5],
                                           kc.kmaxb[0:1, 6:7]), sep=True)
            vop(kc, lambda V: V.tensor_sub(kc.kmaxb[0:1, 7:8], kc.kmaxb[0:1, 7:8],
                                           kc.kmaxb[0:1, 13:14]), sep=True)
            vop(kc, lambda V: V.tensor_mul(kc.kmaxb[0:1, 7:8], kc.kmaxb[0:1, 7:8],
                                           kc.kmaxb[0:1, 2:3]), sep=True)
            vop(kc, lambda V: V.tensor_add(kc.kmaxb[0:1, 2:3], kc.kmaxb[0:1, 2:3],
                                           kc.kmaxb[0:1, 7:8]), sep=True)
            tvb = kc.s_v.v
            gw(kc, "scalar", kc.s_v, tvb)
            dma(kc, "scalar", kc.sd_st, bass.AP(kc.scrM, 0, [[1, 1], [1, 1]]),
                kc.kmaxb[0:1, 1:2])
            dma(kc, "scalar", kc.sd_st, bass.AP(kc.scr128, 0, [[1, 1], [1, 1]]),
                kc.kmaxb[0:1, 2:3])
            gw(kc, "scalar", kc.sd_st)
            dma(kc, "scalar", kc.sd_st, kc.stats[:, mwcol:mwcol + 1],
                bass.AP(kc.scrM, 0, [[0, 128], [1, 1]]))
            dma(kc, "scalar", kc.sd_st, kc.stats[:, swcol:swcol + 1],
                bass.AP(kc.scr128, 0, [[0, 128], [1, 1]]))
            gw(kc, "scalar", kc.sd_st)
            gw(kc, "vector", kc.sd_st)
            if getattr(kc, "debug2", False):
                dma(kc, "scalar", kc.sd_out,
                    kc.dbg["dbg_kmx"][0:16, (0 if key.endswith("up") else 128):
                                      (128 if key.endswith("up") else 256)],
                    kc.kmaxb[0:16, :])
                dma(kc, "scalar", kc.sd_out,
                    kc.dbg["dbg_acc"][:, (0 if key.endswith("up") else 4):
                                      (4 if key.endswith("up") else 8)],
                    kc.stats[:, 16:20])
            for ch in range(min(8, nch)):
                r_, c_ = pieces[ch]
                dma(kc, "scalar", kc.ys(f"b{ch % 8}"), wstage[:, ch % 8, 0:slw],
                    src[r_:r_ + 128, c_:c_ + slw])
            prev_tr = [None, None]
            prev_tv = [None, None]
            for ch in range(nch):
                r_, c_ = pieces[ch]
                pb = ch % 2
                gw(kc, "vector", kc.ys(f"b{ch % 8}"))
                if prev_tr[pb] is not None:
                    gw(kc, "vector", kc.s_s, prev_tr[pb])
                if ch >= 2:
                    gw(kc, "vector", kc.ys(f"bq{pb}"))
                tm = vop(kc, lambda V, ch=ch, slw=slw, c=swcol, pb=pb: V.tensor_scalar(
                    wstage2[:, pb, 0:slw], wstage[:, ch % 8, 0:slw], kc.stats[:, c:c + 1],
                    MAGIC, ALU.mult, ALU.add))
                gw(kc, "scalar", kc.s_v, tm)
                t_r = sop(kc, lambda Sc, slw=slw, pb=pb: Sc.activation(
                    wstage2[:, pb, 0:slw], wstage2[:, pb, 0:slw], AF.Identity,
                    bias=kc.cvals[:, 2:3]))
                prev_tr[pb] = t_r
                gw(kc, "vector", kc.s_s, t_r)
                tv = vop(kc, lambda V, slw=slw, pb=pb: V.tensor_scalar(
                    wqb[:, pb, 0:slw], wstage2[:, pb, 0:slw], -1.0, 1.0, ALU.max, ALU.min))
                prev_tv[pb] = tv
                gw(kc, "scalar", kc.s_v, tv)
                dma(kc, "scalar", kc.ys(f"bq{pb}"), qdst[r_:r_ + 128, c_:c_ + slw],
                    wqb[:, pb, 0:slw])
                if ch + 8 < nch:
                    r_, c_ = pieces[ch + 8]
                    dma(kc, "scalar", kc.ys(f"b{ch % 8}"), wstage[:, ch % 8, 0:slw],
                        src[r_:r_ + 128, c_:c_ + slw])


def emit_bitnet_ffn(kc):
    nc = kc.nc
    with contextlib.ExitStack() as actx:
        ee = actx.enter_context
        featbuf = ee(nc.sbuf_tensor("featbufB", [128, 16, 1024], BF16))
        xn32 = ee(nc.sbuf_tensor("xn32", [128, D], F32))
        t32 = ee(nc.sbuf_tensor("t32", [128, D], F32))
        wnb = ee(nc.sbuf_tensor("wnb", [128, D], F32))
        rbuf = ee(nc.sbuf_tensor("rbuf", [128, 256], F32))
        tbuf = ee(nc.sbuf_tensor("tbuf", [128, 256], F32))
        hbuf = ee(nc.sbuf_tensor("hbuf", [128, 2, 256], BF16))
        phase_barrier(kc)

        t_wn = dma(kc, "scalar", kc.sd_st, wnb[:, :],
                   bass.AP(kc.W["L0_ffn_norm_weight"], 0, [[0, 128], [1, D]]))
        vop(kc, lambda V: V.memset(kc.pvec3[:, :], 0.0))
        txq_prev = None
        for tt in range(CT):
            xv = emit_ln(kc, tt, xn32[:, :], rms_col=tt)
            gw(kc, "vector", kc.sd_st)
            vop(kc, lambda V: V.tensor_mul(xn32[:, :], xn32[:, :], wnb[:, :]), sep=True)
            tv = vop(kc, lambda V: V.tensor_reduce(
                kc.stats[:, 16:17], xn32[:, :], AX.X, ALU.max, apply_absolute_value=True),
                sep=True)
            vop(kc, lambda V, tt=tt: V.tensor_mul(kc.stats[:, 17:18],
                                                  kc.pvec0[:, tt:tt + 1], kc.stats[:, 16:17]),
                sep=True)
            vop(kc, lambda V: V.tensor_scalar(kc.stats[:, 17:18], kc.stats[:, 17:18],
                                              1e-5, None, ALU.max), sep=True)
            vop(kc, lambda V: V.reciprocal(kc.stats[:, 18:19], kc.stats[:, 17:18]), sep=True)
            vop(kc, lambda V, tt=tt: V.tensor_mul(kc.stats[:, 19:20],
                                                  kc.stats[:, 18:19], kc.pvec0[:, tt:tt + 1]),
                sep=True)
            vop(kc, lambda V: V.tensor_scalar(kc.stats[:, 19:20], kc.stats[:, 19:20],
                                              127.0, None, ALU.mult), sep=True)
            vop(kc, lambda V: V.tensor_scalar(kc.stats[:, 20:21], kc.stats[:, 17:18],
                                              1.0 / 127.0, None, ALU.mult))
            vop(kc, lambda V: V.tensor_mul(kc.stats[:, 20:21], kc.stats[:, 20:21],
                                           kc.stats[:, 10:11]), sep=True)
            vop(kc, lambda V: V.tensor_mul(kc.stats[:, 21:22], kc.stats[:, 20:21],
                                           kc.stats[:, 20:21]), sep=True)
            vop(kc, lambda V, tt=tt: V.tensor_mul(kc.pvec2[:, tt:tt + 1],
                                                  kc.stats[:, 21:22], kc.stats[:, 20:21]),
                sep=True)
            if txq_prev is not None:
                gw(kc, "vector", kc.s_s, txq_prev)
            tq = vop(kc, lambda V: V.tensor_scalar(t32[:, :], xn32[:, :],
                                                   kc.stats[:, 19:20], MAGIC,
                                                   ALU.mult, ALU.add))
            gw(kc, "scalar", kc.s_v, tq)
            buf_wait(kc, "scalar", f"xn{tt % 2}")
            txq = sop(kc, lambda Sc, tt=tt: Sc.activation(
                kc.xn_blk[:, tt % 2, :], t32[:, :], AF.Identity, bias=kc.cvals[:, 2:3]))
            txq_prev = txq
            if getattr(kc, "debug2", False) and tt == 0:
                gw(kc, "scalar", kc.s_s, txq)
                dma(kc, "scalar", kc.sd_out, kc.dbg["dbg_xq"][:, :], kc.xn_blk[:, 0, :])
                dma(kc, "scalar", kc.sd_out, kc.dbg["dbg_sc"][:, :], kc.stats[:, :])
            gw(kc, "tensor", kc.s_s, txq)
            emit_t2f(kc, (lambda kcc, tt=tt: kc.xn_blk[:, tt % 2, kcc * 128:(kcc + 1) * 128]),
                     tt * 128, 16, featbuf)
            buf_done(kc, f"xn{tt % 2}", kc.s_t, kc.s_t.v)
        featbuf_ready = kc.s_v.v
        gw(kc, "tensor", kc.s_v, featbuf_ready)
        gw(kc, "sync", kc.ys("bq0"))
        gw(kc, "sync", kc.ys("bq1"))
        prev_tb = None
        for fs in range(22):
            pp = fs % 2
            gbase, ubase = pp * 4096, pp * 4096
            buf_wait(kc, "sync", f"a0h{pp}")
            t_g = slab_load(kc, kc.arena[0], gbase, 256, kc.gu_q.ap(), fs * 256, 256, 16,
                            sem=kc.ys(f"a0h{pp}"), f32=False)
            buf_wait(kc, "sync", f"a1h{pp}")
            t_u = slab_load(kc, kc.arena[1], ubase, 256, kc.gu_q.ap(),
                            F0 + fs * 256, 256, 16, sem=kc.ys(f"a1h{pp}"), f32=False)
            gw(kc, "tensor", *t_g)
            gw(kc, "tensor", *t_u)
            t_last = None
            for tt in range(CT):
                bank_wait(kc, 4)
                for i in range(16):
                    top(kc, lambda T, i=i, tt=tt, gbase=gbase: T.matmul(
                        kc.ps[4][:, 0:256], featbuf[:, i, tt * 128:(tt + 1) * 128],
                        kc.arena[0][:, gbase + i * 256:gbase + (i + 1) * 256],
                        start=(i == 0), stop=(i == 15)), bump=(i == 15))
                bank_wait(kc, 5)
                for i in range(16):
                    top(kc, lambda T, i=i, tt=tt, ubase=ubase: T.matmul(
                        kc.ps[5][:, 0:256], featbuf[:, i, tt * 128:(tt + 1) * 128],
                        kc.arena[1][:, ubase + i * 256:ubase + (i + 1) * 256],
                        start=(i == 0), stop=(i == 15)), bump=(i == 15))
                t_last = kc.s_t.v
                gw(kc, "scalar", kc.s_t, t_last)
                if prev_tb is not None:
                    gw(kc, "scalar", kc.s_v, prev_tb)
                t_re = sop(kc, lambda Sc: Sc.activation(rbuf[:, :], kc.ps[4][:, 0:256],
                                                        AF.Relu))
                gw(kc, "vector", kc.s_s, t_re)
                prev_tb = vop(kc, lambda V: V.scalar_tensor_tensor(
                    tbuf[:, :], kc.ps[4][:, 0:256], 1.0, rbuf[:, :], ALU.mult, ALU.mult))
                buf_wait(kc, "vector", f"hb{tt % 2}")
                tv = vop(kc, lambda V, tt=tt: V.scalar_tensor_tensor(
                    hbuf[:, tt % 2, :], kc.ps[5][:, 0:256], 1.0, tbuf[:, :],
                    ALU.mult, ALU.mult), sep=True)
                bank_done(kc, 4, kc.s_v, tv)
                bank_done(kc, 5, kc.s_v, tv)
                vop(kc, lambda V, tt=tt: V.tensor_reduce(
                    kc.stats[:, 16:17], hbuf[:, tt % 2, :], AX.X, ALU.max,
                    apply_absolute_value=True), sep=True)
                vop(kc, lambda V, tt=tt: V.tensor_max(
                    kc.pvec3[:, tt:tt + 1], kc.pvec3[:, tt:tt + 1], kc.stats[:, 16:17]),
                    sep=True)
                gw(kc, "scalar", kc.s_v, kc.s_v.v)
                dma(kc, "scalar", kc.ys("h"),
                    kc.h_dram[tt * 128:(tt + 1) * 128, fs * 256:(fs + 1) * 256],
                    hbuf[:, tt % 2, :])
                buf_done(kc, f"hb{tt % 2}", kc.ys("h"), None)
            buf_done(kc, f"a0h{pp}", kc.s_t, t_last)
            buf_done(kc, f"a1h{pp}", kc.s_t, t_last)
        if getattr(kc, "debug2", False):
            gw(kc, "gpsimd", kc.ys("h"))
            dma(kc, "gpsimd", kc.sd_out, kc.dbg["dbg_h"][:, :], kc.h_dram[0:128, :])
            dma(kc, "gpsimd", kc.sd_out, kc.dbg["dbg_guq"][:, :], kc.gu_q[:, :])
            dma(kc, "gpsimd", kc.sd_out, kc.dbg["dbg_wdq"][:, :], kc.wd_q[:, :])
            gw(kc, "gpsimd", kc.s_v, kc.s_v.v)
            dma(kc, "gpsimd", kc.sd_out, kc.dbg["dbg_pv"][:, :], kc.pvec2[:, :])
            dma(kc, "gpsimd", kc.sd_out, kc.dbg["dbg_pv3"][:, :], kc.pvec3[:, :])

    with contextlib.ExitStack() as actx:
        ee = actx.enter_context
        hqT = ee(nc.sbuf_tensor("hqT", [128, 44, 512], BF16))
        t32h = ee(nc.sbuf_tensor("t32h", [128, F0], F32))
        hb = ee(nc.sbuf_tensor("hb", [128, F0], BF16))
        phase_barrier(kc)

        prev_tp = None
        prev_tq = None
        for halfb in range(2):
            for ttl in range(4):
                tt = halfb * 4 + ttl
                if prev_tp is not None:
                    gw(kc, "scalar", kc.s_t, prev_tp)
                dma(kc, "scalar", kc.ys("h"), hb[:, :],
                    kc.h_dram[tt * 128:(tt + 1) * 128, :])
                vop(kc, lambda V, tt=tt: V.tensor_mul(
                    kc.stats[:, 16:17], kc.pvec2[:, tt:tt + 1], kc.pvec3[:, tt:tt + 1]))
                vop(kc, lambda V: V.tensor_scalar(kc.stats[:, 16:17], kc.stats[:, 16:17],
                                                  1e-5, None, ALU.max), sep=True)
                vop(kc, lambda V: V.reciprocal(kc.stats[:, 17:18], kc.stats[:, 16:17]), sep=True)
                vop(kc, lambda V, tt=tt: V.tensor_mul(kc.stats[:, 18:19],
                                                      kc.stats[:, 17:18],
                                                      kc.pvec2[:, tt:tt + 1]), sep=True)
                vop(kc, lambda V: V.tensor_scalar(kc.stats[:, 18:19], kc.stats[:, 18:19],
                                                  127.0, None, ALU.mult), sep=True)
                vop(kc, lambda V: V.tensor_scalar(kc.stats[:, 19:20], kc.stats[:, 16:17],
                                                  1.0 / 127.0, None, ALU.mult))
                vop(kc, lambda V, tt=tt: V.tensor_mul(kc.pvec4[:, tt:tt + 1],
                                                      kc.stats[:, 19:20],
                                                      kc.stats[:, 11:12]), sep=True)
                gw(kc, "vector", kc.ys("h"))
                if prev_tq is not None:
                    gw(kc, "vector", kc.s_s, prev_tq)
                tq = vop(kc, lambda V: V.tensor_scalar(
                    t32h[:, :], hb[:, :], kc.stats[:, 18:19], MAGIC, ALU.mult, ALU.add))
                gw(kc, "scalar", kc.s_v, tq)
                thq = sop(kc, lambda Sc: Sc.activation(hb[:, :], t32h[:, :],
                                                       AF.Identity, bias=kc.cvals[:, 2:3]))
                prev_tq = thq
                for g in range(11):
                    bk = 6 + (kc.tp_flip % 2)
                    kc.tp_flip += 1
                    bank_wait(kc, bk)
                    if g == 0:
                        gw(kc, "tensor", kc.s_s, thq)
                    for j in range(4):
                        kcc = g * 4 + j
                        top(kc, lambda T, bk=bk, j=j, kcc=kcc: T.matmul(
                            kc.ps[bk][:, j * 128:(j + 1) * 128],
                            hb[:, kcc * 128:(kcc + 1) * 128], kc.ident[:, :],
                            start=True, stop=True), bump=(j == 3))
                    gw(kc, "vector", kc.s_t, kc.s_t.v)
                    prev_tp = kc.s_t.v
                    tv = vop(kc, lambda V, bk=bk, g=g, ttl=ttl: V.tensor_copy(
                        bass.AP(hqT, g * 4 * 512 + ttl * 128,
                                [[44 * 512, 128], [512, 4], [1, 128]]),
                        bass.AP(kc.ps[bk], 0, [[512, 128], [128, 4], [1, 128]])))
                    bank_done(kc, bk, kc.s_v, tv)
            hq_ready = kc.s_v.v
            for sl in range(8):
                pp = sl % 2
                buf_wait(kc, "sync", f"a{pp}")
                t_w = slab_load(kc, kc.arena[pp], 0, 256, kc.wd_q.ap(), sl * 256, 256, 44,
                                sem=kc.ys(f"a{pp}"), f32=False)
                if getattr(kc, "debug2", False) and halfb == 1 and sl == 4:
                    gw(kc, "gpsimd", kc.ys(f"a{pp}"))
                    dma(kc, "gpsimd", kc.sd_out, kc.dbg["dbg_slab"][:, :],
                        kc.arena[pp][:, 29 * 256:30 * 256])
                    gw(kc, "gpsimd", kc.s_v, hq_ready)
                    dma(kc, "gpsimd", kc.sd_out, kc.dbg["dbg_hqt"][:, :],
                        bass.AP(hqT, 29 * 512, [[44 * 512, 128], [1, 512]]))
                gw(kc, "tensor", *t_w)
                gw(kc, "tensor", kc.s_v, hq_ready)
                t_last = None
                for ttl in range(4):
                    tt = halfb * 4 + ttl
                    b = ttl % 4
                    bank_wait(kc, b)
                    for i in range(44):
                        top(kc, lambda T, b=b, i=i, ttl=ttl, pp=pp: T.matmul(
                            kc.ps[b][:, 0:256], hqT[:, i, ttl * 128:(ttl + 1) * 128],
                            kc.arena[pp][:, i * 256:(i + 1) * 256],
                            start=(i == 0), stop=(i == 43)), bump=(i == 43))
                    t_mm = kc.s_t.v
                    t_last = t_mm
                    gw(kc, "vector", kc.s_t, t_mm)
                    tv = vop(kc, lambda V, b=b, tt=tt, sl=sl: V.scalar_tensor_tensor(
                        kc.x_res[:, tt, sl * 256:(sl + 1) * 256], kc.ps[b][:, 0:256],
                        kc.pvec4[:, tt:tt + 1],
                        kc.x_res[:, tt, sl * 256:(sl + 1) * 256], ALU.mult, ALU.add))
                    bank_done(kc, b, kc.s_v, tv)
                buf_done(kc, f"a{pp}", kc.s_t, t_last)
        kc.x_ready_v = kc.s_v.v


def emit_moe(kc):
    nc = kc.nc
    with contextlib.ExitStack() as actx:
        ee = actx.enter_context
        featbuf = ee(nc.sbuf_tensor("featbufD", [128, 16, 1024], BF16))
        hT = ee(nc.sbuf_tensor("hT", [128, 11, 1024], BF16))
        gwT = ee(nc.sbuf_tensor("gwT", [128, 16, 8], BF16))
        sgwT = ee(nc.sbuf_tensor("sgwT", [128, 16, 1], BF16))
        cmb = ee(nc.sbuf_tensor("cmb", [128, CT, NEXP], F32))
        slb = ee(nc.sbuf_tensor("slb", [128, 512], BF16))
        gwsb = ee(nc.sbuf_tensor("gwsb", [8, D], BF16))
        sgwsb = ee(nc.sbuf_tensor("sgwsb", [1, D], BF16))
        stgM = ee(nc.sbuf_tensor("stgM", [128, 8, 512], F32))
        phase_barrier(kc)
        kc.stgT, kc.stg_n = stgM, 8
        kc.stg_dep = [None] * 16
        kc.stg_flip = 0

        for tt in range(CT):
            buf_wait(kc, "vector", f"xn{tt % 2}")
            xv = emit_ln(kc, tt, kc.xn_blk[:, tt % 2, :])
            emit_t2f(kc, (lambda kcc, tt=tt: kc.xn_blk[:, tt % 2, kcc * 128:(kcc + 1) * 128]),
                     tt * 128, 16, featbuf, src_ready=(kc.s_v, xv))
            buf_done(kc, f"xn{tt % 2}", kc.s_t, kc.s_t.v)

        featbuf_ready = kc.s_v.v
        gw(kc, "tensor", kc.s_v, featbuf_ready)
        kc._slb_prev = None
        dma(kc, "scalar", kc.sd_st,
            bass.AP(stgM, 0, [[8 * 512, 8], [512, 4], [1, 512]]),
            kc.W["L1_ffn_gate_w"][:, :])
        dma(kc, "scalar", kc.sd_st,
            bass.AP(stgM, 4 * 512, [[8 * 512, 1], [512, 4], [1, 512]]),
            kc.W["L1_ffn_shared_gate_w"][:, :])
        gw(kc, "vector", kc.sd_st)
        vop(kc, lambda V: V.tensor_copy(
            bass.AP(gwsb, 0, [[D, 8], [512, 4], [1, 512]]),
            bass.AP(stgM, 0, [[8 * 512, 8], [512, 4], [1, 512]])))
        tgc = vop(kc, lambda V: V.tensor_copy(
            bass.AP(sgwsb, 0, [[D, 1], [512, 4], [1, 512]]),
            bass.AP(stgM, 4 * 512, [[8 * 512, 1], [512, 4], [1, 512]])))
        gw(kc, "tensor", kc.s_v, tgc)
        gw(kc, "scalar", kc.s_v, tgc)
        tss0 = sop(kc, lambda Sc: Sc.activation(kc.jsep[:, 0:1], kc.cvals[:, 0:1],
                                                AF.Identity))
        kc.stg_dep = [tss0] * 16
        for i in range(16):
            bk = 6 + (kc.tp_flip % 2)
            kc.tp_flip += 1
            bank_wait(kc, bk)
            top(kc, lambda T, bk=bk, i=i: T.matmul(
                kc.ps[bk][:, 0:8], gwsb[0:8, i * 128:(i + 1) * 128],
                kc.ident[0:8, 0:8], start=True, stop=True), bump=False)
            top(kc, lambda T, bk=bk, i=i: T.matmul(
                kc.ps[bk][:, 8:9], sgwsb[0:1, i * 128:(i + 1) * 128],
                kc.ident[0:1, 0:1], start=True, stop=True))
            gw(kc, "vector", kc.s_t, kc.s_t.v)
            vop(kc, lambda V, bk=bk, i=i: V.tensor_copy(gwT[:, i, :], kc.ps[bk][:, 0:8]))
            tv = vop(kc, lambda V, bk=bk, i=i: V.tensor_copy(sgwT[:, i, :], kc.ps[bk][:, 8:9]))
            bank_done(kc, bk, kc.s_v, tv)

        gw(kc, "tensor", kc.s_v, kc.s_v.v)
        prev_cmb = None
        for tt in range(CT):
            bank_wait(kc, 4)
            for i in range(16):
                top(kc, lambda T, i=i, tt=tt: T.matmul(
                    kc.ps[4][:, 0:8], featbuf[:, i, tt * 128:(tt + 1) * 128], gwT[:, i, :],
                    start=(i == 0), stop=(i == 15)), bump=(i == 15))
            bank_wait(kc, 5)
            for i in range(16):
                top(kc, lambda T, i=i, tt=tt: T.matmul(
                    kc.ps[5][:, 0:1], featbuf[:, i, tt * 128:(tt + 1) * 128], sgwT[:, i, :],
                    start=(i == 0), stop=(i == 15)), bump=(i == 15))
            t_mm = kc.s_t.v
            gw(kc, "vector", kc.s_t, t_mm)
            vop(kc, lambda V: V.tensor_reduce(kc.stats[:, 16:17], kc.ps[4][:, 0:8],
                                              AX.X, ALU.max))
            tv = vop(kc, lambda V: V.tensor_scalar(kc.stats[:, 16:17], kc.stats[:, 16:17],
                                                   -1.0, None, ALU.mult), sep=True)
            gw(kc, "scalar", kc.s_v, tv)
            if prev_cmb is not None:
                gw(kc, "scalar", kc.s_v, prev_cmb)
            t_e = sop(kc, lambda Sc: Sc.activation(kc.stats[:, 20:28], kc.ps[4][:, 0:8],
                                                   AF.Exp, bias=kc.stats[:, 16:17]))
            t_g = sop(kc, lambda Sc, tt=tt: Sc.activation(
                kc.pvec5[:, tt:tt + 1], kc.ps[5][:, 0:1], AF.Sigmoid))
            bank_done(kc, 4, kc.s_s, t_e)
            bank_done(kc, 5, kc.s_s, t_g)
            gw(kc, "vector", kc.s_s, t_g)
            vop(kc, lambda V: V.tensor_reduce(kc.stats[:, 16:17], kc.stats[:, 20:28],
                                              AX.X, ALU.add))
            vop(kc, lambda V: V.reciprocal(kc.stats[:, 17:18], kc.stats[:, 16:17]), sep=True)
            vop(kc, lambda V: V.tensor_scalar(kc.stats[:, 20:28], kc.stats[:, 20:28],
                                              kc.stats[:, 17:18], None, ALU.mult), sep=True)
            scr8 = kc.svec[:, 32:40]
            vop(kc, lambda V: V.tensor_reduce(kc.stats[:, 16:17], kc.stats[:, 20:28],
                                              AX.X, ALU.max), sep=True)
            vop(kc, lambda V, scr8=scr8: V.tensor_scalar(
                scr8, kc.stats[:, 20:28], kc.stats[:, 16:17], None, ALU.is_ge), sep=True)
            vop(kc, lambda V, scr8=scr8: V.scalar_tensor_tensor(
                scr8, scr8, -1e30, kc.stats[:, 20:28], ALU.mult, ALU.add), sep=True)
            vop(kc, lambda V, scr8=scr8: V.tensor_reduce(
                kc.stats[:, 18:19], scr8, AX.X, ALU.max), sep=True)
            vop(kc, lambda V: V.tensor_tensor(kc.stats[:, 18:19], kc.stats[:, 18:19],
                                              kc.stats[:, 16:17], ALU.min), sep=True)
            vop(kc, lambda V, scr8=scr8: V.tensor_scalar(
                scr8, kc.stats[:, 20:28], kc.stats[:, 18:19], None, ALU.is_ge), sep=True)
            prev_cmb = vop(kc, lambda V, tt=tt, scr8=scr8: V.tensor_mul(
                cmb[:, tt, :], kc.stats[:, 20:28], scr8), sep=True)

        def fc_slabs(e_idx, which, fs):
            key = "L1_ffn_expert_fc1" if which == 0 else "L1_ffn_expert_fc2"
            a = kc.arena[which]
            pp = fs % 2
            base = pp * 4096
            nm = f"a{which}h{pp}"
            buf_wait(kc, "scalar", nm)
            w_ = 256 if fs < 5 else 128
            t = slab_load(kc, a, base, 256, kc.W[key].ap(),
                          e_idx * FM + fs * 256, w_, 16, sem=kc.ys(nm))
            return nm, base, w_, t

        for e_idx in range(NEXP):
            pend = (fc_slabs(e_idx, 0, 0), fc_slabs(e_idx, 1, 0))
            for fs in range(6):
                nxt = ((fc_slabs(e_idx, 0, fs + 1), fc_slabs(e_idx, 1, fs + 1))
                       if fs + 1 < 6 else None)
                (nm1, base1, w_, t1), (nm2, base2, _, t2) = pend
                pend = nxt
                gw(kc, "tensor", *t1)
                gw(kc, "tensor", *t2)
                t_last = None
                for fcq in range(w_ // 128):
                    fcc = fs * 2 + fcq
                    for qc in range(2):
                        bank_wait(kc, 4)
                        for i in range(16):
                            top(kc, lambda T, i=i, qc=qc, base1=base1, fcq=fcq: T.matmul(
                                kc.ps[4][:, 0:512],
                                kc.arena[0][:, base1 + i * 256 + fcq * 128:
                                            base1 + i * 256 + (fcq + 1) * 128],
                                featbuf[:, i, qc * 512:(qc + 1) * 512],
                                start=(i == 0), stop=(i == 15)), bump=(i == 15))
                        t_m1 = kc.s_t.v
                        gw(kc, "scalar", kc.s_t, t_m1)
                        if kc._slb_prev is not None:
                            gw(kc, "scalar", kc.s_v, kc._slb_prev)
                        t_si = sop(kc, lambda Sc: Sc.activation(slb[:, :], kc.ps[4][:, 0:512],
                                                                AF.Silu))
                        bank_done(kc, 4, kc.s_s, t_si)
                        bank_wait(kc, 5)
                        for i in range(16):
                            top(kc, lambda T, i=i, qc=qc, base2=base2, fcq=fcq: T.matmul(
                                kc.ps[5][:, 0:512],
                                kc.arena[1][:, base2 + i * 256 + fcq * 128:
                                            base2 + i * 256 + (fcq + 1) * 128],
                                featbuf[:, i, qc * 512:(qc + 1) * 512],
                                start=(i == 0), stop=(i == 15)), bump=(i == 15))
                        t_m2 = kc.s_t.v
                        t_last = t_m2
                        gw(kc, "vector", kc.s_t, t_m2)
                        gw(kc, "vector", kc.s_s, t_si)
                        tv = vop(kc, lambda V, fcc=fcc, qc=qc: V.scalar_tensor_tensor(
                            hT[:, fcc, qc * 512:(qc + 1) * 512], kc.ps[5][:, 0:512],
                            1.0, slb[:, :], ALU.mult, ALU.mult))
                        kc._slb_prev = tv
                        bank_done(kc, 5, kc.s_v, tv)
                buf_done(kc, nm1, kc.s_t, t_last)
                buf_done(kc, nm2, kc.s_t, t_last)
            hT_ready = kc.s_v.v
            pw = kc.W["L1_ffn_expert_proj"].ap()

            def load_proj(sl, e_idx=e_idx, pw=pw):
                pp = sl % 2
                nm = f"aC{pp}"
                buf_wait(kc, "scalar", nm)
                t = slab_load(kc, kc.arena[pp], 8448, 256, pw,
                              e_idx * D + sl * 256, 256, 11, sem=kc.ys(nm))
                return nm, pp, t
            pendp = load_proj(0)
            for sl in range(8):
                nxtp = load_proj(sl + 1) if sl + 1 < 8 else None
                nm, pp, t_w = pendp
                pendp = nxtp
                gw(kc, "tensor", *t_w)
                gw(kc, "tensor", kc.s_v, hT_ready)
                t_last = None
                for tt in range(CT):
                    b = tt % 4
                    bank_wait(kc, b)
                    for i in range(11):
                        top(kc, lambda T, b=b, i=i, tt=tt, pp=pp: T.matmul(
                            kc.ps[b][:, 0:256], hT[:, i, tt * 128:(tt + 1) * 128],
                            kc.arena[pp][:, 8448 + i * 256:8448 + (i + 1) * 256],
                            start=(i == 0), stop=(i == 10)), bump=(i == 10))
                    t_mm = kc.s_t.v
                    t_last = t_mm
                    gw(kc, "vector", kc.s_t, t_mm)
                    tv = vop(kc, lambda V, b=b, tt=tt, sl=sl, e_idx=e_idx:
                             V.scalar_tensor_tensor(
                                 kc.x_res[:, tt, sl * 256:(sl + 1) * 256],
                                 kc.ps[b][:, 0:256], cmb[:, tt, e_idx:e_idx + 1],
                                 kc.x_res[:, tt, sl * 256:(sl + 1) * 256],
                                 ALU.mult, ALU.add))
                    bank_done(kc, b, kc.s_v, tv)
                buf_done(kc, nm, kc.s_t, t_last)

        def shared_fc_load(fbase, fs):
            w_ = 256 if fs < 5 else 128
            pp = fs % 2
            nm1, nm2 = f"a0h{pp}", f"a1h{pp}"
            buf_wait(kc, "scalar", nm1)
            t1 = slab_load(kc, kc.arena[0], pp * 4096, 256,
                           kc.W["L1_ffn_shared_fc1"].ap(), fbase + fs * 256, w_, 16,
                           sem=kc.ys(nm1))
            buf_wait(kc, "scalar", nm2)
            t2 = slab_load(kc, kc.arena[1], pp * 4096, 256,
                           kc.W["L1_ffn_shared_fc2"].ap(), fbase + fs * 256, w_, 16,
                           sem=kc.ys(nm2))
            return w_, pp, nm1, nm2, t1, t2

        for fb in range(4):
            fbase = fb * FM
            pend_s = shared_fc_load(fbase, 0)
            for fs in range(6):
                nxt_s = shared_fc_load(fbase, fs + 1) if fs + 1 < 6 else None
                w_, pp, nm1, nm2, t1, t2 = pend_s
                pend_s = nxt_s
                gw(kc, "tensor", *t1)
                gw(kc, "tensor", *t2)
                t_last = None
                for fcq in range(w_ // 128):
                    fcc = fs * 2 + fcq
                    for qc in range(2):
                        bank_wait(kc, 4)
                        for i in range(16):
                            top(kc, lambda T, i=i, qc=qc, pp=pp, fcq=fcq: T.matmul(
                                kc.ps[4][:, 0:512],
                                kc.arena[0][:, pp * 4096 + i * 256 + fcq * 128:
                                            pp * 4096 + i * 256 + (fcq + 1) * 128],
                                featbuf[:, i, qc * 512:(qc + 1) * 512],
                                start=(i == 0), stop=(i == 15)), bump=(i == 15))
                        t_m1 = kc.s_t.v
                        gw(kc, "scalar", kc.s_t, t_m1)
                        if kc._slb_prev is not None:
                            gw(kc, "scalar", kc.s_v, kc._slb_prev)
                        t_si = sop(kc, lambda Sc: Sc.activation(slb[:, :], kc.ps[4][:, 0:512],
                                                                AF.Silu))
                        bank_done(kc, 4, kc.s_s, t_si)
                        bank_wait(kc, 5)
                        for i in range(16):
                            top(kc, lambda T, i=i, qc=qc, pp=pp, fcq=fcq: T.matmul(
                                kc.ps[5][:, 0:512],
                                kc.arena[1][:, pp * 4096 + i * 256 + fcq * 128:
                                            pp * 4096 + i * 256 + (fcq + 1) * 128],
                                featbuf[:, i, qc * 512:(qc + 1) * 512],
                                start=(i == 0), stop=(i == 15)), bump=(i == 15))
                        t_m2 = kc.s_t.v
                        t_last = t_m2
                        gw(kc, "vector", kc.s_t, t_m2)
                        gw(kc, "vector", kc.s_s, t_si)
                        tv = vop(kc, lambda V, fcc=fcc, qc=qc: V.scalar_tensor_tensor(
                            hT[:, fcc, qc * 512:(qc + 1) * 512], kc.ps[5][:, 0:512],
                            1.0, slb[:, :], ALU.mult, ALU.mult))
                        kc._slb_prev = tv
                        bank_done(kc, 5, kc.s_v, tv)
                buf_done(kc, nm1, kc.s_t, t_last)
                buf_done(kc, nm2, kc.s_t, t_last)
            hT_ready = kc.s_v.v
            def load_sproj(sl, fbase=fbase):
                pp = sl % 2
                nm = f"aC{pp}"
                buf_wait(kc, "scalar", nm)
                t = slab_load(kc, kc.arena[pp], 8448, 256,
                              kc.W["L1_ffn_shared_proj"].ap(), sl * 256, 256, 11,
                              sem=kc.ys(nm), r0=fbase)
                return nm, pp, t
            pendq = load_sproj(0)
            for sl in range(8):
                nxtq = load_sproj(sl + 1) if sl + 1 < 8 else None
                nm, pp, t_w = pendq
                pendq = nxtq
                gw(kc, "tensor", *t_w)
                gw(kc, "tensor", kc.s_v, hT_ready)
                t_last = None
                for tt in range(CT):
                    b = tt % 4
                    bank_wait(kc, b)
                    for i in range(11):
                        top(kc, lambda T, b=b, i=i, tt=tt, pp=pp: T.matmul(
                            kc.ps[b][:, 0:256], hT[:, i, tt * 128:(tt + 1) * 128],
                            kc.arena[pp][:, 8448 + i * 256:8448 + (i + 1) * 256],
                            start=(i == 0), stop=(i == 10)), bump=(i == 10))
                    t_mm = kc.s_t.v
                    t_last = t_mm
                    gw(kc, "vector", kc.s_t, t_mm)
                    tv = vop(kc, lambda V, b=b, tt=tt, sl=sl: V.scalar_tensor_tensor(
                        kc.x_res[:, tt, sl * 256:(sl + 1) * 256], kc.ps[b][:, 0:256],
                        kc.pvec5[:, tt:tt + 1],
                        kc.x_res[:, tt, sl * 256:(sl + 1) * 256], ALU.mult, ALU.add))
                    bank_done(kc, b, kc.s_v, tv)
                buf_done(kc, nm, kc.s_t, t_last)
        kc.x_ready_v = kc.s_v.v


def build(debug=False):
    nc = bass.Bass("TRN2", target_bir_lowering=False)
    kc = KC()
    kc.nc = nc
    kc.tp_flip = 0
    kc.bankdep = {}
    kc.bufdep = {}
    kc.x_ready_v = 0

    kc.xin = nc.dram_tensor("xin", [QTOK, D], F32, kind="ExternalInput")
    kc.ident_in = nc.dram_tensor("ident_in", [128, 128], BF16, kind="ExternalInput")
    kc.mask_in = nc.dram_tensor("masktbl", [128, 16 * 512], BF16, kind="ExternalInput")
    kc.W = {k: nc.dram_tensor(k, list(sh), F32, kind="ExternalInput")
            for k, sh in WSHAPES.items()}
    kc.xout = nc.dram_tensor("xout", [QTOK, D], F32, kind="ExternalOutput")
    kc.dbg = {}
    if debug:
        for nm in ("dbg_x1", "dbg_x2", "dbg_x3"):
            kc.dbg[nm] = nc.dram_tensor(nm, [QTOK, D], F32, kind="ExternalOutput")
    kc.debug2 = bool(debug) and DEBUG2
    if kc.debug2:
        kc.dbg["dbg_fb"] = nc.dram_tensor("dbg_fb", [128, 16 * 1024], BF16, kind="ExternalOutput")
        kc.dbg["dbg_xn0"] = nc.dram_tensor("dbg_xn0", [128, D], BF16, kind="ExternalOutput")
        kc.dbg["dbg_st0"] = nc.dram_tensor("dbg_st0", [128, 32], F32, kind="ExternalOutput")
        kc.dbg["dbg_kvg"] = nc.dram_tensor("dbg_kvg", [4 * 1152, 256], BF16, kind="ExternalOutput")
        kc.dbg["dbg_qt"] = nc.dram_tensor("dbg_qt", [128, 8 * 1024], BF16, kind="ExternalOutput")
        kc.dbg["dbg_nmb"] = nc.dram_tensor("dbg_nmb", [128, 16], F32, kind="ExternalOutput")
        kc.dbg["dbg_kt"] = nc.dram_tensor("dbg_kt", [128, 4096], BF16, kind="ExternalOutput")
        kc.dbg["dbg_va"] = nc.dram_tensor("dbg_va", [128, 32 * 136], BF16, kind="ExternalOutput")
        kc.dbg["dbg_z"] = nc.dram_tensor("dbg_z", [128, 8 * 1024], BF16, kind="ExternalOutput")
        kc.dbg["dbg_xq"] = nc.dram_tensor("dbg_xq", [128, D], BF16, kind="ExternalOutput")
        kc.dbg["dbg_sc"] = nc.dram_tensor("dbg_sc", [128, 32], F32, kind="ExternalOutput")
        kc.dbg["dbg_h"] = nc.dram_tensor("dbg_h", [128, F0], BF16, kind="ExternalOutput")
        kc.dbg["dbg_slab"] = nc.dram_tensor("dbg_slab", [128, 256], BF16, kind="ExternalOutput")
        kc.dbg["dbg_hqt"] = nc.dram_tensor("dbg_hqt", [128, 512], BF16, kind="ExternalOutput")
        kc.dbg["dbg_guq"] = nc.dram_tensor("dbg_guq", [D, GU], BF16, kind="ExternalOutput")
        kc.dbg["dbg_wdq"] = nc.dram_tensor("dbg_wdq", [F0, D], BF16, kind="ExternalOutput")
        kc.dbg["dbg_pv"] = nc.dram_tensor("dbg_pv", [128, CT], F32, kind="ExternalOutput")
        kc.dbg["dbg_pv3"] = nc.dram_tensor("dbg_pv3", [128, CT], F32, kind="ExternalOutput")
        kc.dbg["dbg_kmx"] = nc.dram_tensor("dbg_kmx", [16, 256], F32, kind="ExternalOutput")
        kc.dbg["dbg_acc"] = nc.dram_tensor("dbg_acc", [128, 8], F32, kind="ExternalOutput")

    kc.gu_q = nc.dram_tensor("gu_q", [D, GU], BF16)
    kc.wd_q = nc.dram_tensor("wd_q", [F0, D], BF16)
    kc.h_dram = nc.dram_tensor("h_dram", [QTOK, F0], BF16)
    kc.kv0loc = nc.dram_tensor("kv0loc", [1152, 2 * Dh], BF16)
    kc.kv0g = nc.dram_tensor("kv0g", [4 * 1152, 2 * Dh], BF16)
    kc.kv1loc = nc.dram_tensor("kv1loc", [1152, 2 * D], BF16)
    kc.kv1g = nc.dram_tensor("kv1g", [4 * 1152, 2 * D], BF16)
    kc.scr128 = nc.dram_tensor("scr128", [16, 128], F32)
    kc.scrM = nc.dram_tensor("scrM", [1, 16], F32)

    stack = contextlib.ExitStack()
    with stack:
        e = stack.enter_context
        kc.x_res = e(nc.sbuf_tensor("x_res", [128, CT, D], F32))
        kc.arena = [e(nc.sbuf_tensor(f"arena{i}", [128, 11264], BF16)) for i in range(2)]
        kc.xn_blk = e(nc.sbuf_tensor("xn_blk", [128, 2, D], BF16))
        kc.ident = e(nc.sbuf_tensor("identt", [128, 128], BF16))
        kc.ones_bf = e(nc.sbuf_tensor("ones_bf", [128, 1], BF16))
        kc.sqs = e(nc.sbuf_tensor("sqs", [128, 512], BF16))
        kc.stats = e(nc.sbuf_tensor("stats", [128, 32], F32))
        kc.svec = e(nc.sbuf_tensor("svec", [128, 64], F32))
        kc.nmb = e(nc.sbuf_tensor("nmb", [128, 16], F32))
        kc.pvec0 = e(nc.sbuf_tensor("pvec0", [128, CT], F32))
        kc.pvec2 = e(nc.sbuf_tensor("pvec2", [128, CT], F32))
        kc.pvec3 = e(nc.sbuf_tensor("pvec3", [128, CT], F32))
        kc.pvec4 = e(nc.sbuf_tensor("pvec4", [128, CT], F32))
        kc.pvec5 = e(nc.sbuf_tensor("pvec5", [128, CT], F32))
        kc.kmaxb = e(nc.sbuf_tensor("kmaxb", [16, 128], F32))
        kc.cvals = e(nc.sbuf_tensor("cvals", [128, 4], F32))
        kc.jsep = e(nc.sbuf_tensor("jsep", [128, 4], F32))
        kc.ones_row = e(nc.sbuf_tensor("ones_row", [1, 128], BF16))
        kc.cprb = e(nc.sbuf_tensor("cprb", [128, 18], BF16))
        kc.ps = [e(nc.psum_tensor(f"psb{i}", [128, 512], F32)) for i in range(8)]
        kc.stg = e(nc.sbuf_tensor("stg", [128, 2, 512], F32))
        kc.sd_g = Sem(stack, nc, "sd_g")
        kc.sd_bulk = Sem(stack, nc, "sd_bulk")
        kc.sd_x = Sem(stack, nc, "sd_x")
        kc.sd_n = Sem(stack, nc, "sd_n")
        kc.sd_kv = Sem(stack, nc, "sd_kv")
        kc.sd_st = Sem(stack, nc, "sd_st")
        kc.sd_mk = Sem(stack, nc, "sd_mk")
        kc.sd_out = Sem(stack, nc, "sd_out")
        kc.sd_s = Sem(stack, nc, "sd_s")
        kc.sd_y = Sem(stack, nc, "sd_y")
        kc.s_cc = Sem(stack, nc, "s_cc")
        kc.s_t = Sem(stack, nc, "s_t", ordered=True)
        kc.s_v = Sem(stack, nc, "s_v", ordered=True)
        kc.s_s = Sem(stack, nc, "s_s", ordered=True)
        kc._ys = {}

        def ys(name):
            if name not in kc._ys:
                kc._ys[name] = Sem(stack, nc, "ys_" + name)
            return kc._ys[name]
        kc.ys = ys

        def barrier_sems():
            out = [kc.sd_x, kc.sd_n, kc.sd_kv, kc.sd_st, kc.sd_mk, kc.sd_out,
                   kc.sd_s, kc.sd_y]
            out.extend(kc._ys.values())
            return out
        kc.barrier_sems = barrier_sems
        kc.P = Prog(nc)
        kc.stgT = kc.stg
        kc.stg_n = 2
        kc.stg_dep = [None] * 16
        kc.stg_flip = 0

        dma(kc, "gpsimd", kc.sd_x, kc.x_res[:, :, :],
            kc.xin.ap().rearrange("(t p) d -> p t d", p=128))
        dma(kc, "gpsimd", kc.sd_x, kc.ident[:, :], kc.ident_in[:, :])
        gw(kc, "vector", kc.sd_x)
        vop(kc, lambda V: V.memset(kc.ones_bf[:, :], 1.0))
        vop(kc, lambda V: V.memset(kc.ones_row[:, :], 1.0))
        vop(kc, lambda V: V.memset(kc.cvals[:, 0:1], EPS))
        vop(kc, lambda V: V.memset(kc.cvals[:, 1:2], MAGIC))
        vop(kc, lambda V: V.memset(kc.cvals[:, 2:3], -MAGIC))
        gw(kc, "tensor", kc.sd_x)
        gw(kc, "scalar", kc.sd_x)

        emit_attention(kc, 0)
        if debug:
            gw(kc, "scalar", kc.s_v, kc.x_ready_v)
            dma(kc, "scalar", kc.sd_out,
                kc.dbg["dbg_x1"].ap().rearrange("(t p) d -> p t d", p=128),
                kc.x_res[:, :, :])
        emit_bitnet_prep(kc)
        emit_bitnet_ffn(kc)
        if debug:
            gw(kc, "scalar", kc.s_v, kc.x_ready_v)
            dma(kc, "scalar", kc.sd_out,
                kc.dbg["dbg_x2"].ap().rearrange("(t p) d -> p t d", p=128),
                kc.x_res[:, :, :])
        emit_attention(kc, 1)
        if debug:
            gw(kc, "scalar", kc.s_v, kc.x_ready_v)
            dma(kc, "scalar", kc.sd_out,
                kc.dbg["dbg_x3"].ap().rearrange("(t p) d -> p t d", p=128),
                kc.x_res[:, :, :])
        emit_moe(kc)

        gw(kc, "scalar", kc.s_v, kc.x_ready_v)
        dma(kc, "scalar", kc.sd_out,
            kc.xout.ap().rearrange("(t p) d -> p t d", p=128),
            kc.x_res[:, :, :])
        gw(kc, "scalar", kc.sd_out)
        kc.P.flush()
    return nc


def _mask_table(gr):
    m = np.zeros((128, 16, 512), np.float32)
    for i in range(16):
        for s in range(4):
            gq_rel = gr + 4 * s
            blk = m[:, i, s * 128:(s + 1) * 128]
            if i < gq_rel:
                pass
            elif i > gq_rel:
                blk[:, :] = NEG
            else:
                p = np.arange(128)[:, None]
                qf = np.arange(128)[None, :]
                blk[:, :] = np.where(p <= qf, 0.0, NEG)
    return m.reshape(128, 16 * 512).astype(ml_dtypes.bfloat16)


_NC_CACHE = {}


def kernel(**inputs):
    key = "main"
    if key not in _NC_CACHE:
        _NC_CACHE[key] = build(debug=DEBUG)
    nc = _NC_CACHE[key]

    x = np.ascontiguousarray(inputs["L0_attn_x"])
    ident = np.eye(128, dtype=ml_dtypes.bfloat16)
    wts = prep_weights(inputs)
    in_maps = []
    for c in range(8):
        b, gr = c // 4, c % 4
        tiles = [gr + 4 * j for j in range(CT)]
        xs = np.ascontiguousarray(
            np.concatenate([x[b, 128 * g:128 * (g + 1), :] for g in tiles], axis=0))
        m = {"xin": xs, "ident_in": ident, "masktbl": _mask_table(gr)}
        m.update(wts)
        in_maps.append(m)

    res = run_bass_kernel_spmd(nc, in_maps, core_ids=list(range(8)))
    out = np.empty((B, S, D), np.float32)
    for c in range(8):
        b, gr = c // 4, c % 4
        xo = res.results[c]["xout"]
        for j in range(CT):
            g = gr + 4 * j
            out[b, 128 * g:128 * (g + 1), :] = xo[128 * j:128 * (j + 1), :]
    return out


if __name__ == "__main__":
    nc = build(debug=False)
    print("instructions:", nc.n_instructions())


# version 15 (final)
# speedup vs baseline: 1.4768x; 1.0125x over previous
import contextlib
import numpy as np
import ml_dtypes

import concourse.bass as bass
import concourse.mybir as mybir
from concourse.bass_utils import run_bass_kernel_spmd

F32 = mybir.dt.float32
BF16 = mybir.dt.bfloat16
AF = mybir.ActivationFunctionType
ALU = mybir.AluOpType
AX = mybir.AxisListType

B, S, D = 2, 4096, 2048
H, Dh = 16, 128
CT = 8
QTOK = CT * 128
F0 = 5632
GU = 2 * F0
NEXP, FM, FS = 8, 1408, 5632
SCALE = 1.0 / float(np.sqrt(Dh))
SCALE2 = SCALE * SCALE
MAGIC = float(1.5 * 2 ** 23)
NEG = -float(2.0 ** 30)
EPS = 1e-5
DEBUG = False
DEBUG2 = False

WSHAPES = {
    "L0_attn_q_w": (D, D), "L0_attn_k_w": (D, Dh), "L0_attn_v_w": (D, Dh),
    "L0_attn_o_w": (D, D),
    "L0_ffn_w_gate_up": (D, GU), "L0_ffn_w_down": (F0, D),
    "L0_ffn_norm_weight": (D,),
    "L1_attn_W_K": (D, H * Dh), "L1_attn_W_Q": (D, H * Dh),
    "L1_attn_W_V": (D, H * Dh), "L1_attn_W_O": (H * Dh, D),
    "L1_ffn_gate_w": (NEXP, D),
    "L1_ffn_expert_fc1": (D, NEXP * FM), "L1_ffn_expert_fc2": (D, NEXP * FM),
    "L1_ffn_expert_proj": (FM, NEXP * D),
    "L1_ffn_shared_fc1": (D, FS), "L1_ffn_shared_fc2": (D, FS),
    "L1_ffn_shared_proj": (FS, D), "L1_ffn_shared_gate_w": (1, D),
}
NO_T = ("L0_ffn_norm_weight", "L1_ffn_gate_w", "L1_ffn_shared_gate_w")


def prep_weights(inputs):
    out = {}
    for k in WSHAPES:
        a = np.asarray(inputs[k], np.float32)
        if k in NO_T:
            out[k] = np.ascontiguousarray(a.reshape(WSHAPES[k]))
        else:
            out[k] = np.ascontiguousarray(a.reshape(-1, a.shape[-1]).T)
        assert out[k].shape == tuple(WSHAPES[k]), (k, out[k].shape)
    return out


def kg_of(kt):
    return (kt % 8) * 4 + kt // 8


KTS_LOW = [kt for kt in range(32) if kg_of(kt) < 16]
KTS_HIGH = [kt for kt in range(32) if kg_of(kt) >= 16]


class Sem:
    def __init__(self, stack, nc, name, ordered=False):
        self.h = stack.enter_context(nc.semaphore(name))
        self.v = 0
        self.ordered = ordered

    def inc(self, n=1):
        self.v += n
        return self.v


class Prog:
    ENGS = ("gpsimd", "sync", "scalar", "vector", "tensor")

    def __init__(self, nc):
        self.nc = nc
        self.q = {k: [] for k in self.ENGS}

    def emit(self, eng, fn):
        self.q[eng].append(fn)

    def flush(self):
        if not any(self.q.values()):
            return
        with self.nc.Block() as block:
            for name in self.ENGS:
                fns = self.q[name]
                if fns:
                    def runner(e, fns=fns):
                        for f in fns:
                            f(e)
                    getattr(block, name)(runner)
        self.q = {k: [] for k in self.ENGS}


class KC:
    pass


def gw(kc, eng, sem, tgt=None):
    if not sem.ordered:
        tgt = sem.v
    if tgt is None or tgt <= 0:
        return
    assert tgt <= sem.v, f"wait target {tgt} > queued {sem.v}"
    kc.P.emit(eng, lambda E, s=sem, t=tgt: E.wait_ge(s.h, t))


def dma(kc, eng, sem, out, in_, **kw):
    sem.inc(16)
    kc.P.emit(eng, lambda E, o=out, i=in_, s=sem, kw=kw:
              E.dma_start(out=o, in_=i, **kw).then_inc(s.h, 16))
    return sem


def tdma(kc, out, in_, sem=None):
    sem = sem if sem is not None else kc.sd_y
    sem.inc(16)
    kc.P.emit("sync", lambda E, o=out, i=in_, s=sem: E.dma_start(out=o, in_=i, transpose=True)
              .then_inc(s.h, 16))
    return sem


def vop(kc, fn, sep=False):
    if sep:
        kc.P.emit("vector", lambda V, t=kc.s_v.v: V.wait_ge(kc.s_v.h, t))
    tgt = kc.s_v.inc(1)
    kc.P.emit("vector", lambda V, f=fn: f(V).then_inc(kc.s_v.h, 1))
    return tgt


def sop(kc, fn, sep=False):
    if sep:
        kc.P.emit("scalar", lambda Sc, t=kc.s_s.v: Sc.wait_ge(kc.s_s.h, t))
    tgt = kc.s_s.inc(1)
    kc.P.emit("scalar", lambda Sc, f=fn: f(Sc).then_inc(kc.s_s.h, 1))
    return tgt


def top(kc, fn, bump=True):
    if bump:
        tgt = kc.s_t.inc(1)
        kc.P.emit("tensor", lambda T, f=fn: f(T).then_inc(kc.s_t.h, 1))
        return tgt
    kc.P.emit("tensor", lambda T, f=fn: f(T))
    return None


def bank_wait(kc, b):
    dep = kc.bankdep.get(b)
    if dep is not None:
        gw(kc, "tensor", *dep)
        kc.bankdep[b] = None


def bank_done(kc, b, sem, tgt):
    kc.bankdep[b] = (sem, tgt)


def buf_wait(kc, eng, name):
    dep = kc.bufdep.get(name)
    if dep is not None:
        gw(kc, eng, *dep)
        kc.bufdep[name] = None


def buf_done(kc, name, sem, tgt=None):
    kc.bufdep[name] = (sem, tgt)


def phase_barrier(kc):
    tv, tt_, ts_ = kc.s_v.v, kc.s_t.v, kc.s_s.v
    for eng in ("scalar", "vector", "tensor", "sync"):
        gw(kc, eng, kc.s_v, tv)
        gw(kc, eng, kc.s_t, tt_)
        gw(kc, eng, kc.s_s, ts_)
        for sm in kc.barrier_sems():
            gw(kc, eng, sm)


def wapf(kc, k):
    return kc.W[k].ap()


def emit_ln(kc, tt, out_ap, rms_col=None):
    st = kc.stats
    gw(kc, "scalar", kc.s_v, kc.x_ready_v)
    jk = out_ap
    sop(kc, lambda Sc, tt=tt, jk=jk: Sc.activation(jk, kc.x_res[:, tt, :],
                                                   AF.Identity, accum_out=st[:, 0:1]))
    t_s = sop(kc, lambda Sc, tt=tt, jk=jk: Sc.activation(jk, kc.x_res[:, tt, :],
                                                         AF.Square, accum_out=st[:, 1:2]))
    gw(kc, "vector", kc.s_s, t_s)
    vop(kc, lambda V: V.tensor_scalar(st[:, 2:3], st[:, 0:1], 1.0 / D, None, ALU.mult))
    vop(kc, lambda V: V.tensor_scalar(st[:, 3:4], st[:, 1:2], 1.0 / D, None, ALU.mult))
    vop(kc, lambda V: V.tensor_mul(st[:, 4:5], st[:, 2:3], st[:, 2:3]))
    tv = vop(kc, lambda V: V.tensor_sub(st[:, 5:6], st[:, 3:4], st[:, 4:5]), sep=True)
    gw(kc, "scalar", kc.s_v, tv)
    tsq = sop(kc, lambda Sc: Sc.activation(st[:, 6:7], st[:, 5:6], AF.Sqrt, bias=kc.cvals[:, 0:1]))
    gw(kc, "vector", kc.s_s, tsq)
    vop(kc, lambda V: V.reciprocal(st[:, 7:8], st[:, 6:7]))
    vop(kc, lambda V: V.tensor_mul(st[:, 8:9], st[:, 2:3], st[:, 7:8]), sep=True)
    vop(kc, lambda V: V.tensor_scalar(st[:, 8:9], st[:, 8:9], -1.0, None, ALU.mult), sep=True)
    tgt = vop(kc, lambda V, tt=tt, o=out_ap: V.tensor_scalar(
        o, kc.x_res[:, tt, :], st[:, 7:8], st[:, 8:9], ALU.mult, ALU.add), sep=True)
    if rms_col is not None:
        vop(kc, lambda V: V.tensor_mul(st[:, 9:10], st[:, 5:6], st[:, 7:8]))
        tv2 = vop(kc, lambda V: V.tensor_mul(st[:, 9:10], st[:, 9:10], st[:, 7:8]), sep=True)
        gw(kc, "scalar", kc.s_v, tv2)
        ts2 = sop(kc, lambda Sc: Sc.activation(st[:, 9:10], st[:, 9:10], AF.Sqrt, bias=kc.cvals[:, 0:1]))
        gw(kc, "vector", kc.s_s, ts2)
        vop(kc, lambda V, c=rms_col: V.reciprocal(kc.pvec0[:, c:c + 1], st[:, 9:10]))
    return tgt


def emit_t2f(kc, src_fn, dst_col0, kcn, featbuf, src_ready=None, colw=1024):
    for g in range((kcn + 3) // 4):
        nj = min(4, kcn - g * 4)
        b = 6 + (kc.tp_flip % 2)
        kc.tp_flip += 1
        bank_wait(kc, b)
        if src_ready is not None:
            gw(kc, "tensor", *src_ready)
            src_ready = None
        for j in range(nj):
            kcc = g * 4 + j
            top(kc, lambda T, b=b, j=j, kcc=kcc: T.matmul(
                kc.ps[b][:, j * 128:(j + 1) * 128], src_fn(kcc), kc.ident[:, :],
                start=True, stop=True), bump=(j == nj - 1))
        gw(kc, "vector", kc.s_t, kc.s_t.v)
        tv = vop(kc, lambda V, b=b, g=g, c0=dst_col0, nj=nj, fb=featbuf, colw=colw:
                 V.tensor_copy(
                     bass.AP(fb, g * 4 * colw + c0, [[16 * colw, 128], [colw, nj], [1, 128]]),
                     bass.AP(kc.ps[b], 0, [[512, 128], [128, nj], [1, 128]])))
        bank_done(kc, b, kc.s_v, tv)


def slab_load(kc, dst_buf, base, width, wT_ap, o0, ncols, kcn, sem=None, r0=0,
              f32=True):
    if not f32:
        t = None
        for c in range(kcn):
            t = dma(kc, "sync", sem,
                    dst_buf[:, base + c * width: base + c * width + ncols],
                    wT_ap[r0 + c * 128: r0 + (c + 1) * 128, o0:o0 + ncols])
        return (t, None)
    ts_ = None
    stgT, stg_n = kc.stgT, kc.stg_n
    for c in range(kcn):
        sl = kc.stg_flip % stg_n
        kc.stg_flip += 1
        dep = kc.stg_dep[sl]
        if dep is not None:
            gw(kc, "sync", kc.s_s, dep)
        dma(kc, "sync", sem, stgT[:, sl, 0:ncols],
            wT_ap[r0 + c * 128: r0 + (c + 1) * 128, o0:o0 + ncols])
        gw(kc, "scalar", sem)
        ts_ = sop(kc, lambda Sc, c=c, sl=sl, ncols=ncols, base=base, width=width,
                  dst_buf=dst_buf, stgT=stgT: Sc.activation(
                      dst_buf[:, base + c * width: base + c * width + ncols],
                      stgT[:, sl, 0:ncols], AF.Identity))
        kc.stg_dep[sl] = ts_
    return (kc.s_s, ts_)


def cross_part_max(kc, src_ap, nh):
    tv = vop(kc, lambda V: V.tensor_copy(kc.cprb[:, 0:nh], src_ap), sep=True)
    b = 6 + (kc.tp_flip % 2)
    kc.tp_flip += 1
    bank_wait(kc, b)
    gw(kc, "tensor", kc.s_v, tv)
    t_t = top(kc, lambda T, b=b, nh=nh: T.matmul(
        kc.ps[b][0:nh, 0:128], kc.cprb[:, 0:nh], kc.ident[:, :],
        start=True, stop=True))
    gw(kc, "vector", kc.s_t, t_t)
    tv = vop(kc, lambda V, b=b, nh=nh: V.tensor_reduce(
        kc.kmaxb[0:nh, 0:1], kc.ps[b][0:nh, 0:128], AX.X, ALU.max))
    bank_done(kc, b, kc.s_v, tv)
    return tv


def cross_part_sum(kc, src_col):
    vop(kc, lambda V: V.tensor_copy(kc.cprb[:, 16:17], src_col), sep=True)
    vop(kc, lambda V: V.tensor_copy(kc.stats[:, 30:31], kc.cprb[:, 16:17]), sep=True)
    vop(kc, lambda V: V.tensor_sub(kc.stats[:, 31:32], src_col, kc.stats[:, 30:31]), sep=True)
    tv = vop(kc, lambda V: V.tensor_copy(kc.cprb[:, 17:18], kc.stats[:, 31:32]), sep=True)
    b = 6 + (kc.tp_flip % 2)
    kc.tp_flip += 1
    bank_wait(kc, b)
    gw(kc, "tensor", kc.s_v, tv)
    t_t = top(kc, lambda T, b=b: T.matmul(
        kc.ps[b][0:1, 0:2], kc.ones_bf[:, :], kc.cprb[:, 16:18],
        start=True, stop=True))
    gw(kc, "vector", kc.s_t, t_t)
    tv = vop(kc, lambda V, b=b: V.tensor_reduce(
        kc.kmaxb[0:1, 0:1], kc.ps[b][0:1, 0:2], AX.X, ALU.add))
    bank_done(kc, b, kc.s_v, tv)
    return tv


def emit_bcast16(kc, src_col_ap, nh, dst_ap):
    tv = vop(kc, lambda V, nh=nh: V.tensor_copy(kc.cprb[0:nh, 16:17], src_col_ap), sep=True)
    b = 6 + (kc.tp_flip % 2)
    kc.tp_flip += 1
    bank_wait(kc, b)
    gw(kc, "tensor", kc.s_v, tv)
    top(kc, lambda T, b=b, nh=nh: T.matmul(
        kc.ps[b][0:1, 0:nh], kc.cprb[0:nh, 16:17], kc.ident[0:nh, 0:nh],
        start=True, stop=True))
    gw(kc, "vector", kc.s_t, kc.s_t.v)
    tv = vop(kc, lambda V, b=b, nh=nh: V.tensor_copy(kc.cprb[0:1, 0:nh], kc.ps[b][0:1, 0:nh]))
    bank_done(kc, b, kc.s_v, tv)
    b2 = 6 + (kc.tp_flip % 2)
    kc.tp_flip += 1
    bank_wait(kc, b2)
    gw(kc, "tensor", kc.s_v, tv)
    top(kc, lambda T, b2=b2, nh=nh: T.matmul(
        kc.ps[b2][:, 0:nh], kc.ones_row[0:1, :], kc.cprb[0:1, 0:nh],
        start=True, stop=True))
    gw(kc, "vector", kc.s_t, kc.s_t.v)
    tv = vop(kc, lambda V, b2=b2, nh=nh, d=dst_ap: V.tensor_copy(d, kc.ps[b2][:, 0:nh]))
    bank_done(kc, b2, kc.s_v, tv)
    return tv


def emit_bcast_f32(kc, src_11, dst_col):
    vop(kc, lambda V: V.tensor_copy(kc.cprb[0:1, 0:1], src_11), sep=True)
    vop(kc, lambda V: V.tensor_copy(kc.stats[0:1, 30:31], kc.cprb[0:1, 0:1]), sep=True)
    vop(kc, lambda V: V.tensor_sub(kc.stats[0:1, 31:32], src_11, kc.stats[0:1, 30:31]), sep=True)
    tv = vop(kc, lambda V: V.tensor_copy(kc.cprb[0:1, 1:2], kc.stats[0:1, 31:32]), sep=True)
    b = 6 + (kc.tp_flip % 2)
    kc.tp_flip += 1
    bank_wait(kc, b)
    gw(kc, "tensor", kc.s_v, tv)
    top(kc, lambda T, b=b: T.matmul(
        kc.ps[b][:, 0:2], kc.ones_row[0:1, :], kc.cprb[0:1, 0:2],
        start=True, stop=True))
    gw(kc, "vector", kc.s_t, kc.s_t.v)
    vop(kc, lambda V, b=b: V.tensor_copy(kc.stats[:, 30:31], kc.ps[b][:, 0:1]))
    tv2 = vop(kc, lambda V, b=b, d=dst_col: V.tensor_add(
        d, kc.ps[b][:, 1:2], kc.stats[:, 30:31]), sep=True)
    bank_done(kc, b, kc.s_v, tv2)
    return tv2


def emit_attention(kc, layer):
    nc = kc.nc
    l1 = (layer == 1)
    kvloc, kvg = (kc.kv1loc, kc.kv1g) if l1 else (kc.kv0loc, kc.kv0g)
    KV = 2 * (D if l1 else Dh)
    half = KV // 2
    nh = half // 128
    wq, wk, wv, wo = (("L1_attn_W_Q", "L1_attn_W_K", "L1_attn_W_V", "L1_attn_W_O")
                      if l1 else
                      ("L0_attn_q_w", "L0_attn_k_w", "L0_attn_v_w", "L0_attn_o_w"))

    with contextlib.ExitStack() as actx:
        ee = actx.enter_context
        featbuf = ee(nc.sbuf_tensor("featbufA%d" % layer, [128, 16, 1024], BF16))
        QT = ee(nc.sbuf_tensor("QT%d" % layer, [128, 8, 1024], BF16))
        KTs = [ee(nc.sbuf_tensor(f"KTs{layer}{i}", [128, 4096], BF16)) for i in range(2)]
        Vaug = [ee(nc.sbuf_tensor(f"Vaug{layer}{i}", [128, 32, 136], BF16)) for i in range(2)]
        Eb = [ee(nc.sbuf_tensor(f"Eb{layer}{i}", [128, 512], BF16)) for i in range(2)]
        zblk = ee(nc.sbuf_tensor("zblk%d" % layer, [128, 128], BF16))
        phase_barrier(kc)

        for tt in range(CT):
            buf_wait(kc, "vector", f"xn{tt % 2}")
            xv = emit_ln(kc, tt, kc.xn_blk[:, tt % 2, :])
            if getattr(kc, "debug2", False) and not l1 and tt == 0:
                gw(kc, "scalar", kc.s_v, xv)
                dma(kc, "scalar", kc.sd_out, kc.dbg["dbg_xn0"][:, :], kc.xn_blk[:, 0, :])
                dma(kc, "scalar", kc.sd_out, kc.dbg["dbg_st0"][:, :], kc.stats[:, :])
            emit_t2f(kc, (lambda kcc, tt=tt: kc.xn_blk[:, tt % 2, kcc * 128:(kcc + 1) * 128]),
                     tt * 128, 16, featbuf, src_ready=(kc.s_v, xv))
            buf_done(kc, f"xn{tt % 2}", kc.s_t, kc.s_t.v)

        if getattr(kc, "debug2", False) and not l1:
            gw(kc, "scalar", kc.s_v, kc.s_v.v)
            gw(kc, "scalar", kc.s_t, kc.s_t.v)
            dma(kc, "scalar", kc.sd_out, kc.dbg["dbg_fb"][:, :],
                bass.AP(featbuf, 0, [[16 * 1024, 128], [1, 16 * 1024]]))
        featbuf_ready = kc.s_v.v
        gw(kc, "tensor", kc.s_v, featbuf_ready)
        vop(kc, lambda V: V.memset(kc.svec[:, :], 0.0))
        ocw = min(512, KV)
        n_oc = KV // ocw
        kv_store_t = None
        prev_knorm_v = None

        def load_kv_slab(oc):
            ab = kc.arena[oc % 2]
            abn = f"a{oc % 2}"
            buf_wait(kc, "scalar", abn)
            ysem = kc.ys(abn)
            if not l1:
                slab_load(kc, ab, 0, 256, wapf(kc, wk), 0, 128, 16, sem=ysem)
                t_sl = slab_load(kc, ab, 128, 256, wapf(kc, wv), 0, 128, 16, sem=ysem)
            else:
                key = wk if oc < 4 else wv
                r0 = (oc % 4) * 512
                t_sl = slab_load(kc, ab, 0, 512, wapf(kc, key), r0, 512, 16, sem=ysem)
            return ab, abn, t_sl
        pend_kv = load_kv_slab(0)
        for oc in range(n_oc):
            nxt_kv = load_kv_slab(oc + 1) if oc + 1 < n_oc else None
            ab, abn, t_sl = pend_kv
            pend_kv = nxt_kv
            gw(kc, "tensor", *t_sl)
            t_last_mm = None
            for tt in range(CT):
                b = tt % 2
                bank_wait(kc, b)
                for i in range(16):
                    top(kc, lambda T, b=b, i=i, tt=tt, ab=ab, ocw=ocw: T.matmul(
                        kc.ps[b][:, 0:ocw], featbuf[:, i, tt * 128:(tt + 1) * 128],
                        ab[:, i * ocw:(i + 1) * ocw],
                        start=(i == 0), stop=(i == 15)), bump=(i == 15))
                t_mm = kc.s_t.v
                t_last_mm = t_mm
                gw(kc, "vector", kc.s_t, t_mm)
                buf_wait(kc, "vector", f"xn{tt % 2}")
                tv = vop(kc, lambda V, b=b, tt=tt, ocw=ocw: V.tensor_copy(
                    kc.xn_blk[:, tt % 2, 0:ocw], kc.ps[b][:, 0:ocw]))
                bank_done(kc, b, kc.s_v, tv)
                if (not l1) or oc < 4:
                    kcols = 128 if not l1 else 512
                    gw(kc, "scalar", kc.s_v, tv)
                    if prev_knorm_v is not None:
                        gw(kc, "scalar", kc.s_v, prev_knorm_v)
                    t_sq = sop(kc, lambda Sc, tt=tt, kcols=kcols: Sc.activation(
                        kc.sqs[:, 0:kcols], kc.xn_blk[:, tt % 2, 0:kcols], AF.Square))
                    gw(kc, "vector", kc.s_s, t_sq)
                    nseg = kcols // 128
                    prev_knorm_v = vop(kc, lambda V, nseg=nseg: V.tensor_reduce(
                        kc.stats[:, 16:16 + nseg],
                        bass.AP(kc.sqs, 0, [[512, 128], [128, nseg], [1, 128]]),
                        AX.X, ALU.add))
                    c0 = 0 if not l1 else oc * 4
                    vop(kc, lambda V, c0=c0, nseg=nseg: V.tensor_max(
                        kc.svec[:, c0:c0 + nseg], kc.svec[:, c0:c0 + nseg],
                        kc.stats[:, 16:16 + nseg]), sep=True)
                gw(kc, "scalar", kc.s_v, kc.s_v.v)
                kv_store_t = dma(kc, "scalar", kc.sd_kv,
                                 kvloc[tt * 128:(tt + 1) * 128, oc * ocw:(oc + 1) * ocw],
                                 kc.xn_blk[:, tt % 2, 0:ocw])
                buf_done(kc, f"xn{tt % 2}", kc.sd_kv, None)
            buf_done(kc, abn, kc.s_t, t_last_mm)
        gw(kc, "gpsimd", kc.s_v, kc.s_v.v)
        n_t = dma(kc, "gpsimd", kc.sd_n, kvloc[1024:1152, 0:nh], kc.svec[:, 0:nh],
                  allow_slow_non_contiguous=True)
        gw(kc, "gpsimd", kc.sd_kv)
        gw(kc, "gpsimd", kc.sd_n)
        cc_t = kc.s_cc.inc(1)
        kc.P.emit("gpsimd", lambda G, i=kvloc, o=kvg: G.collective_compute(
            "AllGather", ALU.bypass, replica_groups=[[0, 1, 2, 3], [4, 5, 6, 7]],
            ins=[i.ap()], outs=[o.ap()]).then_inc(kc.s_cc.h, 1))
        if getattr(kc, "debug2", False) and not l1:
            gw(kc, "gpsimd", kc.s_cc, cc_t)
            dma(kc, "gpsimd", kc.sd_out, kc.dbg["dbg_kvg"][:, :], kvg[:, :])
        def norms_block():
            gw(kc, "gpsimd", kc.s_cc, cc_t)
            dma(kc, "gpsimd", kc.sd_n, kc.svec[:, 0:4 * nh],
                bass.AP(kvg, 1024 * KV, [[KV, 128], [1152 * KV, 4], [1, nh]]),
                allow_slow_non_contiguous=True)
            gw(kc, "vector", kc.sd_n)
            vop(kc, lambda V, nh=nh: V.tensor_reduce(
                kc.stats[:, 16:16 + nh],
                bass.AP(kc.svec, 0, [[64, 128], [1, nh], [nh, 4]]), AX.X, ALU.max))
            cross_part_max(kc, kc.stats[:, 16:16 + nh], nh)

        def load_kt(buf, h):
            buf_wait(kc, "sync", f"kt{buf}")
            sm = kc.ys(f"kt{buf}")
            for r in range(4):
                tdma(kc, KTs[buf][:, r * 1024:(r + 1) * 1024],
                     kvg[r * 1152:r * 1152 + 1024, h * 128:(h + 1) * 128], sem=sm)
            return sm

        def load_vaug(buf, h):
            buf_wait(kc, "scalar", f"va{buf}")
            sm = kc.ys(f"va{buf}")
            for r in range(4):
                dma(kc, "scalar", sm, Vaug[buf][:, r * 8:(r + 1) * 8, 0:128],
                    kvg.ap()[r * 1152:r * 1152 + 1024,
                             half + h * 128:half + (h + 1) * 128]
                    .rearrange("(t p) d -> p t d", p=128))
            return sm

        norms_block()
        gw(kc, "sync", kc.s_cc, cc_t)
        gw(kc, "scalar", kc.s_cc, cc_t)
        for bb in range(2):
            vop(kc, lambda V, bb=bb: V.memset(
                bass.AP(Vaug[bb], 128, [[32 * 136, 128], [136, 32], [1, 1]]), 1.0))
        kt_t = {0: load_kt(0, 0)}
        va_t = {0: load_vaug(0, 0)}

        for grp in range(2):
            gw(kc, "tensor", kc.s_v, featbuf_ready)
            vop(kc, lambda V: V.memset(kc.stats[0:1, 12:13], 0.0))
            prev_on = None
            for hs in range(2):
                buf_wait(kc, "scalar", "a1")
                t_q = slab_load(kc, kc.arena[1], 0, 512, wapf(kc, wq),
                                grp * 1024 + hs * 512, 512, 16, sem=kc.ys("a1"))
                gw(kc, "tensor", *t_q)
                t_pj = None
                for sl in range(4):
                    s = hs * 4 + sl
                    for qc in range(2):
                        b = qc
                        bank_wait(kc, b)
                        for i in range(16):
                            top(kc, lambda T, b=b, i=i, s=s, sl=sl, qc=qc: T.matmul(
                                kc.ps[b][:, 0:512],
                                kc.arena[1][:, i * 512 + sl * 128:i * 512 + (sl + 1) * 128],
                                featbuf[:, i, qc * 512:(qc + 1) * 512],
                                start=(i == 0), stop=(i == 15)), bump=(i == 15))
                        t_pj = kc.s_t.v
                        gw(kc, "vector", kc.s_t, t_pj)
                        tv = vop(kc, lambda V, b=b, s=s, qc=qc: V.tensor_copy(
                            QT[:, s, qc * 512:(qc + 1) * 512], kc.ps[b][:, 0:512]))
                        bank_done(kc, b, kc.s_v, tv)
                        gw(kc, "scalar", kc.s_v, tv)
                        if prev_on is not None:
                            gw(kc, "scalar", kc.s_t, prev_on)
                        t_sq = sop(kc, lambda Sc, s=s, qc=qc: Sc.activation(
                            kc.sqs[:, 0:512], QT[:, s, qc * 512:(qc + 1) * 512], AF.Square))
                        bank_wait(kc, 5)
                        gw(kc, "tensor", kc.s_s, t_sq)
                        t_on = top(kc, lambda T: T.matmul(
                            kc.ps[5][0:1, 0:512], kc.ones_bf[:, :], kc.sqs[:, 0:512],
                            start=True, stop=True))
                        prev_on = t_on
                        gw(kc, "vector", kc.s_t, t_on)
                        vop(kc, lambda V: V.tensor_reduce(
                            kc.stats[0:1, 13:14], kc.ps[5][0:1, 0:512], AX.X, ALU.max))
                        tv2 = vop(kc, lambda V: V.tensor_max(
                            kc.stats[0:1, 12:13], kc.stats[0:1, 12:13], kc.stats[0:1, 13:14]),
                            sep=True)
                        bank_done(kc, 5, kc.s_v, tv2)
                buf_done(kc, "a1", kc.s_t, t_pj)
            qt_ready = kc.s_v.v
            emit_bcast16(kc, kc.stats[0:1, 12:13], 1, kc.stats[:, 13:14])
            tv = vop(kc, lambda V, nh=nh: V.tensor_mul(
                kc.kmaxb[0:nh, 2:3], kc.kmaxb[0:nh, 0:1], kc.stats[0:nh, 13:14]), sep=True)
            gw(kc, "scalar", kc.s_v, tv)
            ts_ = sop(kc, lambda Sc, nh=nh: Sc.activation(
                kc.kmaxb[0:nh, 3:4], kc.kmaxb[0:nh, 2:3], AF.Sqrt, scale=SCALE2))
            gw(kc, "vector", kc.s_s, ts_)
            tv = vop(kc, lambda V, nh=nh: V.tensor_scalar(
                kc.kmaxb[0:nh, 3:4], kc.kmaxb[0:nh, 3:4], -1.0, -1.0, ALU.mult, ALU.add))
            emit_bcast16(kc, kc.kmaxb[0:nh, 3:4], nh, kc.nmb[:, 0:nh])
            nmb_v = kc.s_v.v
            gw(kc, "scalar", kc.s_v, nmb_v)

            if getattr(kc, "debug2", False) and not l1 and grp == 0:
                gw(kc, "scalar", kc.s_v, kc.s_v.v)
                dma(kc, "scalar", kc.sd_out, kc.dbg["dbg_qt"][:, :],
                    bass.AP(QT, 0, [[8 * 1024, 128], [1, 8 * 1024]]))
                dma(kc, "scalar", kc.sd_out, kc.dbg["dbg_nmb"][:, :], kc.nmb[:, :])
                gw(kc, "scalar", kc.ys("kt0"))
                gw(kc, "scalar", kc.ys("va0"))
                dma(kc, "scalar", kc.sd_out, kc.dbg["dbg_kt"][:, :], KTs[0][:, :])
                dma(kc, "scalar", kc.sd_out, kc.dbg["dbg_va"][:, :],
                    bass.AP(Vaug[0], 0, [[32 * 136, 128], [1, 32 * 136]]))
            if l1:
                buf_wait(kc, "scalar", "a1")
                dma(kc, "scalar", kc.sd_mk, kc.arena[1][:, 0:8192],
                    kc.mask_in[:, 0:8192])
                gw(kc, "tensor", kc.sd_mk)

            gw(kc, "tensor", kc.s_v, qt_ready)
            for s in range(8):
                h = grp * 8 + s
                if l1:
                    if h + 1 < 16:
                        kt_t[(h + 1) % 2] = load_kt((h + 1) % 2, h + 1)
                        va_t[(h + 1) % 2] = load_vaug((h + 1) % 2, h + 1)
                    kbuf = h % 2
                else:
                    kbuf = 0
                gw(kc, "tensor", kt_t[kbuf])
                gw(kc, "tensor", va_t[kbuf])
                gw(kc, "scalar", kc.s_v, nmb_v)
                mcol = h if l1 else 0
                for qc in range(2):
                    if l1:
                        kts = ([(kt, True) for kt in KTS_LOW] if qc == 0 else
                               [(kt, False) for kt in KTS_LOW] +
                               [(kt, True) for kt in KTS_HIGH])
                    else:
                        kts = [(kt, False) for kt in range(32)]
                    nkt = len(kts)
                    for j in range(4):
                        bank_wait(kc, j)
                    e_free = [kc.s_t.v, kc.s_t.v]

                    def emit_qk(idx, s=s, qc=qc, kbuf=kbuf, kts=kts):
                        kt, masked = kts[idx]
                        sb = 4 + (idx % 2)
                        bank_wait(kc, sb)
                        top(kc, lambda T, sb=sb, kt=kt, s=s, qc=qc, kbuf=kbuf, m=masked:
                            T.matmul(kc.ps[sb][:, 0:512],
                                     KTs[kbuf][:, kt * 128:(kt + 1) * 128],
                                     QT[:, s, qc * 512:(qc + 1) * 512],
                                     start=True, stop=not m), bump=not masked)
                        if masked:
                            bi = kg_of(kt) % 16
                            top(kc, lambda T, sb=sb, bi=bi: T.matmul(
                                kc.ps[sb][:, 0:512], kc.ident[:, :],
                                kc.arena[1][:, bi * 512:(bi + 1) * 512],
                                start=False, stop=True))
                        t_qk = kc.s_t.v
                        gw(kc, "scalar", kc.s_t, t_qk)
                        if e_free[idx % 2] is not None:
                            gw(kc, "scalar", kc.s_t, e_free[idx % 2])
                        t_exp = sop(kc, lambda Sc, sb=sb, ix=idx % 2, mc=mcol: Sc.activation(
                            Eb[ix][:, :], kc.ps[sb][:, 0:512], AF.Exp,
                            bias=kc.nmb[:, mc:mc + 1], scale=SCALE))
                        bank_done(kc, sb, kc.s_s, t_exp)
                        return t_exp

                    t_exp_cur = emit_qk(0)
                    for idx in range(nkt):
                        kt, masked = kts[idx]
                        t_exp_nxt = emit_qk(idx + 1) if idx + 1 < nkt else None
                        gw(kc, "tensor", kc.s_s, t_exp_cur)
                        for j in range(4):
                            top(kc, lambda T, j=j, ix=idx % 2, kt=kt, kbuf=kbuf,
                                st=(idx == 0), sp=(idx == nkt - 1): T.matmul(
                                kc.ps[j][:, 0:129], Eb[ix][:, j * 128:(j + 1) * 128],
                                Vaug[kbuf][:, kt, 0:129],
                                start=st, stop=sp), bump=(j == 3))
                        e_free[idx % 2] = kc.s_t.v
                        t_exp_cur = t_exp_nxt
                    t_av = kc.s_t.v
                    gw(kc, "vector", kc.s_t, t_av)
                    for j in range(4):
                        vop(kc, lambda V, j=j: V.reciprocal(
                            kc.stats[:, 14 + j:15 + j], kc.ps[j][:, 128:129]))
                    for j in range(4):
                        tv = vop(kc, lambda V, j=j: V.tensor_scalar(
                            zblk[:, :], kc.ps[j][:, 0:128], kc.stats[:, 14 + j:15 + j], None,
                            ALU.mult))
                        bank_done(kc, j, kc.s_v, tv)
                        b = 6 + (kc.tp_flip % 2)
                        kc.tp_flip += 1
                        bank_wait(kc, b)
                        gw(kc, "tensor", kc.s_v, tv)
                        t_tp = top(kc, lambda T, b=b: T.matmul(
                            kc.ps[b][:, 0:128], zblk[:, :], kc.ident[:, :],
                            start=True, stop=True))
                        gw(kc, "vector", kc.s_t, t_tp)
                        tv2 = vop(kc, lambda V, b=b, s=s, qc=qc, j=j: V.tensor_copy(
                            QT[:, s, qc * 512 + j * 128:qc * 512 + (j + 1) * 128],
                            kc.ps[b][:, 0:128]))
                        bank_done(kc, b, kc.s_v, tv2)
                if l1:
                    buf_done(kc, f"kt{kbuf}", kc.s_t, kc.s_t.v)
                    buf_done(kc, f"va{kbuf}", kc.s_t, kc.s_t.v)
            if l1:
                buf_done(kc, "a1", kc.s_t, kc.s_t.v)
            if getattr(kc, "debug2", False) and not l1 and grp == 0:
                gw(kc, "scalar", kc.s_v, kc.s_v.v)
                dma(kc, "scalar", kc.sd_out, kc.dbg["dbg_z"][:, :],
                    bass.AP(QT, 0, [[8 * 1024, 128], [1, 8 * 1024]]))
            zt_ready = kc.s_v.v
            for dc in range(2):
                buf_wait(kc, "scalar", "a0")
                slab_load(kc, kc.arena[0], 0, 1024, wapf(kc, wo),
                          dc * 1024, 512, 8, sem=kc.ys("a0"), r0=grp * 1024)
                t_wo = slab_load(kc, kc.arena[0], 512, 1024, wapf(kc, wo),
                                 dc * 1024 + 512, 512, 8, sem=kc.ys("a0"),
                                 r0=grp * 1024)
                gw(kc, "tensor", *t_wo)
                gw(kc, "tensor", kc.s_v, zt_ready)
                t_last = None
                for tt in range(CT):
                    for ncc in range(2):
                        b = (tt * 2 + ncc) % 4
                        bank_wait(kc, b)
                        for si in range(8):
                            top(kc, lambda T, b=b, si=si, tt=tt, ncc=ncc: T.matmul(
                                kc.ps[b][:, 0:512], QT[:, si, tt * 128:(tt + 1) * 128],
                                kc.arena[0][:, si * 1024 + ncc * 512:si * 1024 + (ncc + 1) * 512],
                                start=(si == 0), stop=(si == 7)), bump=(si == 7))
                        t_mm = kc.s_t.v
                        t_last = t_mm
                        gw(kc, "vector", kc.s_t, t_mm)
                        tv = vop(kc, lambda V, b=b, tt=tt, dc=dc, ncc=ncc: V.tensor_add(
                            kc.x_res[:, tt, dc * 1024 + ncc * 512:dc * 1024 + (ncc + 1) * 512],
                            kc.x_res[:, tt, dc * 1024 + ncc * 512:dc * 1024 + (ncc + 1) * 512],
                            kc.ps[b][:, 0:512]))
                        bank_done(kc, b, kc.s_v, tv)
                buf_done(kc, "a0", kc.s_t, t_last)
        kc.x_ready_v = kc.s_v.v


def emit_bitnet_prep(kc):
    nc = kc.nc
    with contextlib.ExitStack() as actx:
        ee = actx.enter_context
        wstage = ee(nc.sbuf_tensor("wstage", [128, 8, 1024], F32))
        wstage2 = ee(nc.sbuf_tensor("wstage2", [128, 2, 1024], F32))
        wqb = ee(nc.sbuf_tensor("wqb", [128, 2, 1024], BF16))
        chs = ee(nc.sbuf_tensor("chs", [128, 288], F32))
        grdt = ee(nc.sbuf_tensor("grdt", [128, 288], F32))
        phase_barrier(kc)

        specs = [("L0_ffn_w_gate_up", kc.gu_q, 11, 1024, 10, 18),
                 ("L0_ffn_w_down", kc.wd_q, 2, 1024, 11, 19)]
        for key, qdst, nslc, slw, mwcol, swcol in specs:
            src = kc.W[key].ap()
            T0, T1 = WSHAPES[key]
            pieces = [(bd * 128, sc * slw) for bd in range(T0 // 128)
                      for sc in range(nslc)]
            nch = len(pieces)
            for ch in range(min(8, nch)):
                r_, c_ = pieces[ch]
                dma(kc, "scalar", kc.ys(f"b{ch % 8}"), wstage[:, ch % 8, 0:slw],
                    src[r_:r_ + 128, c_:c_ + slw])
            for ch in range(nch):
                gw(kc, "scalar", kc.ys(f"b{ch % 8}"))
                sop(kc, lambda Sc, ch=ch, slw=slw: Sc.activation(
                    wstage2[:, 0, 0:slw], wstage[:, ch % 8, 0:slw], AF.Abs,
                    accum_out=chs[:, ch:ch + 1]))
                if ch + 8 < nch:
                    r_, c_ = pieces[ch + 8]
                    dma(kc, "scalar", kc.ys(f"b{ch % 8}"), wstage[:, ch % 8, 0:slw],
                        src[r_:r_ + 128, c_:c_ + slw])
            t_s = kc.s_s.v
            gw(kc, "vector", kc.s_s, t_s)
            tgv = vop(kc, lambda V, nch=nch: V.tensor_scalar(
                grdt[:, 0:nch], chs[:, 0:nch], 1024.0, MAGIC, ALU.mult, ALU.add))
            gw(kc, "scalar", kc.s_v, tgv)
            tgs = sop(kc, lambda Sc, nch=nch: Sc.activation(
                grdt[:, 0:nch], grdt[:, 0:nch], AF.Identity, bias=kc.cvals[:, 2:3]), sep=True)
            gw(kc, "vector", kc.s_s, tgs)
            vop(kc, lambda V, nch=nch: V.tensor_scalar(
                grdt[:, 0:nch], grdt[:, 0:nch], 1.0 / 1024.0, None, ALU.mult), sep=True)
            vop(kc, lambda V, nch=nch: V.tensor_sub(
                chs[:, 0:nch], chs[:, 0:nch], grdt[:, 0:nch]), sep=True)
            vop(kc, lambda V, nch=nch: V.tensor_reduce(
                kc.stats[:, 16:17], grdt[:, 0:nch], AX.X, ALU.add), sep=True)
            vop(kc, lambda V, nch=nch: V.tensor_reduce(
                kc.stats[:, 17:18], chs[:, 0:nch], AX.X, ALU.add), sep=True)
            tgv = vop(kc, lambda V: V.tensor_scalar(
                kc.stats[:, 18:19], kc.stats[:, 16:17], 1.0 / 16.0, MAGIC,
                ALU.mult, ALU.add), sep=True)
            gw(kc, "scalar", kc.s_v, tgv)
            tgs = sop(kc, lambda Sc: Sc.activation(
                kc.stats[:, 18:19], kc.stats[:, 18:19], AF.Identity,
                bias=kc.cvals[:, 2:3]), sep=True)
            gw(kc, "vector", kc.s_s, tgs)
            vop(kc, lambda V: V.tensor_scalar(
                kc.stats[:, 18:19], kc.stats[:, 18:19], 16.0, None, ALU.mult), sep=True)
            vop(kc, lambda V: V.tensor_sub(
                kc.stats[:, 19:20], kc.stats[:, 16:17], kc.stats[:, 18:19]), sep=True)
            tgv = vop(kc, lambda V: V.tensor_scalar(
                kc.stats[:, 20:21], kc.stats[:, 19:20], 16.0, MAGIC,
                ALU.mult, ALU.add), sep=True)
            gw(kc, "scalar", kc.s_v, tgv)
            tgs = sop(kc, lambda Sc: Sc.activation(
                kc.stats[:, 20:21], kc.stats[:, 20:21], AF.Identity,
                bias=kc.cvals[:, 2:3]), sep=True)
            gw(kc, "vector", kc.s_s, tgs)
            vop(kc, lambda V: V.tensor_scalar(
                kc.stats[:, 20:21], kc.stats[:, 20:21], 1.0 / 16.0, None, ALU.mult), sep=True)
            vop(kc, lambda V: V.tensor_sub(
                kc.stats[:, 21:22], kc.stats[:, 19:20], kc.stats[:, 20:21]), sep=True)
            vop(kc, lambda V: V.tensor_copy(kc.cprb[:, 3:4], kc.stats[:, 17:18]), sep=True)
            vop(kc, lambda V: V.tensor_copy(kc.stats[:, 22:23], kc.cprb[:, 3:4]), sep=True)
            vop(kc, lambda V: V.tensor_sub(
                kc.stats[:, 23:24], kc.stats[:, 17:18], kc.stats[:, 22:23]), sep=True)
            vop(kc, lambda V: V.tensor_copy(kc.cprb[:, 0:1], kc.stats[:, 18:19]), sep=True)
            vop(kc, lambda V: V.tensor_copy(kc.cprb[:, 1:2], kc.stats[:, 20:21]), sep=True)
            vop(kc, lambda V: V.tensor_copy(kc.cprb[:, 2:3], kc.stats[:, 21:22]), sep=True)
            tvp = vop(kc, lambda V: V.tensor_copy(kc.cprb[:, 4:5], kc.stats[:, 23:24]), sep=True)
            b = 6 + (kc.tp_flip % 2)
            kc.tp_flip += 1
            bank_wait(kc, b)
            gw(kc, "tensor", kc.s_v, tvp)
            t_t = top(kc, lambda T, b=b: T.matmul(
                kc.ps[b][0:1, 0:5], kc.ones_bf[:, :], kc.cprb[:, 0:5],
                start=True, stop=True))
            gw(kc, "vector", kc.s_t, t_t)
            tv = vop(kc, lambda V, b=b: V.tensor_copy(kc.kmaxb[0:1, 8:13], kc.ps[b][0:1, 0:5]))
            bank_done(kc, b, kc.s_v, tv)
            vop(kc, lambda V: V.tensor_add(kc.kmaxb[0:1, 0:1], kc.kmaxb[0:1, 12:13],
                                           kc.kmaxb[0:1, 10:11]), sep=True)
            vop(kc, lambda V: V.tensor_add(kc.kmaxb[0:1, 0:1], kc.kmaxb[0:1, 0:1],
                                           kc.kmaxb[0:1, 11:12]), sep=True)
            vop(kc, lambda V: V.tensor_add(kc.kmaxb[0:1, 0:1], kc.kmaxb[0:1, 0:1],
                                           kc.kmaxb[0:1, 9:10]), sep=True)
            vop(kc, lambda V: V.tensor_add(kc.kmaxb[0:1, 0:1], kc.kmaxb[0:1, 0:1],
                                           kc.kmaxb[0:1, 8:9]), sep=True)
            vop(kc, lambda V, n=T0 * T1: V.tensor_scalar(
                kc.kmaxb[0:1, 1:2], kc.kmaxb[0:1, 0:1], float(np.float32(1.0 / n)),
                1e-5, ALU.mult, ALU.max), sep=True)
            vop(kc, lambda V: V.reciprocal(kc.kmaxb[0:1, 2:3], kc.kmaxb[0:1, 1:2]), sep=True)
            vop(kc, lambda V: V.tensor_scalar(kc.kmaxb[0:1, 3:4], kc.kmaxb[0:1, 1:2],
                                              4097.0, None, ALU.mult), sep=True)
            vop(kc, lambda V: V.tensor_sub(kc.kmaxb[0:1, 4:5], kc.kmaxb[0:1, 3:4],
                                           kc.kmaxb[0:1, 1:2]), sep=True)
            vop(kc, lambda V: V.tensor_sub(kc.kmaxb[0:1, 3:4], kc.kmaxb[0:1, 3:4],
                                           kc.kmaxb[0:1, 4:5]), sep=True)
            vop(kc, lambda V: V.tensor_sub(kc.kmaxb[0:1, 4:5], kc.kmaxb[0:1, 1:2],
                                           kc.kmaxb[0:1, 3:4]), sep=True)
            vop(kc, lambda V: V.tensor_scalar(kc.kmaxb[0:1, 5:6], kc.kmaxb[0:1, 2:3],
                                              4097.0, None, ALU.mult), sep=True)
            vop(kc, lambda V: V.tensor_sub(kc.kmaxb[0:1, 6:7], kc.kmaxb[0:1, 5:6],
                                           kc.kmaxb[0:1, 2:3]), sep=True)
            vop(kc, lambda V: V.tensor_sub(kc.kmaxb[0:1, 5:6], kc.kmaxb[0:1, 5:6],
                                           kc.kmaxb[0:1, 6:7]), sep=True)
            vop(kc, lambda V: V.tensor_sub(kc.kmaxb[0:1, 6:7], kc.kmaxb[0:1, 2:3],
                                           kc.kmaxb[0:1, 5:6]), sep=True)
            vop(kc, lambda V: V.tensor_mul(kc.kmaxb[0:1, 7:8], kc.kmaxb[0:1, 3:4],
                                           kc.kmaxb[0:1, 5:6]), sep=True)
            vop(kc, lambda V: V.tensor_scalar(kc.kmaxb[0:1, 7:8], kc.kmaxb[0:1, 7:8],
                                              -1.0, 1.0, ALU.mult, ALU.add), sep=True)
            vop(kc, lambda V: V.tensor_mul(kc.kmaxb[0:1, 13:14], kc.kmaxb[0:1, 3:4],
                                           kc.kmaxb[0:1, 6:7]), sep=True)
            vop(kc, lambda V: V.tensor_sub(kc.kmaxb[0:1, 7:8], kc.kmaxb[0:1, 7:8],
                                           kc.kmaxb[0:1, 13:14]), sep=True)
            vop(kc, lambda V: V.tensor_mul(kc.kmaxb[0:1, 13:14], kc.kmaxb[0:1, 4:5],
                                           kc.kmaxb[0:1, 5:6]), sep=True)
            vop(kc, lambda V: V.tensor_sub(kc.kmaxb[0:1, 7:8], kc.kmaxb[0:1, 7:8],
                                           kc.kmaxb[0:1, 13:14]), sep=True)
            vop(kc, lambda V: V.tensor_mul(kc.kmaxb[0:1, 13:14], kc.kmaxb[0:1, 4:5],
                                           kc.kmaxb[0:1, 6:7]), sep=True)
            vop(kc, lambda V: V.tensor_sub(kc.kmaxb[0:1, 7:8], kc.kmaxb[0:1, 7:8],
                                           kc.kmaxb[0:1, 13:14]), sep=True)
            vop(kc, lambda V: V.tensor_mul(kc.kmaxb[0:1, 7:8], kc.kmaxb[0:1, 7:8],
                                           kc.kmaxb[0:1, 2:3]), sep=True)
            vop(kc, lambda V: V.tensor_add(kc.kmaxb[0:1, 2:3], kc.kmaxb[0:1, 2:3],
                                           kc.kmaxb[0:1, 7:8]), sep=True)
            tvb = kc.s_v.v
            gw(kc, "scalar", kc.s_v, tvb)
            dma(kc, "scalar", kc.sd_st, bass.AP(kc.scrM, 0, [[1, 1], [1, 1]]),
                kc.kmaxb[0:1, 1:2])
            dma(kc, "scalar", kc.sd_st, bass.AP(kc.scr128, 0, [[1, 1], [1, 1]]),
                kc.kmaxb[0:1, 2:3])
            gw(kc, "scalar", kc.sd_st)
            dma(kc, "scalar", kc.sd_st, kc.stats[:, mwcol:mwcol + 1],
                bass.AP(kc.scrM, 0, [[0, 128], [1, 1]]))
            dma(kc, "scalar", kc.sd_st, kc.stats[:, swcol:swcol + 1],
                bass.AP(kc.scr128, 0, [[0, 128], [1, 1]]))
            gw(kc, "scalar", kc.sd_st)
            gw(kc, "vector", kc.sd_st)
            if getattr(kc, "debug2", False):
                dma(kc, "scalar", kc.sd_out,
                    kc.dbg["dbg_kmx"][0:16, (0 if key.endswith("up") else 128):
                                      (128 if key.endswith("up") else 256)],
                    kc.kmaxb[0:16, :])
                dma(kc, "scalar", kc.sd_out,
                    kc.dbg["dbg_acc"][:, (0 if key.endswith("up") else 4):
                                      (4 if key.endswith("up") else 8)],
                    kc.stats[:, 16:20])
            for ch in range(min(8, nch)):
                r_, c_ = pieces[ch]
                dma(kc, "scalar", kc.ys(f"b{ch % 8}"), wstage[:, ch % 8, 0:slw],
                    src[r_:r_ + 128, c_:c_ + slw])
            prev_tr = [None, None]
            prev_tv = [None, None]
            for ch in range(nch):
                r_, c_ = pieces[ch]
                pb = ch % 2
                gw(kc, "vector", kc.ys(f"b{ch % 8}"))
                if prev_tr[pb] is not None:
                    gw(kc, "vector", kc.s_s, prev_tr[pb])
                if ch >= 2:
                    gw(kc, "vector", kc.ys(f"bq{pb}"))
                tm = vop(kc, lambda V, ch=ch, slw=slw, c=swcol, pb=pb: V.tensor_scalar(
                    wstage2[:, pb, 0:slw], wstage[:, ch % 8, 0:slw], kc.stats[:, c:c + 1],
                    MAGIC, ALU.mult, ALU.add))
                gw(kc, "scalar", kc.s_v, tm)
                t_r = sop(kc, lambda Sc, slw=slw, pb=pb: Sc.activation(
                    wstage2[:, pb, 0:slw], wstage2[:, pb, 0:slw], AF.Identity,
                    bias=kc.cvals[:, 2:3]))
                prev_tr[pb] = t_r
                gw(kc, "vector", kc.s_s, t_r)
                tv = vop(kc, lambda V, slw=slw, pb=pb: V.tensor_scalar(
                    wqb[:, pb, 0:slw], wstage2[:, pb, 0:slw], -1.0, 1.0, ALU.max, ALU.min))
                prev_tv[pb] = tv
                gw(kc, "scalar", kc.s_v, tv)
                dma(kc, "scalar", kc.ys(f"bq{pb}"), qdst[r_:r_ + 128, c_:c_ + slw],
                    wqb[:, pb, 0:slw])
                if ch + 8 < nch:
                    r_, c_ = pieces[ch + 8]
                    dma(kc, "scalar", kc.ys(f"b{ch % 8}"), wstage[:, ch % 8, 0:slw],
                        src[r_:r_ + 128, c_:c_ + slw])


def emit_bitnet_ffn(kc):
    nc = kc.nc
    with contextlib.ExitStack() as actx:
        ee = actx.enter_context
        featbuf = ee(nc.sbuf_tensor("featbufB", [128, 16, 1024], BF16))
        xn32 = ee(nc.sbuf_tensor("xn32", [128, D], F32))
        t32 = ee(nc.sbuf_tensor("t32", [128, D], F32))
        wnb = ee(nc.sbuf_tensor("wnb", [128, D], F32))
        rbuf = ee(nc.sbuf_tensor("rbuf", [128, 256], F32))
        tbuf = ee(nc.sbuf_tensor("tbuf", [128, 256], F32))
        hbuf = ee(nc.sbuf_tensor("hbuf", [128, 2, 256], BF16))
        phase_barrier(kc)

        t_wn = dma(kc, "scalar", kc.sd_st, wnb[:, :],
                   bass.AP(kc.W["L0_ffn_norm_weight"], 0, [[0, 128], [1, D]]))
        vop(kc, lambda V: V.memset(kc.pvec3[:, :], 0.0))
        txq_prev = None
        for tt in range(CT):
            xv = emit_ln(kc, tt, xn32[:, :], rms_col=tt)
            gw(kc, "vector", kc.sd_st)
            vop(kc, lambda V: V.tensor_mul(xn32[:, :], xn32[:, :], wnb[:, :]), sep=True)
            tv = vop(kc, lambda V: V.tensor_reduce(
                kc.stats[:, 16:17], xn32[:, :], AX.X, ALU.max, apply_absolute_value=True),
                sep=True)
            vop(kc, lambda V, tt=tt: V.tensor_mul(kc.stats[:, 17:18],
                                                  kc.pvec0[:, tt:tt + 1], kc.stats[:, 16:17]),
                sep=True)
            vop(kc, lambda V: V.tensor_scalar(kc.stats[:, 17:18], kc.stats[:, 17:18],
                                              1e-5, None, ALU.max), sep=True)
            vop(kc, lambda V: V.reciprocal(kc.stats[:, 18:19], kc.stats[:, 17:18]), sep=True)
            vop(kc, lambda V, tt=tt: V.tensor_mul(kc.stats[:, 19:20],
                                                  kc.stats[:, 18:19], kc.pvec0[:, tt:tt + 1]),
                sep=True)
            vop(kc, lambda V: V.tensor_scalar(kc.stats[:, 19:20], kc.stats[:, 19:20],
                                              127.0, None, ALU.mult), sep=True)
            vop(kc, lambda V: V.tensor_scalar(kc.stats[:, 20:21], kc.stats[:, 17:18],
                                              1.0 / 127.0, None, ALU.mult))
            vop(kc, lambda V: V.tensor_mul(kc.stats[:, 20:21], kc.stats[:, 20:21],
                                           kc.stats[:, 10:11]), sep=True)
            vop(kc, lambda V: V.tensor_mul(kc.stats[:, 21:22], kc.stats[:, 20:21],
                                           kc.stats[:, 20:21]), sep=True)
            vop(kc, lambda V, tt=tt: V.tensor_mul(kc.pvec2[:, tt:tt + 1],
                                                  kc.stats[:, 21:22], kc.stats[:, 20:21]),
                sep=True)
            if txq_prev is not None:
                gw(kc, "vector", kc.s_s, txq_prev)
            tq = vop(kc, lambda V: V.tensor_scalar(t32[:, :], xn32[:, :],
                                                   kc.stats[:, 19:20], MAGIC,
                                                   ALU.mult, ALU.add))
            gw(kc, "scalar", kc.s_v, tq)
            buf_wait(kc, "scalar", f"xn{tt % 2}")
            txq = sop(kc, lambda Sc, tt=tt: Sc.activation(
                kc.xn_blk[:, tt % 2, :], t32[:, :], AF.Identity, bias=kc.cvals[:, 2:3]))
            txq_prev = txq
            if getattr(kc, "debug2", False) and tt == 0:
                gw(kc, "scalar", kc.s_s, txq)
                dma(kc, "scalar", kc.sd_out, kc.dbg["dbg_xq"][:, :], kc.xn_blk[:, 0, :])
                dma(kc, "scalar", kc.sd_out, kc.dbg["dbg_sc"][:, :], kc.stats[:, :])
            gw(kc, "tensor", kc.s_s, txq)
            emit_t2f(kc, (lambda kcc, tt=tt: kc.xn_blk[:, tt % 2, kcc * 128:(kcc + 1) * 128]),
                     tt * 128, 16, featbuf)
            buf_done(kc, f"xn{tt % 2}", kc.s_t, kc.s_t.v)
        featbuf_ready = kc.s_v.v
        gw(kc, "tensor", kc.s_v, featbuf_ready)
        gw(kc, "sync", kc.ys("bq0"))
        gw(kc, "sync", kc.ys("bq1"))
        prev_tb = None
        for fs in range(22):
            pp = fs % 2
            gbase, ubase = pp * 4096, pp * 4096
            buf_wait(kc, "sync", f"a0h{pp}")
            t_g = slab_load(kc, kc.arena[0], gbase, 256, kc.gu_q.ap(), fs * 256, 256, 16,
                            sem=kc.ys(f"a0h{pp}"), f32=False)
            buf_wait(kc, "sync", f"a1h{pp}")
            t_u = slab_load(kc, kc.arena[1], ubase, 256, kc.gu_q.ap(),
                            F0 + fs * 256, 256, 16, sem=kc.ys(f"a1h{pp}"), f32=False)
            gw(kc, "tensor", *t_g)
            gw(kc, "tensor", *t_u)
            t_last = None
            for tt in range(CT):
                bank_wait(kc, 4)
                for i in range(16):
                    top(kc, lambda T, i=i, tt=tt, gbase=gbase: T.matmul(
                        kc.ps[4][:, 0:256], featbuf[:, i, tt * 128:(tt + 1) * 128],
                        kc.arena[0][:, gbase + i * 256:gbase + (i + 1) * 256],
                        start=(i == 0), stop=(i == 15)), bump=(i == 15))
                t_m4 = kc.s_t.v
                bank_wait(kc, 5)
                for i in range(16):
                    top(kc, lambda T, i=i, tt=tt, ubase=ubase: T.matmul(
                        kc.ps[5][:, 0:256], featbuf[:, i, tt * 128:(tt + 1) * 128],
                        kc.arena[1][:, ubase + i * 256:ubase + (i + 1) * 256],
                        start=(i == 0), stop=(i == 15)), bump=(i == 15))
                t_last = kc.s_t.v
                gw(kc, "scalar", kc.s_t, t_m4)
                if prev_tb is not None:
                    gw(kc, "scalar", kc.s_v, prev_tb)
                t_re = sop(kc, lambda Sc: Sc.activation(rbuf[:, :], kc.ps[4][:, 0:256],
                                                        AF.Relu))
                gw(kc, "vector", kc.s_s, t_re)
                prev_tb = vop(kc, lambda V: V.scalar_tensor_tensor(
                    tbuf[:, :], kc.ps[4][:, 0:256], 1.0, rbuf[:, :], ALU.mult, ALU.mult))
                buf_wait(kc, "vector", f"hb{tt % 2}")
                gw(kc, "vector", kc.s_t, t_last)
                tv = vop(kc, lambda V, tt=tt: V.scalar_tensor_tensor(
                    hbuf[:, tt % 2, :], kc.ps[5][:, 0:256], 1.0, tbuf[:, :],
                    ALU.mult, ALU.mult), sep=True)
                bank_done(kc, 4, kc.s_v, tv)
                bank_done(kc, 5, kc.s_v, tv)
                vop(kc, lambda V, tt=tt: V.tensor_reduce(
                    kc.stats[:, 16:17], hbuf[:, tt % 2, :], AX.X, ALU.max,
                    apply_absolute_value=True), sep=True)
                vop(kc, lambda V, tt=tt: V.tensor_max(
                    kc.pvec3[:, tt:tt + 1], kc.pvec3[:, tt:tt + 1], kc.stats[:, 16:17]),
                    sep=True)
                gw(kc, "scalar", kc.s_v, kc.s_v.v)
                dma(kc, "scalar", kc.ys("h"),
                    kc.h_dram[tt * 128:(tt + 1) * 128, fs * 256:(fs + 1) * 256],
                    hbuf[:, tt % 2, :])
                buf_done(kc, f"hb{tt % 2}", kc.ys("h"), None)
            buf_done(kc, f"a0h{pp}", kc.s_t, t_last)
            buf_done(kc, f"a1h{pp}", kc.s_t, t_last)
        if getattr(kc, "debug2", False):
            gw(kc, "gpsimd", kc.ys("h"))
            dma(kc, "gpsimd", kc.sd_out, kc.dbg["dbg_h"][:, :], kc.h_dram[0:128, :])
            dma(kc, "gpsimd", kc.sd_out, kc.dbg["dbg_guq"][:, :], kc.gu_q[:, :])
            dma(kc, "gpsimd", kc.sd_out, kc.dbg["dbg_wdq"][:, :], kc.wd_q[:, :])
            gw(kc, "gpsimd", kc.s_v, kc.s_v.v)
            dma(kc, "gpsimd", kc.sd_out, kc.dbg["dbg_pv"][:, :], kc.pvec2[:, :])
            dma(kc, "gpsimd", kc.sd_out, kc.dbg["dbg_pv3"][:, :], kc.pvec3[:, :])

    with contextlib.ExitStack() as actx:
        ee = actx.enter_context
        hqT = ee(nc.sbuf_tensor("hqT", [128, 44, 512], BF16))
        t32h = ee(nc.sbuf_tensor("t32h", [128, F0], F32))
        hb = ee(nc.sbuf_tensor("hb", [128, F0], BF16))
        phase_barrier(kc)

        prev_tp = None
        prev_tq = None
        for halfb in range(2):
            for ttl in range(4):
                tt = halfb * 4 + ttl
                if prev_tp is not None:
                    gw(kc, "scalar", kc.s_t, prev_tp)
                dma(kc, "scalar", kc.ys("h"), hb[:, :],
                    kc.h_dram[tt * 128:(tt + 1) * 128, :])
                vop(kc, lambda V, tt=tt: V.tensor_mul(
                    kc.stats[:, 16:17], kc.pvec2[:, tt:tt + 1], kc.pvec3[:, tt:tt + 1]))
                vop(kc, lambda V: V.tensor_scalar(kc.stats[:, 16:17], kc.stats[:, 16:17],
                                                  1e-5, None, ALU.max), sep=True)
                vop(kc, lambda V: V.reciprocal(kc.stats[:, 17:18], kc.stats[:, 16:17]), sep=True)
                vop(kc, lambda V, tt=tt: V.tensor_mul(kc.stats[:, 18:19],
                                                      kc.stats[:, 17:18],
                                                      kc.pvec2[:, tt:tt + 1]), sep=True)
                vop(kc, lambda V: V.tensor_scalar(kc.stats[:, 18:19], kc.stats[:, 18:19],
                                                  127.0, None, ALU.mult), sep=True)
                vop(kc, lambda V: V.tensor_scalar(kc.stats[:, 19:20], kc.stats[:, 16:17],
                                                  1.0 / 127.0, None, ALU.mult))
                vop(kc, lambda V, tt=tt: V.tensor_mul(kc.pvec4[:, tt:tt + 1],
                                                      kc.stats[:, 19:20],
                                                      kc.stats[:, 11:12]), sep=True)
                gw(kc, "vector", kc.ys("h"))
                if prev_tq is not None:
                    gw(kc, "vector", kc.s_s, prev_tq)
                tq = vop(kc, lambda V: V.tensor_scalar(
                    t32h[:, :], hb[:, :], kc.stats[:, 18:19], MAGIC, ALU.mult, ALU.add))
                gw(kc, "scalar", kc.s_v, tq)
                thq = sop(kc, lambda Sc: Sc.activation(hb[:, :], t32h[:, :],
                                                       AF.Identity, bias=kc.cvals[:, 2:3]))
                prev_tq = thq
                for g in range(11):
                    bk = 6 + (kc.tp_flip % 2)
                    kc.tp_flip += 1
                    bank_wait(kc, bk)
                    if g == 0:
                        gw(kc, "tensor", kc.s_s, thq)
                    for j in range(4):
                        kcc = g * 4 + j
                        top(kc, lambda T, bk=bk, j=j, kcc=kcc: T.matmul(
                            kc.ps[bk][:, j * 128:(j + 1) * 128],
                            hb[:, kcc * 128:(kcc + 1) * 128], kc.ident[:, :],
                            start=True, stop=True), bump=(j == 3))
                    gw(kc, "vector", kc.s_t, kc.s_t.v)
                    prev_tp = kc.s_t.v
                    tv = vop(kc, lambda V, bk=bk, g=g, ttl=ttl: V.tensor_copy(
                        bass.AP(hqT, g * 4 * 512 + ttl * 128,
                                [[44 * 512, 128], [512, 4], [1, 128]]),
                        bass.AP(kc.ps[bk], 0, [[512, 128], [128, 4], [1, 128]])))
                    bank_done(kc, bk, kc.s_v, tv)
            hq_ready = kc.s_v.v
            for sl in range(8):
                pp = sl % 2
                buf_wait(kc, "sync", f"a{pp}")
                t_w = slab_load(kc, kc.arena[pp], 0, 256, kc.wd_q.ap(), sl * 256, 256, 44,
                                sem=kc.ys(f"a{pp}"), f32=False)
                if getattr(kc, "debug2", False) and halfb == 1 and sl == 4:
                    gw(kc, "gpsimd", kc.ys(f"a{pp}"))
                    dma(kc, "gpsimd", kc.sd_out, kc.dbg["dbg_slab"][:, :],
                        kc.arena[pp][:, 29 * 256:30 * 256])
                    gw(kc, "gpsimd", kc.s_v, hq_ready)
                    dma(kc, "gpsimd", kc.sd_out, kc.dbg["dbg_hqt"][:, :],
                        bass.AP(hqT, 29 * 512, [[44 * 512, 128], [1, 512]]))
                gw(kc, "tensor", *t_w)
                gw(kc, "tensor", kc.s_v, hq_ready)
                t_last = None
                for ttl in range(4):
                    tt = halfb * 4 + ttl
                    b = ttl % 4
                    bank_wait(kc, b)
                    for i in range(44):
                        top(kc, lambda T, b=b, i=i, ttl=ttl, pp=pp: T.matmul(
                            kc.ps[b][:, 0:256], hqT[:, i, ttl * 128:(ttl + 1) * 128],
                            kc.arena[pp][:, i * 256:(i + 1) * 256],
                            start=(i == 0), stop=(i == 43)), bump=(i == 43))
                    t_mm = kc.s_t.v
                    t_last = t_mm
                    gw(kc, "vector", kc.s_t, t_mm)
                    tv = vop(kc, lambda V, b=b, tt=tt, sl=sl: V.scalar_tensor_tensor(
                        kc.x_res[:, tt, sl * 256:(sl + 1) * 256], kc.ps[b][:, 0:256],
                        kc.pvec4[:, tt:tt + 1],
                        kc.x_res[:, tt, sl * 256:(sl + 1) * 256], ALU.mult, ALU.add))
                    bank_done(kc, b, kc.s_v, tv)
                buf_done(kc, f"a{pp}", kc.s_t, t_last)
        kc.x_ready_v = kc.s_v.v


def emit_moe(kc):
    nc = kc.nc
    with contextlib.ExitStack() as actx:
        ee = actx.enter_context
        featbuf = ee(nc.sbuf_tensor("featbufD", [128, 16, 1024], BF16))
        hT = ee(nc.sbuf_tensor("hT", [128, 11, 1024], BF16))
        gwT = ee(nc.sbuf_tensor("gwT", [128, 16, 8], BF16))
        sgwT = ee(nc.sbuf_tensor("sgwT", [128, 16, 1], BF16))
        cmb = ee(nc.sbuf_tensor("cmb", [128, CT, NEXP], F32))
        slb = ee(nc.sbuf_tensor("slb", [128, 512], BF16))
        gwsb = ee(nc.sbuf_tensor("gwsb", [8, D], BF16))
        sgwsb = ee(nc.sbuf_tensor("sgwsb", [1, D], BF16))
        stgM = ee(nc.sbuf_tensor("stgM", [128, 8, 512], F32))
        phase_barrier(kc)
        kc.stgT, kc.stg_n = stgM, 8
        kc.stg_dep = [None] * 16
        kc.stg_flip = 0

        for tt in range(CT):
            buf_wait(kc, "vector", f"xn{tt % 2}")
            xv = emit_ln(kc, tt, kc.xn_blk[:, tt % 2, :])
            emit_t2f(kc, (lambda kcc, tt=tt: kc.xn_blk[:, tt % 2, kcc * 128:(kcc + 1) * 128]),
                     tt * 128, 16, featbuf, src_ready=(kc.s_v, xv))
            buf_done(kc, f"xn{tt % 2}", kc.s_t, kc.s_t.v)

        featbuf_ready = kc.s_v.v
        gw(kc, "tensor", kc.s_v, featbuf_ready)
        kc._slb_prev = None
        dma(kc, "scalar", kc.sd_st,
            bass.AP(stgM, 0, [[8 * 512, 8], [512, 4], [1, 512]]),
            kc.W["L1_ffn_gate_w"][:, :])
        dma(kc, "scalar", kc.sd_st,
            bass.AP(stgM, 4 * 512, [[8 * 512, 1], [512, 4], [1, 512]]),
            kc.W["L1_ffn_shared_gate_w"][:, :])
        gw(kc, "vector", kc.sd_st)
        vop(kc, lambda V: V.tensor_copy(
            bass.AP(gwsb, 0, [[D, 8], [512, 4], [1, 512]]),
            bass.AP(stgM, 0, [[8 * 512, 8], [512, 4], [1, 512]])))
        tgc = vop(kc, lambda V: V.tensor_copy(
            bass.AP(sgwsb, 0, [[D, 1], [512, 4], [1, 512]]),
            bass.AP(stgM, 4 * 512, [[8 * 512, 1], [512, 4], [1, 512]])))
        gw(kc, "tensor", kc.s_v, tgc)
        gw(kc, "scalar", kc.s_v, tgc)
        tss0 = sop(kc, lambda Sc: Sc.activation(kc.jsep[:, 0:1], kc.cvals[:, 0:1],
                                                AF.Identity))
        kc.stg_dep = [tss0] * 16
        for i in range(16):
            bk = 6 + (kc.tp_flip % 2)
            kc.tp_flip += 1
            bank_wait(kc, bk)
            top(kc, lambda T, bk=bk, i=i: T.matmul(
                kc.ps[bk][:, 0:8], gwsb[0:8, i * 128:(i + 1) * 128],
                kc.ident[0:8, 0:8], start=True, stop=True), bump=False)
            top(kc, lambda T, bk=bk, i=i: T.matmul(
                kc.ps[bk][:, 8:9], sgwsb[0:1, i * 128:(i + 1) * 128],
                kc.ident[0:1, 0:1], start=True, stop=True))
            gw(kc, "vector", kc.s_t, kc.s_t.v)
            vop(kc, lambda V, bk=bk, i=i: V.tensor_copy(gwT[:, i, :], kc.ps[bk][:, 0:8]))
            tv = vop(kc, lambda V, bk=bk, i=i: V.tensor_copy(sgwT[:, i, :], kc.ps[bk][:, 8:9]))
            bank_done(kc, bk, kc.s_v, tv)

        gw(kc, "tensor", kc.s_v, kc.s_v.v)
        prev_cmb = None
        for tt in range(CT):
            bank_wait(kc, 4)
            for i in range(16):
                top(kc, lambda T, i=i, tt=tt: T.matmul(
                    kc.ps[4][:, 0:8], featbuf[:, i, tt * 128:(tt + 1) * 128], gwT[:, i, :],
                    start=(i == 0), stop=(i == 15)), bump=(i == 15))
            bank_wait(kc, 5)
            for i in range(16):
                top(kc, lambda T, i=i, tt=tt: T.matmul(
                    kc.ps[5][:, 0:1], featbuf[:, i, tt * 128:(tt + 1) * 128], sgwT[:, i, :],
                    start=(i == 0), stop=(i == 15)), bump=(i == 15))
            t_mm = kc.s_t.v
            gw(kc, "vector", kc.s_t, t_mm)
            vop(kc, lambda V: V.tensor_reduce(kc.stats[:, 16:17], kc.ps[4][:, 0:8],
                                              AX.X, ALU.max))
            tv = vop(kc, lambda V: V.tensor_scalar(kc.stats[:, 16:17], kc.stats[:, 16:17],
                                                   -1.0, None, ALU.mult), sep=True)
            gw(kc, "scalar", kc.s_v, tv)
            if prev_cmb is not None:
                gw(kc, "scalar", kc.s_v, prev_cmb)
            t_e = sop(kc, lambda Sc: Sc.activation(kc.stats[:, 20:28], kc.ps[4][:, 0:8],
                                                   AF.Exp, bias=kc.stats[:, 16:17]))
            t_g = sop(kc, lambda Sc, tt=tt: Sc.activation(
                kc.pvec5[:, tt:tt + 1], kc.ps[5][:, 0:1], AF.Sigmoid))
            bank_done(kc, 4, kc.s_s, t_e)
            bank_done(kc, 5, kc.s_s, t_g)
            gw(kc, "vector", kc.s_s, t_g)
            vop(kc, lambda V: V.tensor_reduce(kc.stats[:, 16:17], kc.stats[:, 20:28],
                                              AX.X, ALU.add))
            vop(kc, lambda V: V.reciprocal(kc.stats[:, 17:18], kc.stats[:, 16:17]), sep=True)
            vop(kc, lambda V: V.tensor_scalar(kc.stats[:, 20:28], kc.stats[:, 20:28],
                                              kc.stats[:, 17:18], None, ALU.mult), sep=True)
            scr8 = kc.svec[:, 32:40]
            vop(kc, lambda V: V.tensor_reduce(kc.stats[:, 16:17], kc.stats[:, 20:28],
                                              AX.X, ALU.max), sep=True)
            vop(kc, lambda V, scr8=scr8: V.tensor_scalar(
                scr8, kc.stats[:, 20:28], kc.stats[:, 16:17], None, ALU.is_ge), sep=True)
            vop(kc, lambda V, scr8=scr8: V.scalar_tensor_tensor(
                scr8, scr8, -1e30, kc.stats[:, 20:28], ALU.mult, ALU.add), sep=True)
            vop(kc, lambda V, scr8=scr8: V.tensor_reduce(
                kc.stats[:, 18:19], scr8, AX.X, ALU.max), sep=True)
            vop(kc, lambda V: V.tensor_tensor(kc.stats[:, 18:19], kc.stats[:, 18:19],
                                              kc.stats[:, 16:17], ALU.min), sep=True)
            vop(kc, lambda V, scr8=scr8: V.tensor_scalar(
                scr8, kc.stats[:, 20:28], kc.stats[:, 18:19], None, ALU.is_ge), sep=True)
            prev_cmb = vop(kc, lambda V, tt=tt, scr8=scr8: V.tensor_mul(
                cmb[:, tt, :], kc.stats[:, 20:28], scr8), sep=True)

        def fc_slabs(e_idx, which, fs):
            key = "L1_ffn_expert_fc1" if which == 0 else "L1_ffn_expert_fc2"
            a = kc.arena[which]
            pp = fs % 2
            base = pp * 4096
            nm = f"a{which}h{pp}"
            buf_wait(kc, "scalar", nm)
            w_ = 256 if fs < 5 else 128
            t = slab_load(kc, a, base, 256, kc.W[key].ap(),
                          e_idx * FM + fs * 256, w_, 16, sem=kc.ys(nm))
            return nm, base, w_, t

        pw = kc.W["L1_ffn_expert_proj"].ap()

        def load_proj(e_idx, sl):
            pp = sl % 2
            nm = f"aC{pp}"
            buf_wait(kc, "scalar", nm)
            t = slab_load(kc, kc.arena[pp], 8448, 256, pw,
                          e_idx * D + sl * 256, 256, 11, sem=kc.ys(nm))
            return nm, pp, t

        pend_fc = (fc_slabs(0, 0, 0), fc_slabs(0, 1, 0))
        for e_idx in range(NEXP):
            pendp = load_proj(e_idx, 0)
            for fs in range(6):
                if fs + 1 < 6:
                    nxt = (fc_slabs(e_idx, 0, fs + 1), fc_slabs(e_idx, 1, fs + 1))
                elif e_idx + 1 < NEXP:
                    nxt = (fc_slabs(e_idx + 1, 0, 0), fc_slabs(e_idx + 1, 1, 0))
                else:
                    nxt = None
                (nm1, base1, w_, t1), (nm2, base2, _, t2) = pend_fc
                pend_fc = nxt
                gw(kc, "tensor", *t1)
                gw(kc, "tensor", *t2)
                t_last = None
                for fcq in range(w_ // 128):
                    fcc = fs * 2 + fcq
                    for qc in range(2):
                        bank_wait(kc, 4)
                        for i in range(16):
                            top(kc, lambda T, i=i, qc=qc, base1=base1, fcq=fcq: T.matmul(
                                kc.ps[4][:, 0:512],
                                kc.arena[0][:, base1 + i * 256 + fcq * 128:
                                            base1 + i * 256 + (fcq + 1) * 128],
                                featbuf[:, i, qc * 512:(qc + 1) * 512],
                                start=(i == 0), stop=(i == 15)), bump=(i == 15))
                        t_m1 = kc.s_t.v
                        gw(kc, "scalar", kc.s_t, t_m1)
                        if kc._slb_prev is not None:
                            gw(kc, "scalar", kc.s_v, kc._slb_prev)
                        t_si = sop(kc, lambda Sc: Sc.activation(slb[:, :], kc.ps[4][:, 0:512],
                                                                AF.Silu))
                        bank_done(kc, 4, kc.s_s, t_si)
                        bank_wait(kc, 5)
                        for i in range(16):
                            top(kc, lambda T, i=i, qc=qc, base2=base2, fcq=fcq: T.matmul(
                                kc.ps[5][:, 0:512],
                                kc.arena[1][:, base2 + i * 256 + fcq * 128:
                                            base2 + i * 256 + (fcq + 1) * 128],
                                featbuf[:, i, qc * 512:(qc + 1) * 512],
                                start=(i == 0), stop=(i == 15)), bump=(i == 15))
                        t_m2 = kc.s_t.v
                        t_last = t_m2
                        gw(kc, "vector", kc.s_t, t_m2)
                        gw(kc, "vector", kc.s_s, t_si)
                        tv = vop(kc, lambda V, fcc=fcc, qc=qc: V.scalar_tensor_tensor(
                            hT[:, fcc, qc * 512:(qc + 1) * 512], kc.ps[5][:, 0:512],
                            1.0, slb[:, :], ALU.mult, ALU.mult))
                        kc._slb_prev = tv
                        bank_done(kc, 5, kc.s_v, tv)
                buf_done(kc, nm1, kc.s_t, t_last)
                buf_done(kc, nm2, kc.s_t, t_last)
            hT_ready = kc.s_v.v
            for sl in range(8):
                nxtp = load_proj(e_idx, sl + 1) if sl + 1 < 8 else None
                nm, pp, t_w = pendp
                pendp = nxtp
                gw(kc, "tensor", *t_w)
                gw(kc, "tensor", kc.s_v, hT_ready)
                t_last = None
                for tt in range(CT):
                    b = tt % 4
                    bank_wait(kc, b)
                    for i in range(11):
                        top(kc, lambda T, b=b, i=i, tt=tt, pp=pp: T.matmul(
                            kc.ps[b][:, 0:256], hT[:, i, tt * 128:(tt + 1) * 128],
                            kc.arena[pp][:, 8448 + i * 256:8448 + (i + 1) * 256],
                            start=(i == 0), stop=(i == 10)), bump=(i == 10))
                    t_mm = kc.s_t.v
                    t_last = t_mm
                    gw(kc, "vector", kc.s_t, t_mm)
                    tv = vop(kc, lambda V, b=b, tt=tt, sl=sl, e_idx=e_idx:
                             V.scalar_tensor_tensor(
                                 kc.x_res[:, tt, sl * 256:(sl + 1) * 256],
                                 kc.ps[b][:, 0:256], cmb[:, tt, e_idx:e_idx + 1],
                                 kc.x_res[:, tt, sl * 256:(sl + 1) * 256],
                                 ALU.mult, ALU.add))
                    bank_done(kc, b, kc.s_v, tv)
                buf_done(kc, nm, kc.s_t, t_last)

        def shared_fc_load(fbase, fs):
            w_ = 256 if fs < 5 else 128
            pp = fs % 2
            nm1, nm2 = f"a0h{pp}", f"a1h{pp}"
            buf_wait(kc, "scalar", nm1)
            t1 = slab_load(kc, kc.arena[0], pp * 4096, 256,
                           kc.W["L1_ffn_shared_fc1"].ap(), fbase + fs * 256, w_, 16,
                           sem=kc.ys(nm1))
            buf_wait(kc, "scalar", nm2)
            t2 = slab_load(kc, kc.arena[1], pp * 4096, 256,
                           kc.W["L1_ffn_shared_fc2"].ap(), fbase + fs * 256, w_, 16,
                           sem=kc.ys(nm2))
            return w_, pp, nm1, nm2, t1, t2

        def load_sproj(sl, fbase):
            pp = sl % 2
            nm = f"aC{pp}"
            buf_wait(kc, "scalar", nm)
            t = slab_load(kc, kc.arena[pp], 8448, 256,
                          kc.W["L1_ffn_shared_proj"].ap(), sl * 256, 256, 11,
                          sem=kc.ys(nm), r0=fbase)
            return nm, pp, t

        pend_s = shared_fc_load(0, 0)
        for fb in range(4):
            fbase = fb * FM
            pendq = load_sproj(0, fbase)
            for fs in range(6):
                if fs + 1 < 6:
                    nxt_s = shared_fc_load(fbase, fs + 1)
                elif fb + 1 < 4:
                    nxt_s = shared_fc_load((fb + 1) * FM, 0)
                else:
                    nxt_s = None
                w_, pp, nm1, nm2, t1, t2 = pend_s
                pend_s = nxt_s
                gw(kc, "tensor", *t1)
                gw(kc, "tensor", *t2)
                t_last = None
                for fcq in range(w_ // 128):
                    fcc = fs * 2 + fcq
                    for qc in range(2):
                        bank_wait(kc, 4)
                        for i in range(16):
                            top(kc, lambda T, i=i, qc=qc, pp=pp, fcq=fcq: T.matmul(
                                kc.ps[4][:, 0:512],
                                kc.arena[0][:, pp * 4096 + i * 256 + fcq * 128:
                                            pp * 4096 + i * 256 + (fcq + 1) * 128],
                                featbuf[:, i, qc * 512:(qc + 1) * 512],
                                start=(i == 0), stop=(i == 15)), bump=(i == 15))
                        t_m1 = kc.s_t.v
                        gw(kc, "scalar", kc.s_t, t_m1)
                        if kc._slb_prev is not None:
                            gw(kc, "scalar", kc.s_v, kc._slb_prev)
                        t_si = sop(kc, lambda Sc: Sc.activation(slb[:, :], kc.ps[4][:, 0:512],
                                                                AF.Silu))
                        bank_done(kc, 4, kc.s_s, t_si)
                        bank_wait(kc, 5)
                        for i in range(16):
                            top(kc, lambda T, i=i, qc=qc, pp=pp, fcq=fcq: T.matmul(
                                kc.ps[5][:, 0:512],
                                kc.arena[1][:, pp * 4096 + i * 256 + fcq * 128:
                                            pp * 4096 + i * 256 + (fcq + 1) * 128],
                                featbuf[:, i, qc * 512:(qc + 1) * 512],
                                start=(i == 0), stop=(i == 15)), bump=(i == 15))
                        t_m2 = kc.s_t.v
                        t_last = t_m2
                        gw(kc, "vector", kc.s_t, t_m2)
                        gw(kc, "vector", kc.s_s, t_si)
                        tv = vop(kc, lambda V, fcc=fcc, qc=qc: V.scalar_tensor_tensor(
                            hT[:, fcc, qc * 512:(qc + 1) * 512], kc.ps[5][:, 0:512],
                            1.0, slb[:, :], ALU.mult, ALU.mult))
                        kc._slb_prev = tv
                        bank_done(kc, 5, kc.s_v, tv)
                buf_done(kc, nm1, kc.s_t, t_last)
                buf_done(kc, nm2, kc.s_t, t_last)
            hT_ready = kc.s_v.v
            for sl in range(8):
                nxtq = load_sproj(sl + 1, fbase) if sl + 1 < 8 else None
                nm, pp, t_w = pendq
                pendq = nxtq
                gw(kc, "tensor", *t_w)
                gw(kc, "tensor", kc.s_v, hT_ready)
                t_last = None
                for tt in range(CT):
                    b = tt % 4
                    bank_wait(kc, b)
                    for i in range(11):
                        top(kc, lambda T, b=b, i=i, tt=tt, pp=pp: T.matmul(
                            kc.ps[b][:, 0:256], hT[:, i, tt * 128:(tt + 1) * 128],
                            kc.arena[pp][:, 8448 + i * 256:8448 + (i + 1) * 256],
                            start=(i == 0), stop=(i == 10)), bump=(i == 10))
                    t_mm = kc.s_t.v
                    t_last = t_mm
                    gw(kc, "vector", kc.s_t, t_mm)
                    tv = vop(kc, lambda V, b=b, tt=tt, sl=sl: V.scalar_tensor_tensor(
                        kc.x_res[:, tt, sl * 256:(sl + 1) * 256], kc.ps[b][:, 0:256],
                        kc.pvec5[:, tt:tt + 1],
                        kc.x_res[:, tt, sl * 256:(sl + 1) * 256], ALU.mult, ALU.add))
                    bank_done(kc, b, kc.s_v, tv)
                buf_done(kc, nm, kc.s_t, t_last)
        kc.x_ready_v = kc.s_v.v


def build(debug=False):
    nc = bass.Bass("TRN2", target_bir_lowering=False)
    kc = KC()
    kc.nc = nc
    kc.tp_flip = 0
    kc.bankdep = {}
    kc.bufdep = {}
    kc.x_ready_v = 0

    kc.xin = nc.dram_tensor("xin", [QTOK, D], F32, kind="ExternalInput")
    kc.ident_in = nc.dram_tensor("ident_in", [128, 128], BF16, kind="ExternalInput")
    kc.mask_in = nc.dram_tensor("masktbl", [128, 16 * 512], BF16, kind="ExternalInput")
    kc.W = {k: nc.dram_tensor(k, list(sh), F32, kind="ExternalInput")
            for k, sh in WSHAPES.items()}
    kc.xout = nc.dram_tensor("xout", [QTOK, D], F32, kind="ExternalOutput")
    kc.dbg = {}
    if debug:
        for nm in ("dbg_x1", "dbg_x2", "dbg_x3"):
            kc.dbg[nm] = nc.dram_tensor(nm, [QTOK, D], F32, kind="ExternalOutput")
    kc.debug2 = bool(debug) and DEBUG2
    if kc.debug2:
        kc.dbg["dbg_fb"] = nc.dram_tensor("dbg_fb", [128, 16 * 1024], BF16, kind="ExternalOutput")
        kc.dbg["dbg_xn0"] = nc.dram_tensor("dbg_xn0", [128, D], BF16, kind="ExternalOutput")
        kc.dbg["dbg_st0"] = nc.dram_tensor("dbg_st0", [128, 32], F32, kind="ExternalOutput")
        kc.dbg["dbg_kvg"] = nc.dram_tensor("dbg_kvg", [4 * 1152, 256], BF16, kind="ExternalOutput")
        kc.dbg["dbg_qt"] = nc.dram_tensor("dbg_qt", [128, 8 * 1024], BF16, kind="ExternalOutput")
        kc.dbg["dbg_nmb"] = nc.dram_tensor("dbg_nmb", [128, 16], F32, kind="ExternalOutput")
        kc.dbg["dbg_kt"] = nc.dram_tensor("dbg_kt", [128, 4096], BF16, kind="ExternalOutput")
        kc.dbg["dbg_va"] = nc.dram_tensor("dbg_va", [128, 32 * 136], BF16, kind="ExternalOutput")
        kc.dbg["dbg_z"] = nc.dram_tensor("dbg_z", [128, 8 * 1024], BF16, kind="ExternalOutput")
        kc.dbg["dbg_xq"] = nc.dram_tensor("dbg_xq", [128, D], BF16, kind="ExternalOutput")
        kc.dbg["dbg_sc"] = nc.dram_tensor("dbg_sc", [128, 32], F32, kind="ExternalOutput")
        kc.dbg["dbg_h"] = nc.dram_tensor("dbg_h", [128, F0], BF16, kind="ExternalOutput")
        kc.dbg["dbg_slab"] = nc.dram_tensor("dbg_slab", [128, 256], BF16, kind="ExternalOutput")
        kc.dbg["dbg_hqt"] = nc.dram_tensor("dbg_hqt", [128, 512], BF16, kind="ExternalOutput")
        kc.dbg["dbg_guq"] = nc.dram_tensor("dbg_guq", [D, GU], BF16, kind="ExternalOutput")
        kc.dbg["dbg_wdq"] = nc.dram_tensor("dbg_wdq", [F0, D], BF16, kind="ExternalOutput")
        kc.dbg["dbg_pv"] = nc.dram_tensor("dbg_pv", [128, CT], F32, kind="ExternalOutput")
        kc.dbg["dbg_pv3"] = nc.dram_tensor("dbg_pv3", [128, CT], F32, kind="ExternalOutput")
        kc.dbg["dbg_kmx"] = nc.dram_tensor("dbg_kmx", [16, 256], F32, kind="ExternalOutput")
        kc.dbg["dbg_acc"] = nc.dram_tensor("dbg_acc", [128, 8], F32, kind="ExternalOutput")

    kc.gu_q = nc.dram_tensor("gu_q", [D, GU], BF16)
    kc.wd_q = nc.dram_tensor("wd_q", [F0, D], BF16)
    kc.h_dram = nc.dram_tensor("h_dram", [QTOK, F0], BF16)
    kc.kv0loc = nc.dram_tensor("kv0loc", [1152, 2 * Dh], BF16)
    kc.kv0g = nc.dram_tensor("kv0g", [4 * 1152, 2 * Dh], BF16)
    kc.kv1loc = nc.dram_tensor("kv1loc", [1152, 2 * D], BF16)
    kc.kv1g = nc.dram_tensor("kv1g", [4 * 1152, 2 * D], BF16)
    kc.scr128 = nc.dram_tensor("scr128", [16, 128], F32)
    kc.scrM = nc.dram_tensor("scrM", [1, 16], F32)

    stack = contextlib.ExitStack()
    with stack:
        e = stack.enter_context
        kc.x_res = e(nc.sbuf_tensor("x_res", [128, CT, D], F32))
        kc.arena = [e(nc.sbuf_tensor(f"arena{i}", [128, 11264], BF16)) for i in range(2)]
        kc.xn_blk = e(nc.sbuf_tensor("xn_blk", [128, 2, D], BF16))
        kc.ident = e(nc.sbuf_tensor("identt", [128, 128], BF16))
        kc.ones_bf = e(nc.sbuf_tensor("ones_bf", [128, 1], BF16))
        kc.sqs = e(nc.sbuf_tensor("sqs", [128, 512], BF16))
        kc.stats = e(nc.sbuf_tensor("stats", [128, 32], F32))
        kc.svec = e(nc.sbuf_tensor("svec", [128, 64], F32))
        kc.nmb = e(nc.sbuf_tensor("nmb", [128, 16], F32))
        kc.pvec0 = e(nc.sbuf_tensor("pvec0", [128, CT], F32))
        kc.pvec2 = e(nc.sbuf_tensor("pvec2", [128, CT], F32))
        kc.pvec3 = e(nc.sbuf_tensor("pvec3", [128, CT], F32))
        kc.pvec4 = e(nc.sbuf_tensor("pvec4", [128, CT], F32))
        kc.pvec5 = e(nc.sbuf_tensor("pvec5", [128, CT], F32))
        kc.kmaxb = e(nc.sbuf_tensor("kmaxb", [16, 128], F32))
        kc.cvals = e(nc.sbuf_tensor("cvals", [128, 4], F32))
        kc.jsep = e(nc.sbuf_tensor("jsep", [128, 4], F32))
        kc.ones_row = e(nc.sbuf_tensor("ones_row", [1, 128], BF16))
        kc.cprb = e(nc.sbuf_tensor("cprb", [128, 18], BF16))
        kc.ps = [e(nc.psum_tensor(f"psb{i}", [128, 512], F32)) for i in range(8)]
        kc.stg = e(nc.sbuf_tensor("stg", [128, 2, 512], F32))
        kc.sd_g = Sem(stack, nc, "sd_g")
        kc.sd_bulk = Sem(stack, nc, "sd_bulk")
        kc.sd_x = Sem(stack, nc, "sd_x")
        kc.sd_n = Sem(stack, nc, "sd_n")
        kc.sd_kv = Sem(stack, nc, "sd_kv")
        kc.sd_st = Sem(stack, nc, "sd_st")
        kc.sd_mk = Sem(stack, nc, "sd_mk")
        kc.sd_out = Sem(stack, nc, "sd_out")
        kc.sd_s = Sem(stack, nc, "sd_s")
        kc.sd_y = Sem(stack, nc, "sd_y")
        kc.s_cc = Sem(stack, nc, "s_cc")
        kc.s_t = Sem(stack, nc, "s_t", ordered=True)
        kc.s_v = Sem(stack, nc, "s_v", ordered=True)
        kc.s_s = Sem(stack, nc, "s_s", ordered=True)
        kc._ys = {}

        def ys(name):
            if name not in kc._ys:
                kc._ys[name] = Sem(stack, nc, "ys_" + name)
            return kc._ys[name]
        kc.ys = ys

        def barrier_sems():
            out = [kc.sd_x, kc.sd_n, kc.sd_kv, kc.sd_st, kc.sd_mk, kc.sd_out,
                   kc.sd_s, kc.sd_y]
            out.extend(kc._ys.values())
            return out
        kc.barrier_sems = barrier_sems
        kc.P = Prog(nc)
        kc.stgT = kc.stg
        kc.stg_n = 2
        kc.stg_dep = [None] * 16
        kc.stg_flip = 0

        dma(kc, "gpsimd", kc.sd_x, kc.x_res[:, :, :],
            kc.xin.ap().rearrange("(t p) d -> p t d", p=128))
        dma(kc, "gpsimd", kc.sd_x, kc.ident[:, :], kc.ident_in[:, :])
        gw(kc, "vector", kc.sd_x)
        vop(kc, lambda V: V.memset(kc.ones_bf[:, :], 1.0))
        vop(kc, lambda V: V.memset(kc.ones_row[:, :], 1.0))
        vop(kc, lambda V: V.memset(kc.cvals[:, 0:1], EPS))
        vop(kc, lambda V: V.memset(kc.cvals[:, 1:2], MAGIC))
        vop(kc, lambda V: V.memset(kc.cvals[:, 2:3], -MAGIC))
        gw(kc, "tensor", kc.sd_x)
        gw(kc, "scalar", kc.sd_x)

        emit_attention(kc, 0)
        if debug:
            gw(kc, "scalar", kc.s_v, kc.x_ready_v)
            dma(kc, "scalar", kc.sd_out,
                kc.dbg["dbg_x1"].ap().rearrange("(t p) d -> p t d", p=128),
                kc.x_res[:, :, :])
        emit_bitnet_prep(kc)
        emit_bitnet_ffn(kc)
        if debug:
            gw(kc, "scalar", kc.s_v, kc.x_ready_v)
            dma(kc, "scalar", kc.sd_out,
                kc.dbg["dbg_x2"].ap().rearrange("(t p) d -> p t d", p=128),
                kc.x_res[:, :, :])
        emit_attention(kc, 1)
        if debug:
            gw(kc, "scalar", kc.s_v, kc.x_ready_v)
            dma(kc, "scalar", kc.sd_out,
                kc.dbg["dbg_x3"].ap().rearrange("(t p) d -> p t d", p=128),
                kc.x_res[:, :, :])
        emit_moe(kc)

        gw(kc, "scalar", kc.s_v, kc.x_ready_v)
        dma(kc, "scalar", kc.sd_out,
            kc.xout.ap().rearrange("(t p) d -> p t d", p=128),
            kc.x_res[:, :, :])
        gw(kc, "scalar", kc.sd_out)
        kc.P.flush()
    return nc


def _mask_table(gr):
    m = np.zeros((128, 16, 512), np.float32)
    for i in range(16):
        for s in range(4):
            gq_rel = gr + 4 * s
            blk = m[:, i, s * 128:(s + 1) * 128]
            if i < gq_rel:
                pass
            elif i > gq_rel:
                blk[:, :] = NEG
            else:
                p = np.arange(128)[:, None]
                qf = np.arange(128)[None, :]
                blk[:, :] = np.where(p <= qf, 0.0, NEG)
    return m.reshape(128, 16 * 512).astype(ml_dtypes.bfloat16)


_NC_CACHE = {}


def kernel(**inputs):
    key = "main"
    if key not in _NC_CACHE:
        _NC_CACHE[key] = build(debug=DEBUG)
    nc = _NC_CACHE[key]

    x = np.ascontiguousarray(inputs["L0_attn_x"])
    ident = np.eye(128, dtype=ml_dtypes.bfloat16)
    wts = prep_weights(inputs)
    in_maps = []
    for c in range(8):
        b, gr = c // 4, c % 4
        tiles = [gr + 4 * j for j in range(CT)]
        xs = np.ascontiguousarray(
            np.concatenate([x[b, 128 * g:128 * (g + 1), :] for g in tiles], axis=0))
        m = {"xin": xs, "ident_in": ident, "masktbl": _mask_table(gr)}
        m.update(wts)
        in_maps.append(m)

    res = run_bass_kernel_spmd(nc, in_maps, core_ids=list(range(8)))
    out = np.empty((B, S, D), np.float32)
    for c in range(8):
        b, gr = c // 4, c % 4
        xo = res.results[c]["xout"]
        for j in range(CT):
            g = gr + 4 * j
            out[b, 128 * g:128 * (g + 1), :] = xo[128 * j:128 * (j + 1), :]
    return out


if __name__ == "__main__":
    nc = build(debug=False)
    print("instructions:", nc.n_instructions())
